# version 01
# baseline (speedup 1.0000x reference)
import numpy as np
import ml_dtypes
import concourse.bass as bass
import concourse.mybir as mybir
from concourse.bass_utils import run_bass_kernel_spmd

F32 = mybir.dt.float32
BF16 = mybir.dt.bfloat16
AF = mybir.ActivationFunctionType
ALU = mybir.AluOpType
AX = mybir.AxisListType.X

B, N, D = 4, 4096, 1024
TPC = 2048
NCH = 8
NT = TPC // 128
CORES = 8

ENG_ATTRS = {"pe": "tensor", "act": "scalar", "dve": "vector", "sp": "sync", "pool": "gpsimd"}
ENG_SEM = {"pe": ("pe", 1), "act": ("act", 1), "dve": ("dve", 1),
           "sp": ("spd", 16), "pool": ("poold", 16)}


class Prog:

    def __init__(self):
        self.nc = bass.Bass("TRN2", target_bir_lowering=False)
        self.q = {e: [] for e in ENG_ATTRS}
        self.counts = {}
        self.waited = {e: {} for e in ENG_ATTRS}
        self._sem_names = []
        self._ctx = []

    def sbuf(self, name, shape, dtype):
        g = self.nc.sbuf_tensor(name, shape, dtype)
        t = g.__enter__()
        self._ctx.append(g)
        return t

    def psum(self, name, shape, dtype=F32):
        g = self.nc.psum_tensor(name, shape, dtype)
        t = g.__enter__()
        self._ctx.append(g)
        return t

    def dram(self, name, shape, dtype, kind="Internal"):
        return self.nc.dram_tensor(name, shape, dtype, kind=kind)

    def _use_sem(self, name):
        if name not in self.counts:
            self.counts[name] = 0
            self._sem_names.append(name)

    def wait(self, eng, *tokens):
        for t in tokens:
            if t is None:
                continue
            sem, val = t
            if val <= 0:
                continue
            self._use_sem(sem)
            if self.waited[eng].get(sem, 0) >= val:
                continue
            self.waited[eng][sem] = val
            self.q[eng].append(("wait", sem, val))

    def op(self, eng, fn, sem="auto", n=None):
        if sem == "auto":
            sem, dn = ENG_SEM[eng]
            n = dn if n is None else n
        if sem is not None:
            self._use_sem(sem)
            self.counts[sem] += (n if n is not None else 1)
            tok = (sem, self.counts[sem])
        else:
            tok = None
        self.q[eng].append(("op", fn, sem, (n if n is not None else 1)))
        return tok

    def dma(self, eng, out, in_, **kw):
        return self.op(eng, lambda E: E.dma_start(out=out, in_=in_, **kw))

    def settle(self, eng):
        j = self._junk
        if eng == "act":
            return self.op("act", lambda E: E.copy(j[0:1, 0:8], j[0:1, 8:16]))
        if eng == "dve":
            return self.op("dve", lambda E: E.tensor_copy(j[0:1, 0:8], j[0:1, 8:16]))
        raise ValueError(eng)

    def cc(self, kind, op_, ins, outs, groups):
        return self.op("pool", lambda E: E.collective_compute(
            kind, op_, replica_groups=groups, ins=ins, outs=outs), sem="cc", n=1)

    def barrier(self):
        toks = []
        for eng in ENG_ATTRS:
            for item in reversed(self.q[eng]):
                if item[0] == "op" and item[2] is not None:
                    toks.append((item[2], self.counts[item[2]]))
                    break
        for eng in ENG_ATTRS:
            self.wait(eng, *toks)
        return toks

    def build(self):
        self._sems = {}
        sem_ctx = []
        for name in self._sem_names:
            g = self.nc.semaphore(name)
            self._sems[name] = g.__enter__()
            sem_ctx.append(g)
        bg = self.nc.Block()
        block = bg.__enter__()
        for eng, attr in ENG_ATTRS.items():
            items = self.q[eng]
            if not items:
                continue

            def body(E, items=items):
                for item in items:
                    if item[0] == "wait":
                        E.wait_ge(self._sems[item[1]], item[2])
                    else:
                        _, fn, sem, n = item
                        instr = fn(E)
                        if sem is not None:
                            instr.then_inc(self._sems[sem], n)
            getattr(block, attr)(body)
        bg.__exit__(None, None, None)
        for g in reversed(sem_ctx):
            g.__exit__(None, None, None)
        for g in reversed(self._ctx):
            g.__exit__(None, None, None)
        return self.nc


class Ctx:
    def __init__(self, P: Prog):
        self.P = P
        self.X = [P.sbuf(f"X{c}", [128, TPC], F32) for c in range(NCH)]
        self.XN = [P.sbuf(f"XN{c}", [128, TPC], BF16) for c in range(NCH)]
        self.BIG = P.sbuf("BIG", [128, 16896], BF16)
        self.STR = P.sbuf("STR", [128, 16896], BF16)
        self.SF = P.sbuf("SF", [128, 2048], F32)
        self.SF2 = P.sbuf("SF2", [128, 1024], F32)
        self.ROWS = P.sbuf("ROWS", [1, 2048], F32)
        self.ROWB = P.sbuf("ROWB", [1, 2048], BF16)
        self.ident = P.sbuf("identsb", [128, 128], BF16)
        self.ones_col = P.sbuf("ones_col", [128, 1], BF16)
        self.ones_row = P.sbuf("ones_row", [1, 128], BF16)
        self.qg = P.sbuf("qgsb", [128, TPC], F32)
        self.kg = P.sbuf("kgsb", [128, 32], F32)
        self.bias128 = P.sbuf("bias128", [128, 4], F32)
        P._junk = P.sbuf("junk", [1, 16], F32)
        self.BT = P.sbuf("BT", [128, 24], F32)
        self.pA = P.psum("pA", [128, 1024])
        self.pB = P.psum("pB", [128, 1024])
        self.pC = P.psum("pC", [128, 1024])
        self.pD = P.psum("pD", [128, 1024])
        self.pairs = [[0, 1], [2, 3], [4, 5], [6, 7]]

    def init_consts(self, x_in, ident_in, qg_in, kg_in):
        P = self.P
        for c in range(NCH):
            tx = P.dma("sp", self.X[c][:, :], x_in[c * 128:(c + 1) * 128, :])
        ti = P.dma("sp", self.ident[:, :], ident_in[:, :])
        tq = P.dma("sp", self.qg[:, :], qg_in[:, :])
        tk = P.dma("sp", self.kg[:, :], kg_in[:, :])
        t2 = P.op("dve", lambda E: E.memset(self.ones_col[:, :], 1.0))
        t2 = P.op("dve", lambda E: E.memset(self.ones_row[:, :], 1.0))
        self.init_tok = tk
        self.init_dve = t2


DEBUG_BARRIER = False
LN_HALVES = 2
MOE_DBG = False
MHA_STOP = None


def _dbg_bar(P):
    if DEBUG_BARRIER:
        P.barrier()


def emit_ln(ctx: Ctx, eps=1e-5, src=None, dst=None, gamma=None, beta=None,
            resid=False):
    P = ctx.P
    X = src or ctx.X
    XN = dst or ctx.XN
    src_bf = X is not ctx.X
    pD = ctx.pD
    x2b = ctx.STR
    last = None
    stats_tok = None
    for half in range(LN_HALVES):
        sl = slice(half * 1024, (half + 1) * 1024)
        if stats_tok is not None:
            P.wait("act", stats_tok)
        copy_toks, sq_toks = [], []
        for c in range(NCH):
            if not src_bf:
                tcopy = P.op("act", lambda E, c=c, sl=sl: E.copy(XN[c][:, sl], X[c][:, sl]))
            else:
                tcopy = None
            tsq = P.op("act", lambda E, c=c, sl=sl: E.activation(
                x2b[:, c * 2048: c * 2048 + 1024], X[c][:, sl], AF.Square))
            copy_toks.append(tcopy)
            sq_toks.append(tsq)
        if last is not None:
            P.wait("pe", last)
        for c in range(NCH):
            P.wait("pe", copy_toks[c], sq_toks[c])
            srcc = XN[c] if not src_bf else X[c]
            for j in range(2):
                jsl = slice(j * 512, (j + 1) * 512)
                P.op("pe", lambda E, c=c, j=j, jsl=jsl, srcc=srcc, sl=sl: E.matmul(
                    pD[0:1, jsl], ctx.ones_col[:, 0:1],
                    srcc[:, sl][:, jsl], start=(c == 0), stop=(c == NCH - 1)),
                    sem=None)
                tok = P.op("pe", lambda E, c=c, j=j, jsl=jsl: E.matmul(
                    pD[32:33, jsl], ctx.ones_col[:, 0:1],
                    x2b[:, c * 2048: c * 2048 + 1024][:, jsl],
                    start=(c == 0), stop=(c == NCH - 1)))
        stats_tok = tok
        _dbg_bar(P)
        R, RB = ctx.ROWS, ctx.ROWB
        m = R[0:1, 0:1024]
        var = R[0:1, 1024:2048]
        P.wait("dve", stats_tok)
        t = P.op("dve", lambda E, m=m: E.tensor_scalar(m, pD[0:1, 0:1024], 1.0 / D, None, ALU.mult))
        P.wait("dve", t)
        t = P.op("dve", lambda E, m=m, var=var: E.tensor_tensor(var, m, m, op=ALU.mult))
        P.wait("dve", t)
        t = P.op("dve", lambda E, var=var: E.scalar_tensor_tensor(
            var, pD[32:33, 0:1024], 1.0 / D, var, ALU.mult, ALU.subtract))
        P.wait("dve", t)
        t = P.op("dve", lambda E, var=var: E.tensor_scalar(var, var, float(eps), None, ALU.add))
        P.wait("act", t)
        t = P.op("act", lambda E, var=var: E.activation(var, var, AF.Sqrt))
        P.wait("dve", t)
        t = P.op("dve", lambda E, var=var: E.reciprocal(var, var))
        P.wait("dve", t)
        t2 = P.op("dve", lambda E, RB=RB, var=var: E.tensor_copy(RB[0:1, 0:1024], var))
        P.wait("dve", t2)
        t1 = P.op("dve", lambda E, m=m, var=var: E.tensor_tensor(m, m, var, op=ALU.mult))
        P.wait("dve", t1)
        t3 = P.op("dve", lambda E, RB=RB, m=m: E.tensor_copy(RB[0:1, 1024:2048], m))
        P.wait("dve", t3)
        t3s = P.settle("dve")
        _dbg_bar(P)
        P.wait("pe", t2, t3, t3s, ctx.init_dve)
        for j in range(2):
            jsl = slice(j * 512, (j + 1) * 512)
            ta = P.op("pe", lambda E, jsl=jsl, RB=RB: E.matmul(
                ctx.pA[:, jsl], ctx.ones_row[0:1, :], RB[0:1, jsl],
                start=True, stop=True))
            tb = P.op("pe", lambda E, jsl=jsl, j=j, RB=RB: E.matmul(
                ctx.pB[:, jsl], ctx.ones_row[0:1, :], RB[0:1, 1024 + j * 512:1024 + (j + 1) * 512],
                start=True, stop=True))
        _dbg_bar(P)
        P.wait("dve", ta, tb)
        for c in range(NCH):
            t = P.op("dve", lambda E, c=c, sl=sl: E.tensor_tensor(
                ctx.SF[:, 0:1024], X[c][:, sl], ctx.pA[:, :], op=ALU.mult))
            P.wait("dve", t)
            if gamma is None:
                t = P.op("dve", lambda E, c=c, sl=sl: E.tensor_tensor(
                    XN[c][:, sl], ctx.SF[:, 0:1024], ctx.pB[:, :], op=ALU.subtract))
            else:
                t = P.op("dve", lambda E, c=c: E.tensor_tensor(
                    ctx.SF2[:, 0:1024], ctx.SF[:, 0:1024], ctx.pB[:, :], op=ALU.subtract))
                P.wait("dve", t)
                t = P.op("dve", lambda E, c=c, sl=sl: E.tensor_scalar(
                    XN[c][:, sl], ctx.SF2[:, 0:1024], gamma[:, c:c + 1], beta[:, c:c + 1],
                    ALU.mult, ALU.add))
            P.wait("dve", t)
        last = t
        _dbg_bar(P)
    return last


def emit_moe(ctx: Ctx, gate_ap, gate_layout, E, INTER, topk,
             w1_ap, w3_ap, w2_ap, shared=None, eps20=False):
    P = ctx.P
    XN, X = ctx.XN, ctx.X
    pA, pB, pC, pD = ctx.pA, ctx.pB, ctx.pC, ctx.pD
    SF, SF2 = ctx.SF, ctx.SF2
    NIC = INTER // 128
    W1B = [0, 2048]
    W3B = [4096, 6144]
    W2B = [8192, 10240]
    SCO = 12288
    GWT = 12800
    GWR = 14000
    STR = ctx.STR

    gwt = [STR[:, GWT + c * E: GWT + (c + 1) * E] for c in range(NCH)]
    if gate_layout == "he":
        tg = P.dma("pool", STR[:, GWT:GWT + NCH * E].rearrange("p (c e) -> p c e", c=NCH),
                   gate_ap.rearrange("(c p) e -> p c e", p=128))
        gw_tok = tg
    else:
        traw = P.dma("pool", STR[0:16, GWR:GWR + 1024], gate_ap[:, :])
        P.wait("pe", traw, ctx.init_tok)
        tcp = None
        for c in range(NCH):
            P.wait("pe", tcp)
            t = P.op("pe", lambda E_, c=c: E_.matmul(
                pD[:, 0:128], STR[0:16, GWR + c * 128:GWR + (c + 1) * 128],
                ctx.ident[0:16, :], start=True, stop=True))
            P.wait("act", t)
            tcp = P.op("act", lambda E_, c=c: E_.copy(gwt[c][:, :], pD[:, 0:E]))
        gw_tok = P.settle("act")

    if MOE_DBG:
        P.wait("dve", gw_tok)
        tdbg = P.op("dve", lambda E_: E_.tensor_copy(SF[:, 1792:1792 + NCH * E], STR[:, GWT:GWT + NCH * E]))
        P.wait("dve", tdbg)
    P.wait("pe", gw_tok)
    for i in range(NT):
        for c in range(NCH):
            tl = P.op("pe", lambda E_, i=i, c=c: E_.matmul(
                pC[:, i * E:(i + 1) * E], XN[c][:, i * 128:(i + 1) * 128],
                gwt[c][:, :], start=(c == 0), stop=(c == NCH - 1)),
                sem=None if c < NCH - 1 else "pe")
    logits_tok = tl
    if MOE_DBG:
        P.wait("dve", logits_tok)
        tdb = P.op("dve", lambda E_: E_.tensor_copy(SF[:, 1536:1536 + 16 * E], pC[:, 0:16 * E]))
        P.wait("dve", tdb)

    ex = SF[:, 0:16 * E]
    work = SF[:, 512:512 + 16 * E]
    msk = SF[:, 1024:1024 + 16 * E]
    mx = SF2[:, 0:16]
    Ssum = SF2[:, 16:32]
    den = SF2[:, 48:64]

    def v3(ap_):
        return ap_.rearrange("p (t e) -> p t e", e=E)

    def b3(ap16):
        return ap16.rearrange("p (t o) -> p t o", o=1).broadcast_to([128, 16, E])

    P.wait("dve", logits_tok)
    t = P.op("dve", lambda E_: E_.reduce_max(mx.rearrange("p (t o) -> p t o", o=1),
                                             v3(pC[:, 0:16 * E]), axis=AX))
    P.wait("dve", t)
    t = P.op("dve", lambda E_: E_.tensor_tensor(v3(ex), v3(pC[:, 0:16 * E]), b3(mx), op=ALU.subtract))
    P.wait("act", t)
    t = P.op("act", lambda E_: E_.activation(ex, ex, AF.Exp))
    P.wait("dve", t)
    t = P.op("dve", lambda E_: E_.reduce_sum(Ssum.rearrange("p (t o) -> p t o", o=1), v3(ex), axis=AX))
    P.wait("dve", t)
    t = P.op("dve", lambda E_: E_.tensor_copy(work, ex))
    for r in range(topk - 1):
        P.wait("dve", t)
        t = P.op("dve", lambda E_: E_.reduce_max(mx.rearrange("p (t o) -> p t o", o=1), v3(work), axis=AX))
        P.wait("dve", t)
        t = P.op("dve", lambda E_: E_.tensor_tensor(v3(msk), v3(work), b3(mx), op=ALU.is_lt))
        P.wait("dve", t)
        t = P.op("dve", lambda E_: E_.tensor_tensor(work, work, msk, op=ALU.mult))
    P.wait("dve", t)
    t = P.op("dve", lambda E_: E_.reduce_max(mx.rearrange("p (t o) -> p t o", o=1), v3(work), axis=AX))
    P.wait("dve", t)
    t = P.op("dve", lambda E_: E_.tensor_tensor(v3(msk), v3(ex), b3(mx), op=ALU.is_ge))
    P.wait("dve", t)
    t = P.op("dve", lambda E_: E_.tensor_tensor(ex, ex, msk, op=ALU.mult))
    P.wait("dve", t)
    t = P.op("dve", lambda E_: E_.reduce_sum(den.rearrange("p (t o) -> p t o", o=1), v3(ex), axis=AX))
    P.wait("dve", t)
    if eps20:
        t = P.op("dve", lambda E_: E_.scalar_tensor_tensor(den, Ssum, 1e-20, den, ALU.mult, ALU.add))
        P.wait("dve", t)
    t = P.op("dve", lambda E_: E_.reciprocal(den, den))
    P.wait("dve", t)
    t = P.op("dve", lambda E_: E_.tensor_tensor(v3(STR[:, SCO:SCO + 16 * E]), v3(ex), b3(den), op=ALU.mult))
    scores_tok = t

    experts = [(e, w1_ap, w3_ap, w2_ap, True) for e in range(E)]
    if shared is not None:
        experts.append((0, shared[0], shared[1], shared[2], False))

    dma_tok = {}
    use_tok = {}
    halfsl = [slice(0, 1024), slice(1024, 2048)]
    prev_silu = None
    prev_umul = None
    prev_tact = None
    prev_down = None
    prev_xadd = {}
    bcast_read_tok = None

    for half in range(2):
        hs = halfsl[half]
        for ei, (e, a1, a3, a2, scaled) in enumerate(experts):
            slot = ei % 2
            if scaled:
                tts = []
                P.wait("pe", scores_tok)
                if bcast_read_tok is not None:
                    P.wait("pe", bcast_read_tok)
                for il in range(8):
                    i = half * 8 + il
                    tt = P.op("pe", lambda E_, i=i, il=il, e=e: E_.matmul(
                        pD[0:1, il * 128:(il + 1) * 128],
                        STR[:, SCO + i * E + e: SCO + i * E + e + 1],
                        ctx.ident[:, :], start=True, stop=True))
                    tts.append(tt)
                P.wait("dve", tts[-1])
                trow = P.op("dve", lambda E_: E_.tensor_copy(ctx.ROWB[0:1, 0:1024], pD[0:1, 0:1024]))
                P.wait("dve", trow)
                trow = P.settle("dve")
                P.wait("pe", trow)
                for j in range(2):
                    tb = P.op("pe", lambda E_, j=j: E_.matmul(
                        pD[:, j * 512:(j + 1) * 512], ctx.ones_row[0:1, :],
                        ctx.ROWB[0:1, j * 512:(j + 1) * 512], start=True, stop=True))
                bcast_tok = tb
            else:
                bcast_tok = None

            act_toks = []
            for ic in range(NIC):
                for kind, base, wap in (("w1", W1B, a1), ("w3", W3B, a3)):
                    key = (kind, slot)
                    P.wait("pool", use_tok.get(key))
                    src = wap[e * 1024:(e + 1) * 1024, ic * 128:(ic + 1) * 128] if scaled else \
                        wap[:, ic * 128:(ic + 1) * 128]
                    tdma = P.dma("pool", STR[:, base[slot]:base[slot] + 1024]
                                 .rearrange("p (c i) -> p c i", c=NCH),
                                 src.rearrange("(c p) i -> p c i", p=128))
                    dma_tok[key] = tdma
                P.wait("pe", dma_tok[("w1", slot)], dma_tok[("w3", slot)])
                P.wait("pe", prev_silu)
                P.wait("pe", prev_umul)
                for c in range(NCH):
                    for j in range(2):
                        tg_ = P.op("pe", lambda E_, c=c, j=j, base=W1B[slot], hs=hs: E_.matmul(
                            pB[:, j * 512:(j + 1) * 512],
                            STR[:, base + c * 128:base + (c + 1) * 128],
                            XN[c][:, hs][:, j * 512:(j + 1) * 512],
                            start=(c == 0), stop=(c == NCH - 1)),
                            sem=None if c < NCH - 1 else "pe")
                for c in range(NCH):
                    for j in range(2):
                        tu_ = P.op("pe", lambda E_, c=c, j=j, base=W3B[slot], hs=hs: E_.matmul(
                            pC[:, j * 512:(j + 1) * 512],
                            STR[:, base + c * 128:base + (c + 1) * 128],
                            XN[c][:, hs][:, j * 512:(j + 1) * 512],
                            start=(c == 0), stop=(c == NCH - 1)),
                            sem=None if c < NCH - 1 else "pe")
                use_tok[("w1", slot)] = tg_
                use_tok[("w3", slot)] = tu_
                P.wait("act", tg_, prev_tact)
                tsil = P.op("act", lambda E_: E_.activation(SF2[:, 0:1024], pB[:, :], AF.Silu))
                prev_silu = tsil
                P.wait("dve", tsil, tu_)
                tum = P.op("dve", lambda E_: E_.tensor_tensor(SF2[:, 0:1024], SF2[:, 0:1024], pC[:, :], op=ALU.mult))
                prev_umul = tum
                P.wait("dve", tum, prev_down)
                if scaled:
                    P.wait("dve", bcast_tok)
                    tact = P.op("dve", lambda E_, ic=ic: E_.tensor_tensor(
                        ctx.BIG[:, ic * 1024:(ic + 1) * 1024], SF2[:, 0:1024], pD[:, :], op=ALU.mult))
                    bcast_read_tok = tact
                else:
                    tact = P.op("dve", lambda E_, ic=ic: E_.tensor_copy(
                        ctx.BIG[:, ic * 1024:(ic + 1) * 1024], SF2[:, 0:1024]))
                P.wait("dve", tact)
                prev_tact = tact
                act_toks.append(tact)

            for hc in range(NCH):
                key = ("w2", slot)
                P.wait("pool", use_tok.get(key))
                src2 = a2[e * INTER:(e + 1) * INTER, hc * 128:(hc + 1) * 128] if scaled else \
                    a2[:, hc * 128:(hc + 1) * 128]
                tdma = P.dma("pool", STR[:, W2B[slot]:W2B[slot] + NIC * 128]
                             .rearrange("p (c i) -> p c i", c=NIC),
                             src2.rearrange("(c p) i -> p c i", p=128))
                P.wait("pe", tdma, act_toks[-1])
                P.wait("pe", prev_xadd.get(hc))
                for ic in range(NIC):
                    for j in range(2):
                        td_ = P.op("pe", lambda E_, ic=ic, j=j, base=W2B[slot]: E_.matmul(
                            pA[:, j * 512:(j + 1) * 512],
                            STR[:, base + ic * 128:base + (ic + 1) * 128],
                            ctx.BIG[:, ic * 1024 + j * 512: ic * 1024 + j * 512 + 512],
                            start=(ic == 0), stop=(ic == NIC - 1)),
                            sem=None if ic < NIC - 1 else "pe")
                use_tok[key] = td_
                prev_down = td_
                P.wait("dve", td_)
                tx = P.op("dve", lambda E_, hc=hc, hs=hs: E_.tensor_tensor(
                    X[hc][:, hs], X[hc][:, hs], pA[:, :], op=ALU.add))
                prev_xadd[hc] = tx
    return tx


def _load_wT(ctx, w_ap, tag, n=1024):
    P = ctx.P
    STR = ctx.STR
    raw = STR[:, 0:8192]
    wt = STR[:, 8192:16384]
    t_ld = P.dma("pool", raw.rearrange("p (c i) -> p c i", c=8),
                 w_ap.rearrange("(c p) i -> p c i", p=128))
    P.wait("pe", t_ld, ctx.init_tok)
    tcp = None
    for r in range(8):
        P.wait("pe", tcp)
        for c in range(8):
            tt = P.op("pe", lambda E_, r=r, c=c: E_.matmul(
                ctx.pD[:, c * 128:(c + 1) * 128],
                raw.rearrange("p (c i) -> p c i", c=8)[:, c, r * 128:(r + 1) * 128],
                ctx.ident[:, :], start=True, stop=True),
                sem=None if c < 7 else "pe")
        P.wait("act", tt)
        tcp = P.op("act", lambda E_, r=r: E_.copy(wt[:, r * 1024:(r + 1) * 1024], ctx.pD[:, :]))
    tok = P.settle("act")
    chunks = [wt[:, r * 1024:(r + 1) * 1024] for r in range(8)]
    return tok, chunks


def _load_w_direct(ctx, w_ap, tag, dout=1024, off=0):
    P = ctx.P
    n_el = 8 * dout
    buf = ctx.STR[:, off:off + n_el]
    t_ld = P.dma("pool", buf.rearrange("p (c i) -> p c i", c=8),
                 w_ap.rearrange("(c p) i -> p c i", p=128))
    chunks = [buf[:, c * dout:(c + 1) * dout] for c in range(8)]
    return t_ld, chunks


def emit_fm_proj(ctx, w_chunks, w_tok, r_list, scale=None, bias_col=None,
                 dram_out=None, dram_row0=0, norm_units=None, umax=None,
                 norm_accum=False, extra_wait=None, src_chunks=None,
                 dst_chunks=None):
    P = ctx.P
    src = src_chunks or ctx.XN
    stage = ctx.BIG[:, 0:2048]
    sq = ctx.BIG[:, 2048:4096]
    last_dma = None
    tnorm = None
    prev_ev = None
    nr_total = len(r_list)
    for ri, r in enumerate(r_list):
        P.wait("pe", w_tok, extra_wait)
        P.wait("pe", prev_ev)
        for c in range(NCH):
            for j in range(4):
                psl = ctx.pA if j < 2 else ctx.pB
                po = (j % 2) * 512
                tmm = P.op("pe", lambda E_, r=r, c=c, j=j, psl=psl, po=po, src=src: E_.matmul(
                    psl[:, po:po + 512], w_chunks[c][:, r * 128:(r + 1) * 128],
                    src[c][:, j * 512:(j + 1) * 512],
                    start=(c == 0), stop=(c == NCH - 1)),
                    sem=None if (c < NCH - 1) else "pe")
        P.wait("act", tmm, last_dma)
        if dst_chunks is not None:
            stage = dst_chunks[r]
        for j in range(4):
            psl = ctx.pA if j < 2 else ctx.pB
            po = (j % 2) * 512
            if bias_col is not None:
                tev = P.op("act", lambda E_, r=r, j=j, psl=psl, po=po, stage=stage: E_.activation(
                    stage[:, j * 512:(j + 1) * 512], psl[:, po:po + 512], AF.Identity,
                    bias=bias_col[:, r:r + 1], scale=(scale if scale is not None else 1.0)))
            elif scale is not None:
                tev = P.op("act", lambda E_, r=r, j=j, psl=psl, po=po, stage=stage: E_.activation(
                    stage[:, j * 512:(j + 1) * 512], psl[:, po:po + 512], AF.Copy,
                    scale=scale))
            else:
                tev = P.op("act", lambda E_, r=r, j=j, psl=psl, po=po, stage=stage: E_.copy(
                    stage[:, j * 512:(j + 1) * 512], psl[:, po:po + 512]))
        if dram_out is not None:
            P.wait("sp", tev)
            last_dma = P.dma("sp", dram_out[dram_row0 + r * 128:dram_row0 + (r + 1) * 128, :],
                             stage[:, :])
        units = []
        if norm_units is not None:
            units = norm_units.get(r, [])
        if units or norm_accum:
            P.wait("act", tev, tnorm)
            tsq = P.op("act", lambda E_, stage=stage: E_.activation(sq[:, :], stage[:, :], AF.Square))
            if norm_accum:
                units = [(0, 0, 128)] if not units else units
            for (u, base, nrows) in units:
                P.wait("pe", tsq, tnorm)
                st_flag = (ri == 0) if norm_accum else True
                sp_flag = (ri == nr_total - 1) if norm_accum else True
                for j in range(4):
                    prow = ctx.pD[0:1, :] if j < 2 else ctx.pD[32:33, :]
                    po = (j % 2) * 512
                    tpn = P.op("pe", lambda E_, base=base, nrows=nrows, j=j, prow=prow,
                               po=po, st=st_flag, sp=sp_flag: E_.matmul(
                        prow[0:1, po:po + 512], ctx.ones_col[base:base + nrows, 0:1],
                        sq[base:base + nrows, j * 512:(j + 1) * 512],
                        start=st, stop=sp), sem=None if j < 3 else "pe")
                if norm_accum and ri < nr_total - 1:
                    continue
                P.wait("dve", tpn)
                t1 = P.op("dve", lambda E_, u=u: E_.reduce_max(
                    ctx.SF2[0:1, 40:41], ctx.pD[0:1, 0:1024], axis=AX))
                P.wait("dve", t1)
                t2 = P.op("dve", lambda E_, u=u: E_.reduce_max(
                    ctx.SF2[0:1, 41:42], ctx.pD[32:33, 0:1024], axis=AX))
                P.wait("dve", t2)
                t3 = P.op("dve", lambda E_, u=u, umax=umax: E_.tensor_tensor(
                    umax[0:1, u:u + 1], ctx.SF2[0:1, 40:41], ctx.SF2[0:1, 41:42],
                    op=ALU.max))
                P.wait("dve", t3)
                tnorm = t3
        prev_ev = tev
    return last_dma, tnorm


def emit_v_proj(ctx, w_chunks, w_tok, dram_out, dh1=None, bias_row=None,
                scale=None, extra_wait=None):
    P = ctx.P
    stage = ctx.BIG[:, 4096:6144]
    last_dma = None
    prev_ev = None
    for tp in range(NT // 2):
        P.wait("pe", w_tok, extra_wait, prev_ev, last_dma)
        for half in range(2):
            tt = tp * 2 + half
            psl = ctx.pA if half == 0 else ctx.pB
            for c in range(NCH):
                for j in range(2):
                    lastmm = (c == NCH - 1 and j == 1 and bias_row is None)
                    tmm = P.op("pe", lambda E_, tt=tt, c=c, j=j, psl=psl: E_.matmul(
                        psl[:, j * 512:(j + 1) * 512],
                        ctx.XN[c][:, tt * 128:(tt + 1) * 128],
                        w_chunks[c][:, j * 512:(j + 1) * 512],
                        start=(c == 0), stop=(c == NCH - 1 and bias_row is None)),
                        sem="pe" if lastmm else None)
            if bias_row is not None:
                for j in range(2):
                    tmm = P.op("pe", lambda E_, j=j, psl=psl: E_.matmul(
                        psl[:, j * 512:(j + 1) * 512], ctx.ones_row[0:1, :],
                        bias_row[0:1, j * 512:(j + 1) * 512],
                        start=False, stop=True),
                        sem=None if j < 1 else "pe")
        P.wait("act", tmm)
        for half in range(2):
            psl = ctx.pA if half == 0 else ctx.pB
            if scale is not None:
                tev = P.op("act", lambda E_, half=half, psl=psl: E_.activation(
                    stage[:, half * 1024:(half + 1) * 1024], psl[:, :], AF.Copy, scale=scale))
            else:
                tev = P.op("act", lambda E_, half=half, psl=psl: E_.copy(
                    stage[:, half * 1024:(half + 1) * 1024], psl[:, :]))
        prev_ev = tev
        P.wait("sp", tev)
        last_dma = P.dma("sp",
                         dram_out[tp * 256:(tp + 1) * 256, :].rearrange("(h p) e -> p h e", p=128),
                         stage.rearrange("p (h e) -> p h e", h=2))
    return last_dma


def emit_attn_mha(ctx: Ctx, tag, H, dh, wq_ap, wk_ap, wv_ap, transposed,
                  w_out_ap=None, b_out_ap=None, gamma_ap=None, beta_ap=None,
                  core_id_pairs=None):
    P = ctx.P
    STR, BIG = ctx.STR, ctx.BIG
    scale = float(dh) ** -0.5
    U = 16
    G = H // 2
    grows = 2 * dh

    Qd = P.dram(f"{tag}_Qd", [1024, TPC], BF16).ap()
    Kd = P.dram(f"{tag}_Kd", [1024, TPC], BF16).ap()
    Vd = P.dram(f"{tag}_Vd", [TPC, 1024], BF16).ap()
    Kg = P.dram(f"{tag}_Kg", [2048, TPC], BF16).ap()
    Vg = P.dram(f"{tag}_Vg", [2 * TPC, 1024], BF16).ap()
    kmd = P.dram(f"{tag}_kmd", [1, U], F32).ap()
    kmg = P.dram(f"{tag}_kmg", [1, U], F32).ap()

    qmax = ctx.SF2[0:1, 0:U]
    kmax = ctx.SF2[0:1, 32:32 + U]
    norm_units = {r: [(2 * r, 0, 64), (2 * r + 1, 64, 64)] for r in range(NCH)}

    if transposed:
        wtok, wch = _load_wT(ctx, wq_ap, tag + "q")
    else:
        wtok, wch = _load_w_direct(ctx, wq_ap, tag + "q")
    tq, tnq = emit_fm_proj(ctx, wch, wtok, list(range(NCH)), scale=scale,
                           dram_out=Qd, norm_units=norm_units, umax=qmax)
    P.barrier()
    if transposed:
        wtok, wch = _load_wT(ctx, wk_ap, tag + "k")
    else:
        wtok, wch = _load_w_direct(ctx, wk_ap, tag + "k")
    tk, tnk = emit_fm_proj(ctx, wch, wtok, list(range(NCH)),
                           dram_out=Kd, norm_units=norm_units, umax=kmax)
    P.barrier()
    if transposed:
        wtok, wch = _load_wT(ctx, wv_ap, tag + "v")
    else:
        wtok, wch = _load_w_direct(ctx, wv_ap, tag + "v")
    tv = emit_v_proj(ctx, wch, wtok, Vd)
    P.wait("sp", tnk)
    tkm = P.dma("sp", kmd[0:1, :], kmax)
    P.barrier()

    c1 = P.cc("AllGather", ALU.bypass, [Kd.tensor.ap().opt()], [Kg.tensor.ap().opt()], ctx.pairs)
    c2 = P.cc("AllGather", ALU.bypass, [Vd.tensor.ap().opt()], [Vg.tensor.ap().opt()], ctx.pairs)
    c3 = P.cc("AllReduce", ALU.max, [kmd.tensor.ap().opt()], [kmg.tensor.ap().opt()], ctx.pairs)
    P.wait("sp", c3)
    tkg = P.dma("sp", kmax, kmg[0:1, :])

    if MHA_STOP == "proj":
        P.barrier()
        return (Qd, Kd, Vd, Kg, Vg, kmg)
    P.wait("dve", tkg, tnq)
    t = P.op("dve", lambda E_: E_.tensor_tensor(ctx.SF2[0:1, 64:64 + U], qmax, kmax, op=ALU.mult))
    P.wait("act", t)
    t = P.op("act", lambda E_: E_.activation(ctx.SF2[0:1, 64:64 + U], ctx.SF2[0:1, 64:64 + U], AF.Sqrt))
    P.wait("dve", t)
    t = P.op("dve", lambda E_: E_.tensor_scalar(ctx.ROWB[0:1, 0:U], ctx.SF2[0:1, 64:64 + U],
                                                -1.0, None, ALU.mult))
    P.wait("dve", t)
    brow_tok = P.settle("dve")

    KB = STR[:, 0:4096]
    QB = STR[:, 4096:6144]
    W_ = 2 * dh + 2
    VPW = 32 * W_
    VP = STR[:, 6144:6144 + VPW]
    VPv = VP.rearrange("p (ki w) -> p ki w", w=W_)
    EB = STR[:, 10400:14496]

    if w_out_ap is not None:
        O_all = [BIG[:, c * 2048:(c + 1) * 2048] for c in range(NCH)]
        OS = STR[:, 14496:15520]

    ep_toks = []
    prev_core = {}
    exp_war = {}
    for g in range(G):
        if dh == 64:
            ua, ub = 2 * g, 2 * g + 1
        else:
            ua = ub = g
        P.wait("sp", c1, c2)
        pk = prev_core.get("kb")
        P.wait("sp", pk)
        d1 = P.dma("sp", KB[0:grows, 0:2048], Kg[grows * g:grows * (g + 1), :])
        d2 = P.dma("sp", KB[0:grows, 2048:4096], Kg[1024 + grows * g:1024 + grows * (g + 1), :])
        d3 = P.dma("sp", QB[0:grows, :], Qd[grows * g:grows * (g + 1), :])
        d4 = P.dma("sp", VPv[:, :, 0:dh],
                   Vg[:, grows * g:grows * g + dh].rearrange("(ki p) e -> p ki e", p=128))
        d5 = P.dma("sp", VPv[:, :, dh + 1:2 * dh + 1],
                   Vg[:, grows * g + dh:grows * (g + 1)].rearrange("(ki p) e -> p ki e", p=128))
        P.wait("dve", d5)
        tm1 = P.op("dve", lambda E_: E_.memset(VPv[:, :, dh:dh + 1], 1.0))
        P.wait("dve", tm1)
        tm2 = P.op("dve", lambda E_: E_.memset(VPv[:, :, 2 * dh + 1:2 * dh + 2], 1.0))
        P.wait("pe", brow_tok, d5, tm2, exp_war.get("a"), exp_war.get("b"), prev_core.get("kb"))
        tb1 = P.op("pe", lambda E_, ua=ua: E_.matmul(
            ctx.pA[:, 0:1], ctx.ones_row[0:1, :], ctx.ROWB[0:1, ua:ua + 1],
            start=True, stop=True))
        tb2 = P.op("pe", lambda E_, ub=ub: E_.matmul(
            ctx.pA[:, 1:2], ctx.ones_row[0:1, :], ctx.ROWB[0:1, ub:ub + 1],
            start=True, stop=True))
        P.wait("dve", tb2, exp_war.get("a"), exp_war.get("b"))
        tb3 = P.op("dve", lambda E_: E_.tensor_copy(ctx.bias128[:, 0:2], ctx.pA[:, 0:2]))
        P.wait("dve", tb3)
        tb4 = P.settle("dve")

        exp_tok = {"a": None, "b": None}
        pv_tok = {}
        st_tok = {}
        ep = None
        for qc in range(2):
            qsl = slice(qc * 1024, (qc + 1) * 1024)
            P.wait("pe", ep)
            for ki in range(32):
                for hi, hname in ((0, "a"), (1, "b")):
                    rows = slice(dh * hi, dh * (hi + 1))
                    ps = ctx.pA if hi == 0 else ctx.pB
                    pvp = ctx.pC if hi == 0 else ctx.pD
                    voff = (dh + 1) * hi
                    P.wait("pe", exp_tok[hname], tb4)
                    for j in range(2):
                        tst = P.op("pe", lambda E_, ki=ki, j=j, rows=rows, ps=ps, qsl=qsl: E_.matmul(
                            ps[:, j * 512:(j + 1) * 512],
                            KB[rows, ki * 128:(ki + 1) * 128],
                            QB[rows, qsl][:, j * 512:(j + 1) * 512],
                            start=True, stop=True),
                            sem=None if j < 1 else "pe")
                    st_tok[hname] = tst
                    eoff = 1024 * (2 * (ki % 2) + hi)
                    P.wait("act", tst, pv_tok.get((hname, ki % 2)), tb4)
                    tex = P.op("act", lambda E_, ps=ps, eoff=eoff, hi=hi: E_.activation(
                        EB[:, eoff:eoff + 1024], ps[:, :], AF.Exp,
                        bias=ctx.bias128[:, hi:hi + 1]))
                    exp_tok[hname] = tex
                    exp_war[hname] = tex
                    P.wait("pe", tex)
                    for j in range(2):
                        tpv = P.op("pe", lambda E_, ki=ki, j=j, pvp=pvp, voff=voff, eoff=eoff: E_.matmul(
                            pvp[0:dh + 1, j * 512:(j + 1) * 512],
                            VPv[:, ki, voff:voff + dh + 1],
                            EB[:, eoff + j * 512:eoff + (j + 1) * 512],
                            start=(ki == 0), stop=(ki == 31)),
                            sem=None if j < 1 else "pe")
                    pv_tok[(hname, ki % 2)] = tpv
            for hi, hname in ((0, "a"), (1, "b")):
                pvp = ctx.pC if hi == 0 else ctx.pD
                ps = ctx.pA if hi == 0 else ctx.pB
                P.wait("dve", pv_tok[(hname, 0)], pv_tok[(hname, 1)])
                td = P.op("dve", lambda E_, pvp=pvp: E_.reciprocal(
                    ctx.ROWS[0:1, 0:1024], pvp[dh:dh + 1, 0:1024]))
                P.wait("dve", td)
                td = P.op("dve", lambda E_: E_.tensor_copy(ctx.ROWB[0:1, 1024:2048],
                                                           ctx.ROWS[0:1, 0:1024]))
                P.wait("dve", td)
                td = P.settle("dve")
                P.wait("pe", td, exp_tok[hname])
                for j in range(2):
                    tbc = P.op("pe", lambda E_, j=j, ps=ps: E_.matmul(
                        ps[:, j * 512:(j + 1) * 512], ctx.ones_row[0:1, :],
                        ctx.ROWB[0:1, 1024 + j * 512:1024 + (j + 1) * 512],
                        start=True, stop=True),
                        sem=None if j < 1 else "pe")
                P.wait("act", tbc)
                tnb = P.op("act", lambda E_, ps=ps: E_.copy(ctx.SF2[:, 0:1024], ps[0:128, :]))
                P.wait("dve", tnb)
                if w_out_ap is None:
                    t1 = P.op("dve", lambda E_, pvp=pvp, hi=hi: E_.tensor_tensor(
                        ctx.SF[dh * hi:dh * (hi + 1), 0:1024], pvp[0:dh, :],
                        ctx.SF2[0:dh, 0:1024], op=ALU.mult))
                    P.wait("dve", t1)
                    t2 = P.op("dve", lambda E_, g=g, hi=hi, qsl=qsl: E_.tensor_tensor(
                        ctx.X[g][dh * hi:dh * (hi + 1), qsl],
                        ctx.X[g][dh * hi:dh * (hi + 1), qsl],
                        ctx.SF[dh * hi:dh * (hi + 1), 0:1024], op=ALU.add))
                else:
                    t2 = P.op("dve", lambda E_, pvp=pvp, hi=hi: E_.tensor_tensor(
                        OS[32 * hi:32 * (hi + 1), 0:1024], pvp[0:dh, :],
                        ctx.SF2[0:dh, 0:1024], op=ALU.mult))
                P.wait("dve", t2)
                ep = t2
            if w_out_ap is not None:
                cch, rbase = g // 2, 64 * (g % 2)
                t3 = P.op("dve", lambda E_, cch=cch, rbase=rbase, qsl=qsl: E_.tensor_copy(
                    O_all[cch][rbase:rbase + 64, qsl], OS[0:64, 0:1024]))
                P.wait("dve", t3)
                ep = t3
            prev_core["kb"] = ep
            ep_toks.append(ep)

    P.barrier()
    if MHA_STOP == "core" or w_out_ap is None:
        return ep_toks[-1]

    wtok, wch = _load_w_direct(ctx, w_out_ap, tag + "o")
    bo = ctx.BT[:, 0:8]
    tbo = P.dma("pool", bo, b_out_ap.rearrange("(c p) -> p c", p=128), allow_slow_non_contiguous=True)
    P.wait("pe", tbo)
    _, _ = emit_fm_proj(ctx, wch, wtok, list(range(NCH)), bias_col=bo,
                        src_chunks=O_all, dst_chunks=ctx.XN)
    P.barrier()
    if MHA_STOP == "oproj":
        return ep_toks[-1]
    gm = ctx.BT[:, 8:16]
    bt = ctx.BT[:, 16:24]
    tg1 = P.dma("pool", gm, gamma_ap.rearrange("(c p) -> p c", p=128), allow_slow_non_contiguous=True)
    tg2 = P.dma("pool", bt, beta_ap.rearrange("(c p) -> p c", p=128), allow_slow_non_contiguous=True)
    P.wait("dve", tg1, tg2)
    P.wait("act", tg2)
    P.wait("pe", tg2)
    if MHA_STOP == "noaffine":
        tln = emit_ln(ctx, eps=1e-6, src=ctx.XN, dst=ctx.XN)
    else:
        tln = emit_ln(ctx, eps=1e-6, src=ctx.XN, dst=ctx.XN, gamma=gm, beta=bt)
    P.barrier()
    if MHA_STOP == "ln":
        return tln
    for c in range(NCH):
        tx = P.op("dve", lambda E_, c=c: E_.tensor_tensor(
            ctx.X[c][:, :], ctx.X[c][:, :], ctx.XN[c][:, :], op=ALU.add))
        P.wait("dve", tx)
    return tx


def emit_attn_causal(ctx: Ctx, tag, wq_ap, bq_ap, wk_ap, bk_ap, wv_ap, bv_ap):
    P = ctx.P
    STR, BIG = ctx.STR, ctx.BIG
    d = 1024
    scale = 1.0 / 32.0

    Qd = P.dram(f"{tag}_Qd", [1024, TPC], BF16).ap()
    Kd = P.dram(f"{tag}_Kd", [1024, TPC], BF16).ap()
    Vd = P.dram(f"{tag}_Vd", [TPC, 1024], BF16).ap()
    Kg = P.dram(f"{tag}_Kg", [2048, TPC], BF16).ap()
    Vg = P.dram(f"{tag}_Vg", [2 * TPC, 1024], BF16).ap()
    kmd = P.dram(f"{tag}_kmd", [1, 16], F32).ap()
    kmg = P.dram(f"{tag}_kmg", [1, 16], F32).ap()

    qmax = ctx.SF2[0:1, 0:16]
    kmax = ctx.SF2[0:1, 32:48]

    bqt = ctx.BT[:, 0:8]
    bkt = ctx.BT[:, 8:16]
    tb1 = P.dma("pool", bqt, bq_ap.rearrange("(c p) -> p c", p=128), allow_slow_non_contiguous=True)
    tb2 = P.dma("pool", bkt, bk_ap.rearrange("(c p) -> p c", p=128), allow_slow_non_contiguous=True)
    bvr = ctx.ROWB[0:1, 1024:2048]
    tb3 = P.dma("pool", bvr, bv_ap.rearrange("e -> 1 e" if False else "(o e) -> o e", o=1))
    P.wait("dve", tb1)
    tb4 = P.op("dve", lambda E_: E_.tensor_scalar(ctx.BT[:, 16:24], bqt, scale, None, ALU.mult))
    P.wait("dve", tb4)
    tb5 = P.settle("dve")
    bq32 = ctx.BT[:, 16:24]

    wtok, wch = _load_wT(ctx, wq_ap, tag + "q")
    P.wait("act", tb5)
    _, tnq = emit_fm_proj(ctx, wch, wtok, list(range(NCH)), scale=scale, bias_col=bq32,
                          dram_out=Qd, norm_accum=True, umax=qmax)
    P.barrier()
    wtok, wch = _load_wT(ctx, wk_ap, tag + "k")
    P.wait("act", tb2)
    _, tnk = emit_fm_proj(ctx, wch, wtok, list(range(NCH)), bias_col=bkt,
                          dram_out=Kd, norm_accum=True, umax=kmax)
    P.barrier()
    wtok, wch = _load_wT(ctx, wv_ap, tag + "v")
    P.wait("pool", tb3)
    tv = emit_v_proj(ctx, wch, wtok, Vd, bias_row=bvr)
    P.wait("sp", tnk)
    tkm = P.dma("sp", kmd[0:1, :], kmax)
    P.barrier()

    c1 = P.cc("AllGather", ALU.bypass, [Kd.tensor.ap().opt()], [Kg.tensor.ap().opt()], ctx.pairs)
    c2 = P.cc("AllGather", ALU.bypass, [Vd.tensor.ap().opt()], [Vg.tensor.ap().opt()], ctx.pairs)
    c3 = P.cc("AllReduce", ALU.max, [kmd.tensor.ap().opt()], [kmg.tensor.ap().opt()], ctx.pairs)
    P.wait("sp", c3)
    tkg = P.dma("sp", kmax, kmg[0:1, :])

    P.wait("dve", tkg, tnq)
    t = P.op("dve", lambda E_: E_.tensor_tensor(ctx.SF2[0:1, 64:65], qmax[0:1, 0:1], kmax[0:1, 0:1], op=ALU.mult))
    P.wait("act", t)
    t = P.op("act", lambda E_: E_.activation(ctx.SF2[0:1, 64:65], ctx.SF2[0:1, 64:65], AF.Sqrt))
    P.wait("dve", t)
    t = P.op("dve", lambda E_: E_.tensor_scalar(ctx.ROWB[0:1, 0:1], ctx.SF2[0:1, 64:65], -1.0, None, ALU.mult))
    P.wait("dve", t)
    t = P.settle("dve")
    P.wait("pe", t)
    tbb = P.op("pe", lambda E_: E_.matmul(ctx.pA[:, 0:1], ctx.ones_row[0:1, :],
                                          ctx.ROWB[0:1, 0:1], start=True, stop=True))
    P.wait("dve", tbb)
    tbc = P.op("dve", lambda E_: E_.tensor_copy(ctx.bias128[:, 0:1], ctx.pA[:, 0:1]))
    P.wait("dve", tbc)
    bias_tok = P.settle("dve")

    KT = [STR[:, 0:1024], STR[:, 1024:2048]]
    VT = [STR[:, 2048:3072], STR[:, 3072:4096]]
    QB = STR[:, 4096:8192]
    MB = STR[:, 8192:8704]
    EX = STR[:, 8704:9216]
    EE = [BIG[:, i * 512:(i + 1) * 512] for i in range(32)]

    kdma = {}
    kuse = {}
    prev_ph2 = None
    prev_recip = None
    for qc in range(4):
        qsl = slice(qc * 512, (qc + 1) * 512)
        P.wait("sp", prev_ph2)
        dq = P.dma("sp", QB.rearrange("p (c k) -> p c k", c=8),
                   Qd[:, qsl].rearrange("(c p) k -> p c k", p=128))
        tden = None
        texp_prev = None
        tmask_prev = None
        for ki in range(32):
            sl_ = ki % 2
            ps = ctx.pA if sl_ == 0 else ctx.pB
            rank, kloc = ki // 16, ki % 16
            P.wait("sp", kuse.get(("k", sl_)), c1)
            dk = P.dma("sp", KT[sl_].rearrange("p (c k) -> p c k", c=8),
                       Kg[rank * 1024:(rank + 1) * 1024, kloc * 128:(kloc + 1) * 128]
                       .rearrange("(c p) k -> p c k", p=128))
            kdma[("k", sl_)] = dk
            P.wait("pe", dk, dq, texp_prev, bias_tok)
            for c in range(NCH):
                tst = P.op("pe", lambda E_, c=c, ps=ps, sl_=sl_: E_.matmul(
                    ps[:, 0:512], KT[sl_][:, c * 128:(c + 1) * 128],
                    QB[:, c * 512:(c + 1) * 512],
                    start=(c == 0), stop=(c == NCH - 1)),
                    sem=None if c < NCH - 1 else "pe")
            kuse[("k", sl_)] = tst
            P.wait("act", tst, tmask_prev)
            tex = P.op("act", lambda E_, ps=ps: E_.activation(
                EX[:, :], ps[:, :512], AF.Exp, bias=ctx.bias128[:, 0:1]))
            texp_prev = tex
            P.wait("dve", tex, prev_ph2)
            tmk = P.op("dve", lambda E_, ki=ki, qsl=qsl: E_.tensor_scalar(
                MB[:, :], ctx.qg[:, qsl], ctx.kg[:, ki:ki + 1], None, ALU.is_ge))
            P.wait("dve", tmk)
            tmm = P.op("dve", lambda E_, ki=ki: E_.tensor_tensor(
                EE[ki][:, :], EX[:, :], MB[:, :], op=ALU.mult))
            tmask_prev = tmm
            P.wait("pe", tmm, prev_recip)
            tdn = P.op("pe", lambda E_, ki=ki: E_.matmul(
                ctx.pC[0:1, 0:512], ctx.ones_col[:, 0:1], EE[ki][:, :],
                start=(ki == 0), stop=(ki == 31)))
            tden = tdn
        P.wait("dve", tden)
        td = P.op("dve", lambda E_: E_.reciprocal(ctx.ROWS[0:1, 0:512], ctx.pC[0:1, 0:512]))
        P.wait("dve", td)
        td = P.op("dve", lambda E_: E_.tensor_copy(ctx.ROWB[0:1, 512:1024], ctx.ROWS[0:1, 0:512]))
        P.wait("dve", td)
        td2 = P.settle("dve")
        prev_recip = td2
        P.wait("pe", td2)
        tbc2 = P.op("pe", lambda E_: E_.matmul(ctx.pD[:, 0:512], ctx.ones_row[0:1, :],
                                               ctx.ROWB[0:1, 512:1024], start=True, stop=True))
        P.wait("act", tbc2)
        tnb = P.op("act", lambda E_: E_.copy(ctx.SF[:, 0:512], ctx.pD[:, 0:512]))
        pslots = [ctx.pA[:, 0:512], ctx.pA[:, 512:1024], ctx.pB[:, 0:512], ctx.pB[:, 512:1024],
                  ctx.pC[:, 0:512], ctx.pC[:, 512:1024], ctx.pD[:, 0:512], ctx.pD[:, 512:1024]]
        tpv = {}
        for ki in range(32):
            sl_ = ki % 2
            P.wait("sp", kuse.get(("v", sl_)), c2)
            dv = P.dma("sp", VT[sl_].rearrange("p (c k) -> p c k", c=8) if False else VT[sl_],
                       Vg[ki * 128:(ki + 1) * 128, :])
            P.wait("pe", dv)
            if ki < 2:
                P.wait("pe", tnb, texp_prev)
            for mc in range(NCH):
                tp = P.op("pe", lambda E_, ki=ki, mc=mc, sl_=sl_, pslots=pslots: E_.matmul(
                    pslots[mc], VT[sl_][:, mc * 128:(mc + 1) * 128], EE[ki][:, :],
                    start=(ki == 0), stop=(ki == 31)),
                    sem=None if mc < NCH - 1 else "pe")
            kuse[("v", sl_)] = tp
            tpv[ki] = tp
        P.wait("dve", tpv[31], tpv[30])
        tlast = None
        for mc in range(NCH):
            P.wait("dve", tlast)
            t1 = P.op("dve", lambda E_, mc=mc, pslots=pslots: E_.tensor_tensor(
                ctx.SF[:, 512:1024], pslots[mc], ctx.SF[:, 0:512], op=ALU.mult))
            P.wait("dve", t1)
            t2 = P.op("dve", lambda E_, mc=mc, qsl=qsl: E_.tensor_tensor(
                ctx.X[mc][:, qsl], ctx.X[mc][:, qsl], ctx.SF[:, 512:1024], op=ALU.add))
            tlast = t2
        prev_ph2 = tlast
    P.barrier()
    return tlast


def emit_performer(ctx: Ctx, tag, wqkv_ap, bqkv_ap, wout_ap, bout_ap, omega_ap):
    P = ctx.P
    STR, BIG = ctx.STR, ctx.BIG
    H, dh, M = 16, 64, 256

    qkvd = P.dram(f"{tag}_qkv", [3072, TPC], BF16).ap()
    kvd = P.dram(f"{tag}_kv", [H * 256, 65], F32).ap()
    kvs = P.dram(f"{tag}_kvs", [H * 256, 65], F32).ap()

    bqkvt = ctx.BT[:, 0:24]
    tb = P.dma("pool", bqkvt, bqkv_ap.rearrange("(c p) -> p c", p=128),
               allow_slow_non_contiguous=True)
    om = STR[0:65, 16384:16640]
    tom = P.dma("pool", om[0:64, :], omega_ap[:, :])
    P.wait("dve", tom)
    tom2 = P.op("dve", lambda E_: E_.memset(om[64:65, :], 1.0))
    P.wait("dve", tom2)
    tom3 = P.settle("dve")

    for grp in range(3):
        wtok, wch = _load_w_direct(ctx, wqkv_ap[:, grp * 1024:(grp + 1) * 1024], f"{tag}w{grp}")
        P.wait("pe", tb)
        tq, _ = emit_fm_proj(ctx, wch, wtok, list(range(NCH)),
                             bias_col=bqkvt[:, grp * 8:(grp + 1) * 8],
                             dram_out=qkvd, dram_row0=grp * 1024)
        P.barrier()

    if MHA_STOP == "pqkv":
        P.barrier()
        return (qkvd, kvd, kvs)
    kh = STR[0:64, 0:2048]
    k2 = STR[0:64, 2048:4096]
    kp = STR[:, 4096:8192]
    vh = STR[0:64, 8192:10240]
    vt = STR[:, 10240:11280]
    vtv = STR[:, 10240:11280].rearrange("p (t w) -> p t w", w=65) if False else None
    prev_kv = None
    for h in range(H):
        dk = P.dma("sp", kh[:, :], qkvd[1024 + h * 64:1024 + (h + 1) * 64, :])
        dv = P.dma("sp", vh[:, :], qkvd[2048 + h * 64:2048 + (h + 1) * 64, :])
        P.wait("pe", prev_kv, dv)
        P.wait("act", dk, prev_kv)
        tk2 = P.op("act", lambda E_: E_.activation(k2[:, :], kh[:, :], AF.Square))
        P.wait("pe", tk2, tom3)
        for tt in range(NT):
            tn = P.op("pe", lambda E_, tt=tt: E_.matmul(
                ctx.pD[0:128, tt:tt + 1] if False else ctx.pC[:, tt:tt + 1],
                k2[:, tt * 128:(tt + 1) * 128], ctx.ones_col[0:64, 0:1],
                start=True, stop=True),
                sem=None if tt < NT - 1 else "pe")
        P.wait("dve", tn)
        tsc = P.op("dve", lambda E_: E_.tensor_scalar(ctx.SF2[:, 0:16], ctx.pC[:, 0:16],
                                                      -0.5, None, ALU.mult))
        P.wait("dve", tsc)
        tsc2 = P.settle("dve")
        prev_exp = None
        for tt in range(NT):
            ps = ctx.pA if tt % 2 == 0 else ctx.pB
            P.wait("pe", prev_exp, tsc2)
            tmm = P.op("pe", lambda E_, tt=tt, ps=ps: E_.matmul(
                ps[:, 0:256], kh[:, tt * 128:(tt + 1) * 128], om[0:64, :],
                start=True, stop=True))
            P.wait("act", tmm)
            tex = P.op("act", lambda E_, tt=tt, ps=ps: E_.activation(
                kp[:, tt * 256:(tt + 1) * 256], ps[:, 0:256], AF.Exp,
                bias=ctx.SF2[:, tt:tt + 1]))
            prev_exp = tex
        prev_vt = None
        for tt in range(NT):
            P.wait("pe", prev_vt)
            tvt = P.op("pe", lambda E_, tt=tt: E_.matmul(
                ctx.pC[:, 256:384], vh[:, tt * 128:(tt + 1) * 128], ctx.ident[0:64, :],
                start=True, stop=True))
            P.wait("dve", tvt)
            tcv = P.op("dve", lambda E_: E_.tensor_copy(vt[:, 0:64], ctx.pC[:, 256:320]))
            P.wait("dve", tcv)
            tcv2 = P.op("dve", lambda E_: E_.memset(vt[:, 64:65], 1.0))
            P.wait("dve", tcv2)
            tcv3 = P.settle("dve")
            P.wait("pe", tcv3, prev_exp)
            for mc in range(2):
                tkv = P.op("pe", lambda E_, tt=tt, mc=mc: E_.matmul(
                    ctx.pD[:, mc * 512:mc * 512 + 65],
                    kp[:, tt * 256 + mc * 128:tt * 256 + (mc + 1) * 128],
                    vt[:, 0:65], start=(tt == 0), stop=(tt == NT - 1)),
                    sem=None if (mc < 1 or tt < NT - 1) else "pe")
            prev_vt = tkv if tt == NT - 1 else tvt
        P.wait("act", tkv)
        te1 = P.op("act", lambda E_: E_.copy(ctx.SF[:, 0:65], ctx.pD[:, 0:65]))
        P.wait("act", te1)
        te2 = P.op("act", lambda E_: E_.copy(ctx.SF[:, 512:577], ctx.pD[:, 512:577]))
        P.wait("sp", te2)
        s1 = P.dma("sp", kvd[h * 256:h * 256 + 128, :], ctx.SF[:, 0:65])
        s2 = P.dma("sp", kvd[h * 256 + 128:h * 256 + 256, :], ctx.SF[:, 512:577])
        prev_kv = s2
    P.barrier()

    if MHA_STOP == "pkv":
        P.barrier()
        c1 = P.cc("AllReduce", ALU.add, [kvd.tensor.ap().opt()], [kvs.tensor.ap().opt()], ctx.pairs)
        P.wait("pool", c1)
        return (qkvd, kvd, kvs)
    c1 = P.cc("AllReduce", ALU.add, [kvd.tensor.ap().opt()], [kvs.tensor.ap().opt()], ctx.pairs)
    P.wait("pool", c1)

    qa = STR[0:65, 0:2048]
    q2 = STR[0:64, 2048:4096]
    qp = STR[:, 4096:8192]
    kvb = STR[:, 16640:16770]
    O_all = [BIG[:, c * 2048:(c + 1) * 2048] for c in range(NCH)]
    prev_num = None
    prev_nb = None
    for h in range(H):
        P.wait("sp", prev_num)
        dq = P.dma("sp", qa[0:64, :], qkvd[h * 64:(h + 1) * 64, :])
        P.wait("pool", prev_num)
        dkv = P.dma("pool", kvb.rearrange("p (c w) -> p c w", c=2),
                    kvs[h * 256:(h + 1) * 256, :].rearrange("(c p) w -> p c w", p=128))
        P.wait("act", dq, prev_num)
        tq2 = P.op("act", lambda E_: E_.activation(q2[:, :], qa[0:64, :], AF.Square))
        P.wait("pe", tq2, prev_nb)
        for j in range(4):
            prow = ctx.pC[0:1, :] if j < 2 else ctx.pC[32:33, :]
            po = (j % 2) * 512
            tqn = P.op("pe", lambda E_, j=j, prow=prow, po=po: E_.matmul(
                prow[0:1, po:po + 512], ctx.ones_col[0:64, 0:1],
                q2[:, j * 512:(j + 1) * 512], start=True, stop=True),
                sem=None if j < 3 else "pe")
        P.wait("dve", tqn)
        tr1 = P.op("dve", lambda E_: E_.tensor_scalar(ctx.ROWS[0:1, 0:1024], ctx.pC[0:1, 0:1024],
                                                      -0.5, None, ALU.mult))
        P.wait("dve", tr1)
        tr2 = P.op("dve", lambda E_: E_.tensor_copy(qa[64:65, 0:1024], ctx.ROWS[0:1, 0:1024]))
        P.wait("dve", tr2)
        tr3 = P.op("dve", lambda E_: E_.tensor_scalar(ctx.ROWS[0:1, 0:1024], ctx.pC[32:33, 0:1024],
                                                      -0.5, None, ALU.mult))
        P.wait("dve", tr3)
        tr4 = P.op("dve", lambda E_: E_.tensor_copy(qa[64:65, 1024:2048], ctx.ROWS[0:1, 0:1024]))
        P.wait("dve", tr4)
        tr5 = P.settle("dve")
        prev_exq = None
        for mc in range(2):
            P.wait("pe", tr5, prev_exq)
            for j in range(4):
                psl = ctx.pA if j < 2 else ctx.pB
                po = (j % 2) * 512
                tmm = P.op("pe", lambda E_, mc=mc, j=j, psl=psl, po=po: E_.matmul(
                    psl[:, po:po + 512],
                    om[0:65, mc * 128:(mc + 1) * 128],
                    qa[0:65, j * 512:(j + 1) * 512],
                    start=True, stop=True),
                    sem=None if j < 3 else "pe")
            P.wait("act", tmm)
            tex = P.op("act", lambda E_, mc=mc: E_.activation(
                qp[:, mc * 2048:mc * 2048 + 1024], ctx.pA[:, :], AF.Exp))
            P.wait("act", tex)
            tex = P.op("act", lambda E_, mc=mc: E_.activation(
                qp[:, mc * 2048 + 1024:(mc + 1) * 2048], ctx.pB[:, :], AF.Exp))
            prev_exq = tex
        P.wait("pe", dkv, prev_exq)
        for mc in range(2):
            for j in range(4):
                tnm = P.op("pe", lambda E_, mc=mc, j=j: E_.matmul(
                    (ctx.pC if j < 2 else ctx.pD)[0:65, (j % 2) * 512:(j % 2) * 512 + 512],
                    kvb[:, mc * 65:(mc + 1) * 65],
                    qp[:, mc * 2048 + j * 512:mc * 2048 + (j + 1) * 512],
                    start=(mc == 0), stop=(mc == 1)),
                    sem=None if (mc < 1 or j < 3) else "pe")
        prev_num = tnm
        P.wait("dve", tnm)
        td = P.op("dve", lambda E_: E_.tensor_scalar(ctx.ROWS[0:1, 0:1024], ctx.pC[64:65, 0:1024],
                                                     1e-6, None, ALU.add))
        P.wait("dve", td)
        td = P.op("dve", lambda E_: E_.reciprocal(ctx.ROWS[0:1, 0:1024], ctx.ROWS[0:1, 0:1024]))
        P.wait("dve", td)
        td = P.op("dve", lambda E_: E_.tensor_copy(ctx.ROWB[0:1, 0:1024], ctx.ROWS[0:1, 0:1024]))
        P.wait("dve", td)
        td2 = P.op("dve", lambda E_: E_.tensor_scalar(ctx.ROWS[0:1, 0:1024], ctx.pD[64:65, 0:1024],
                                                      1e-6, None, ALU.add))
        P.wait("dve", td2)
        td2 = P.op("dve", lambda E_: E_.reciprocal(ctx.ROWS[0:1, 0:1024], ctx.ROWS[0:1, 0:1024]))
        P.wait("dve", td2)
        td2 = P.op("dve", lambda E_: E_.tensor_copy(ctx.ROWB[0:1, 1024:2048], ctx.ROWS[0:1, 0:1024]))
        P.wait("dve", td2)
        td3 = P.settle("dve")
        P.wait("pe", td3)
        for j in range(4):
            tbc = P.op("pe", lambda E_, j=j: E_.matmul(
                (ctx.pA if j < 2 else ctx.pB)[0:64, (j % 2) * 512:(j % 2) * 512 + 512],
                ctx.ones_row[0:1, 0:64],
                ctx.ROWB[0:1, j * 512:(j + 1) * 512], start=True, stop=True),
                sem=None if j < 3 else "pe")
        P.wait("act", tbc)
        tnb1 = P.op("act", lambda E_: E_.copy(ctx.SF2[0:64, 0:1024], ctx.pA[0:64, :]))
        P.wait("act", tnb1)
        tnb2 = P.op("act", lambda E_: E_.copy(ctx.SF[0:64, 0:1024], ctx.pB[0:64, :]))
        prev_nb = tnb2
        cch, rb = h // 2, 64 * (h % 2)
        P.wait("dve", tnb1, tnb2)
        to1 = P.op("dve", lambda E_, cch=cch, rb=rb: E_.tensor_tensor(
            O_all[cch][rb:rb + 64, 0:1024], ctx.pC[0:64, :], ctx.SF2[0:64, 0:1024], op=ALU.mult))
        P.wait("dve", to1)
        to2 = P.op("dve", lambda E_, cch=cch, rb=rb: E_.tensor_tensor(
            O_all[cch][rb:rb + 64, 1024:2048], ctx.pD[0:64, :], ctx.SF[0:64, 0:1024], op=ALU.mult))
        P.wait("dve", to2)
        prev_num = to2
    P.barrier()

    wtok, wch = _load_w_direct(ctx, wout_ap, tag + "o")
    bo = ctx.BT[:, 0:8]
    tbo = P.dma("pool", bo, bout_ap.rearrange("(c p) -> p c", p=128),
                allow_slow_non_contiguous=True)
    prev_ev = None
    for r in range(NCH):
        P.wait("pe", wtok, tbo, prev_ev)
        for c in range(NCH):
            for j in range(4):
                psl = ctx.pA if j < 2 else ctx.pB
                po = (j % 2) * 512
                tmm = P.op("pe", lambda E_, r=r, c=c, j=j, psl=psl, po=po: E_.matmul(
                    psl[:, po:po + 512], wch[c][:, r * 128:(r + 1) * 128],
                    O_all[c][:, j * 512:(j + 1) * 512],
                    start=(c == 0), stop=(c == NCH - 1)),
                    sem=None if (c < NCH - 1) else "pe")
        P.wait("dve", tmm)
        for j in range(4):
            psl = ctx.pA if j < 2 else ctx.pB
            po = (j % 2) * 512
            tev = P.op("dve", lambda E_, r=r, j=j, psl=psl, po=po: E_.scalar_tensor_tensor(
                ctx.X[r][:, j * 512:(j + 1) * 512], psl[:, po:po + 512], bo[:, r:r + 1],
                ctx.X[r][:, j * 512:(j + 1) * 512], ALU.add, ALU.add))
            P.wait("dve", tev)
        prev_ev = tev
    P.barrier()
    return prev_ev


SUBLAYERS = list(range(8))
GROUPS = [SUBLAYERS]


def build_group(subs, tag):
    P = Prog()
    F = lambda n, shp: P.dram(n, shp, F32, kind="ExternalInput").ap()
    x_in = F("x_in", [1024, TPC])
    ident_in = P.dram("identity_in", [128, 128], BF16, kind="ExternalInput").ap()
    qg_in = F("qgb_in", [128, TPC])
    kg_in = F("kg_in", [128, 32])
    y_out = P.dram("y_out", [1024, TPC], F32, kind="ExternalOutput").ap()

    w = {}
    w["l0aa_wqkv"] = F("L0_attn_a_Wqkv", [1024, 3072])
    w["l0aa_bqkv"] = F("L0_attn_a_bqkv", [3072])
    w["l0aa_wout"] = F("L0_attn_a_Wout", [1024, 1024])
    w["l0aa_bout"] = F("L0_attn_a_bout", [1024])
    w["l0aa_om"] = F("L0_attn_a_omega", [64, 256])
    w["l0ab_wq"] = F("L0_attn_b_Wq", [1024, 1024])
    w["l0ab_wk"] = F("L0_attn_b_Wk", [1024, 1024])
    w["l0ab_wv"] = F("L0_attn_b_Wv", [1024, 1024])
    w["l0f_g"] = F("L0_ffn_gate_w", [16, 1024])
    w["l0f_w1"] = F("L0_ffn_w1", [16 * 1024, 512])
    w["l0f_w3"] = F("L0_ffn_w3", [16 * 1024, 512])
    w["l0f_w2"] = F("L0_ffn_w2", [16 * 512, 1024])
    w["l1aa_wqkv"] = F("L1_attn_a_w_qkv", [1024, 3072])
    w["l1aa_wo"] = F("L1_attn_a_w_out", [1024, 1024])
    w["l1aa_bo"] = F("L1_attn_a_b_out", [1024])
    w["l1aa_g"] = F("L1_attn_a_ln_gamma", [1024])
    w["l1aa_b"] = F("L1_attn_a_ln_beta", [1024])
    w["l1ab_wq"] = F("L1_attn_b_Wq", [1024, 1024])
    w["l1ab_bq"] = F("L1_attn_b_bq", [1024])
    w["l1ab_wk"] = F("L1_attn_b_Wk", [1024, 1024])
    w["l1ab_bk"] = F("L1_attn_b_bk", [1024])
    w["l1ab_wv"] = F("L1_attn_b_Wv", [1024, 1024])
    w["l1ab_bv"] = F("L1_attn_b_bv", [1024])
    w["l1f_g"] = F("L1_ffn_gate_w", [1024, 8])
    w["l1f_wg"] = F("L1_ffn_w_gate", [8 * 1024, 2048])
    w["l1f_wu"] = F("L1_ffn_w_up", [8 * 1024, 2048])
    w["l1f_wd"] = F("L1_ffn_w_down", [8 * 2048, 1024])
    w["l1f_sg"] = F("L1_ffn_ws_gate", [1024, 2048])
    w["l1f_su"] = F("L1_ffn_ws_up", [1024, 2048])
    w["l1f_sd"] = F("L1_ffn_ws_down", [2048, 1024])

    ctx = Ctx(P)
    ctx.init_consts(x_in, ident_in, qg_in, kg_in)
    P.wait("act", ctx.init_tok)
    P.wait("pe", ctx.init_tok)
    P.wait("dve", ctx.init_tok)

    for s in subs:
        emit_ln(ctx)
        P.barrier()
        if s == 0:
            emit_performer(ctx, "p0", w["l0aa_wqkv"], w["l0aa_bqkv"],
                           w["l0aa_wout"], w["l0aa_bout"], w["l0aa_om"])
        elif s in (1, 3):
            emit_moe(ctx, w["l0f_g"], "eh", 16, 512, 4,
                     w["l0f_w1"], w["l0f_w3"], w["l0f_w2"])
        elif s == 2:
            emit_attn_mha(ctx, f"a0_{s}", 16, 64, w["l0ab_wq"], w["l0ab_wk"],
                          w["l0ab_wv"], transposed=True)
        elif s == 4:
            emit_attn_mha(ctx, "a1", 32, 32,
                          w["l1aa_wqkv"][:, 0:1024], w["l1aa_wqkv"][:, 1024:2048],
                          w["l1aa_wqkv"][:, 2048:3072], transposed=False,
                          w_out_ap=w["l1aa_wo"], b_out_ap=w["l1aa_bo"],
                          gamma_ap=w["l1aa_g"], beta_ap=w["l1aa_b"])
        elif s in (5, 7):
            emit_moe(ctx, w["l1f_g"], "he", 8, 2048, 2,
                     w["l1f_wg"], w["l1f_wu"], w["l1f_wd"],
                     shared=(w["l1f_sg"], w["l1f_su"], w["l1f_sd"]), eps20=True)
        elif s == 6:
            emit_attn_causal(ctx, "c0", w["l1ab_wq"], w["l1ab_bq"],
                             w["l1ab_wk"], w["l1ab_bk"], w["l1ab_wv"], w["l1ab_bv"])
        P.barrier()

    st = None
    for c in range(NCH):
        t = P.op("dve", lambda E_, c=c: E_.tensor_copy(ctx.SF[:, :], ctx.X[c][:, :]))
        P.wait("sp", t)
        st = P.dma("sp", y_out[c * 128:(c + 1) * 128, :], ctx.SF[:, :])
        P.wait("dve", st)
    P.wait("sp", st)
    return P.build()


_IDENT = None


def kernel(**inputs):
    global _IDENT
    import numpy as _np
    xf = inputs["L0_attn_a_x"]
    ident = _np.eye(128, dtype=ml_dtypes.bfloat16)
    kg = _np.ascontiguousarray(
        _np.arange(4096, dtype=_np.float32).reshape(32, 128).T)

    base = {
        "L0_attn_a_Wqkv": inputs["L0_attn_a_Wqkv"],
        "L0_attn_a_bqkv": inputs["L0_attn_a_bqkv"],
        "L0_attn_a_Wout": inputs["L0_attn_a_Wout"],
        "L0_attn_a_bout": inputs["L0_attn_a_bout"],
        "L0_attn_a_omega": inputs["L0_attn_a_omega"],
        "L0_attn_b_Wq": inputs["L0_attn_b_Wq"],
        "L0_attn_b_Wk": inputs["L0_attn_b_Wk"],
        "L0_attn_b_Wv": inputs["L0_attn_b_Wv"],
        "L0_ffn_gate_w": inputs["L0_ffn_gate_w"],
        "L0_ffn_w1": _np.ascontiguousarray(inputs["L0_ffn_w1"].reshape(16 * 1024, 512)),
        "L0_ffn_w3": _np.ascontiguousarray(inputs["L0_ffn_w3"].reshape(16 * 1024, 512)),
        "L0_ffn_w2": _np.ascontiguousarray(inputs["L0_ffn_w2"].reshape(16 * 512, 1024)),
        "L1_attn_a_w_qkv": inputs["L1_attn_a_w_qkv"],
        "L1_attn_a_w_out": inputs["L1_attn_a_w_out"],
        "L1_attn_a_b_out": inputs["L1_attn_a_b_out"],
        "L1_attn_a_ln_gamma": inputs["L1_attn_a_ln_gamma"],
        "L1_attn_a_ln_beta": inputs["L1_attn_a_ln_beta"],
        "L1_attn_b_Wq": inputs["L1_attn_b_Wq"],
        "L1_attn_b_bq": inputs["L1_attn_b_bq"],
        "L1_attn_b_Wk": inputs["L1_attn_b_Wk"],
        "L1_attn_b_bk": inputs["L1_attn_b_bk"],
        "L1_attn_b_Wv": inputs["L1_attn_b_Wv"],
        "L1_attn_b_bv": inputs["L1_attn_b_bv"],
        "L1_ffn_gate_w": inputs["L1_ffn_gate_w"],
        "L1_ffn_w_gate": _np.ascontiguousarray(inputs["L1_ffn_w_gate"].reshape(8 * 1024, 2048)),
        "L1_ffn_w_up": _np.ascontiguousarray(inputs["L1_ffn_w_up"].reshape(8 * 1024, 2048)),
        "L1_ffn_w_down": _np.ascontiguousarray(inputs["L1_ffn_w_down"].reshape(8 * 2048, 1024)),
        "L1_ffn_ws_gate": inputs["L1_ffn_ws_gate"],
        "L1_ffn_ws_up": inputs["L1_ffn_ws_up"],
        "L1_ffn_ws_down": inputs["L1_ffn_ws_down"],
        "identity_in": ident,
        "kg_in": kg,
    }

    xcur = [None] * CORES
    for c in range(CORES):
        b, h = c // 2, c % 2
        xcur[c] = _np.ascontiguousarray(xf[b, h * TPC:(h + 1) * TPC, :].T)

    for gi, subs in enumerate(GROUPS):
        nc = build_group(subs, f"g{gi}")
        in_maps = []
        for c in range(CORES):
            b, h = c // 2, c % 2
            im = dict(base)
            im["x_in"] = xcur[c]
            im["qgb_in"] = _np.tile(
                _np.arange(h * TPC, (h + 1) * TPC, dtype=_np.float32), (128, 1))
            in_maps.append(im)
        res = run_bass_kernel_spmd(nc, in_maps, core_ids=list(range(CORES)))
        for c in range(CORES):
            xcur[c] = _np.ascontiguousarray(res.results[c]["y_out"])

    out = _np.empty((B, N, D), _np.float32)
    for c in range(CORES):
        b, h = c // 2, c % 2
        out[b, h * TPC:(h + 1) * TPC, :] = xcur[c].T
    return out


# version 03
# speedup vs baseline: 1.4016x; 1.4016x over previous
import numpy as np
import ml_dtypes
import concourse.bass as bass
import concourse.mybir as mybir
from concourse.bass_utils import run_bass_kernel_spmd

F32 = mybir.dt.float32
BF16 = mybir.dt.bfloat16
AF = mybir.ActivationFunctionType
ALU = mybir.AluOpType
AX = mybir.AxisListType.X

B, N, D = 4, 4096, 1024
TPC = 2048
NCH = 8
NT = TPC // 128
CORES = 8

ENG_ATTRS = {"pe": "tensor", "act": "scalar", "dve": "vector", "sp": "sync", "pool": "gpsimd"}
ENG_SEM = {"pe": ("pe", 1), "act": ("act", 1), "dve": ("dve", 1),
           "sp": ("spd", 16), "pool": ("poold", 16)}


class Prog:

    def __init__(self):
        self.nc = bass.Bass("TRN2", target_bir_lowering=False)
        self.q = {e: [] for e in ENG_ATTRS}
        self.counts = {}
        self.waited = {e: {} for e in ENG_ATTRS}
        self._sem_names = []
        self._ctx = []

    def sbuf(self, name, shape, dtype):
        g = self.nc.sbuf_tensor(name, shape, dtype)
        t = g.__enter__()
        self._ctx.append(g)
        return t

    def psum(self, name, shape, dtype=F32):
        g = self.nc.psum_tensor(name, shape, dtype)
        t = g.__enter__()
        self._ctx.append(g)
        return t

    def dram(self, name, shape, dtype, kind="Internal"):
        return self.nc.dram_tensor(name, shape, dtype, kind=kind)

    def _use_sem(self, name):
        if name not in self.counts:
            self.counts[name] = 0
            self._sem_names.append(name)

    def wait(self, eng, *tokens):
        for t in tokens:
            if t is None:
                continue
            sem, val = t
            if val <= 0:
                continue
            self._use_sem(sem)
            if self.waited[eng].get(sem, 0) >= val:
                continue
            self.waited[eng][sem] = val
            self.q[eng].append(("wait", sem, val))

    def op(self, eng, fn, sem="auto", n=None):
        if sem == "auto":
            sem, dn = ENG_SEM[eng]
            n = dn if n is None else n
        if sem is not None:
            self._use_sem(sem)
            self.counts[sem] += (n if n is not None else 1)
            tok = (sem, self.counts[sem])
        else:
            tok = None
        self.q[eng].append(("op", fn, sem, (n if n is not None else 1)))
        return tok

    def dma(self, eng, out, in_, **kw):
        return self.op(eng, lambda E: E.dma_start(out=out, in_=in_, **kw))

    def settle(self, eng):
        j = self._junk
        if eng == "act":
            return self.op("act", lambda E: E.copy(j[0:1, 0:8], j[0:1, 8:16]))
        if eng == "dve":
            return self.op("dve", lambda E: E.tensor_copy(j[0:1, 0:8], j[0:1, 8:16]))
        raise ValueError(eng)

    def cc(self, kind, op_, ins, outs, groups):
        return self.op("pool", lambda E: E.collective_compute(
            kind, op_, replica_groups=groups, ins=ins, outs=outs), sem="cc", n=1)

    def barrier(self):
        toks = []
        for eng in ENG_ATTRS:
            for item in reversed(self.q[eng]):
                if item[0] == "op" and item[2] is not None:
                    toks.append((item[2], self.counts[item[2]]))
                    break
        for eng in ENG_ATTRS:
            self.wait(eng, *toks)
        return toks

    def build(self):
        self._sems = {}
        sem_ctx = []
        for name in self._sem_names:
            g = self.nc.semaphore(name)
            self._sems[name] = g.__enter__()
            sem_ctx.append(g)
        bg = self.nc.Block()
        block = bg.__enter__()
        for eng, attr in ENG_ATTRS.items():
            items = self.q[eng]
            if not items:
                continue

            def body(E, items=items):
                for item in items:
                    if item[0] == "wait":
                        E.wait_ge(self._sems[item[1]], item[2])
                    else:
                        _, fn, sem, n = item
                        instr = fn(E)
                        if sem is not None:
                            instr.then_inc(self._sems[sem], n)
            getattr(block, attr)(body)
        bg.__exit__(None, None, None)
        for g in reversed(sem_ctx):
            g.__exit__(None, None, None)
        for g in reversed(self._ctx):
            g.__exit__(None, None, None)
        return self.nc


class Ctx:
    def __init__(self, P: Prog):
        self.P = P
        self.X = [P.sbuf(f"X{c}", [128, TPC], F32) for c in range(NCH)]
        self.XN = [P.sbuf(f"XN{c}", [128, TPC], BF16) for c in range(NCH)]
        self.BIG = P.sbuf("BIG", [128, 16896], BF16)
        self.STR = P.sbuf("STR", [128, 16896], BF16)
        self.SF = P.sbuf("SF", [128, 2048], F32)
        self.SF2 = P.sbuf("SF2", [128, 1024], F32)
        self.ROWS = P.sbuf("ROWS", [1, 2048], F32)
        self.ROWB = P.sbuf("ROWB", [1, 2048], BF16)
        self.ident = P.sbuf("identsb", [128, 128], BF16)
        self.ones_col = P.sbuf("ones_col", [128, 1], BF16)
        self.ones_row = P.sbuf("ones_row", [1, 128], BF16)
        self.qg = P.sbuf("qgsb", [128, TPC], F32)
        self.kg = P.sbuf("kgsb", [128, 32], F32)
        self.bias128 = P.sbuf("bias128", [128, 4], F32)
        P._junk = P.sbuf("junk", [1, 16], F32)
        self.BT = P.sbuf("BT", [128, 24], F32)
        self.pA = P.psum("pA", [128, 1024])
        self.pB = P.psum("pB", [128, 1024])
        self.pC = P.psum("pC", [128, 1024])
        self.pD = P.psum("pD", [128, 1024])
        self.pairs = [[0, 1], [2, 3], [4, 5], [6, 7]]

    def init_consts(self, x_in, ident_in, qg_in, kg_in):
        P = self.P
        for c in range(NCH):
            tx = P.dma("sp", self.X[c][:, :], x_in[c * 128:(c + 1) * 128, :])
        ti = P.dma("sp", self.ident[:, :], ident_in[:, :])
        tq = P.dma("sp", self.qg[:, :], qg_in[:, :])
        tk = P.dma("sp", self.kg[:, :], kg_in[:, :])
        t2 = P.op("dve", lambda E: E.memset(self.ones_col[:, :], 1.0))
        t2 = P.op("dve", lambda E: E.memset(self.ones_row[:, :], 1.0))
        self.init_tok = tk
        self.init_dve = t2


DEBUG_BARRIER = False
LN_HALVES = 2
MOE_DBG = False
MHA_STOP = None


def _dbg_bar(P):
    if DEBUG_BARRIER:
        P.barrier()


def emit_ln(ctx: Ctx, eps=1e-5, src=None, dst=None, gamma=None, beta=None,
            resid=False):
    P = ctx.P
    X = src or ctx.X
    XN = dst or ctx.XN
    src_bf = X is not ctx.X
    pD = ctx.pD
    x2b = ctx.STR
    last = None
    stats_tok = None
    for half in range(LN_HALVES):
        sl = slice(half * 1024, (half + 1) * 1024)
        if stats_tok is not None:
            P.wait("act", stats_tok)
        copy_toks, sq_toks = [], []
        for c in range(NCH):
            if not src_bf:
                tcopy = P.op("act", lambda E, c=c, sl=sl: E.copy(XN[c][:, sl], X[c][:, sl]))
            else:
                tcopy = None
            tsq = P.op("act", lambda E, c=c, sl=sl: E.activation(
                x2b[:, c * 2048: c * 2048 + 1024], X[c][:, sl], AF.Square))
            copy_toks.append(tcopy)
            sq_toks.append(tsq)
        if last is not None:
            P.wait("pe", last)
        for c in range(NCH):
            P.wait("pe", copy_toks[c], sq_toks[c])
            srcc = XN[c] if not src_bf else X[c]
            for j in range(2):
                jsl = slice(j * 512, (j + 1) * 512)
                P.op("pe", lambda E, c=c, j=j, jsl=jsl, srcc=srcc, sl=sl: E.matmul(
                    pD[0:1, jsl], ctx.ones_col[:, 0:1],
                    srcc[:, sl][:, jsl], start=(c == 0), stop=(c == NCH - 1)),
                    sem=None)
                tok = P.op("pe", lambda E, c=c, j=j, jsl=jsl: E.matmul(
                    pD[32:33, jsl], ctx.ones_col[:, 0:1],
                    x2b[:, c * 2048: c * 2048 + 1024][:, jsl],
                    start=(c == 0), stop=(c == NCH - 1)))
        stats_tok = tok
        _dbg_bar(P)
        R, RB = ctx.ROWS, ctx.ROWB
        m = R[0:1, 0:1024]
        var = R[0:1, 1024:2048]
        P.wait("dve", stats_tok)
        t = P.op("dve", lambda E, m=m: E.tensor_scalar(m, pD[0:1, 0:1024], 1.0 / D, None, ALU.mult))
        P.wait("dve", t)
        t = P.op("dve", lambda E, m=m, var=var: E.tensor_tensor(var, m, m, op=ALU.mult))
        P.wait("dve", t)
        t = P.op("dve", lambda E, var=var: E.scalar_tensor_tensor(
            var, pD[32:33, 0:1024], 1.0 / D, var, ALU.mult, ALU.subtract))
        P.wait("dve", t)
        t = P.op("dve", lambda E, var=var: E.tensor_scalar(var, var, float(eps), None, ALU.add))
        P.wait("act", t)
        t = P.op("act", lambda E, var=var: E.activation(var, var, AF.Sqrt))
        P.wait("dve", t)
        t = P.op("dve", lambda E, var=var: E.reciprocal(var, var))
        P.wait("dve", t)
        t2 = P.op("dve", lambda E, RB=RB, var=var: E.tensor_copy(RB[0:1, 0:1024], var))
        P.wait("dve", t2)
        t1 = P.op("dve", lambda E, m=m, var=var: E.tensor_tensor(m, m, var, op=ALU.mult))
        P.wait("dve", t1)
        t3 = P.op("dve", lambda E, RB=RB, m=m: E.tensor_copy(RB[0:1, 1024:2048], m))
        P.wait("dve", t3)
        t3s = P.settle("dve")
        _dbg_bar(P)
        P.wait("pe", t2, t3, t3s, ctx.init_dve)
        for j in range(2):
            jsl = slice(j * 512, (j + 1) * 512)
            ta = P.op("pe", lambda E, jsl=jsl, RB=RB: E.matmul(
                ctx.pA[:, jsl], ctx.ones_row[0:1, :], RB[0:1, jsl],
                start=True, stop=True))
            tb = P.op("pe", lambda E, jsl=jsl, j=j, RB=RB: E.matmul(
                ctx.pB[:, jsl], ctx.ones_row[0:1, :], RB[0:1, 1024 + j * 512:1024 + (j + 1) * 512],
                start=True, stop=True))
        _dbg_bar(P)
        P.wait("dve", ta, tb)
        for c in range(NCH):
            t = P.op("dve", lambda E, c=c, sl=sl: E.tensor_tensor(
                ctx.SF[:, 0:1024], X[c][:, sl], ctx.pA[:, :], op=ALU.mult))
            P.wait("dve", t)
            if gamma is None:
                t = P.op("dve", lambda E, c=c, sl=sl: E.tensor_tensor(
                    XN[c][:, sl], ctx.SF[:, 0:1024], ctx.pB[:, :], op=ALU.subtract))
            else:
                t = P.op("dve", lambda E, c=c: E.tensor_tensor(
                    ctx.SF2[:, 0:1024], ctx.SF[:, 0:1024], ctx.pB[:, :], op=ALU.subtract))
                P.wait("dve", t)
                t = P.op("dve", lambda E, c=c, sl=sl: E.tensor_scalar(
                    XN[c][:, sl], ctx.SF2[:, 0:1024], gamma[:, c:c + 1], beta[:, c:c + 1],
                    ALU.mult, ALU.add))
            P.wait("dve", t)
        last = t
        _dbg_bar(P)
    return last


def emit_moe(ctx: Ctx, gate_ap, gate_layout, E, INTER, topk,
             w1_ap, w3_ap, w2_ap, shared=None, eps20=False):
    P = ctx.P
    XN, X = ctx.XN, ctx.X
    pA, pB, pC, pD = ctx.pA, ctx.pB, ctx.pC, ctx.pD
    SF, SF2 = ctx.SF, ctx.SF2
    NIC = INTER // 128
    W1B = [0, 2048]
    W3B = [4096, 6144]
    W2B = [8192, 10240]
    SCO = 12288
    GWT = 12800
    GWR = 14000
    STR = ctx.STR

    gwt = [STR[:, GWT + c * E: GWT + (c + 1) * E] for c in range(NCH)]
    if gate_layout == "he":
        tg = P.dma("pool", STR[:, GWT:GWT + NCH * E].rearrange("p (c e) -> p c e", c=NCH),
                   gate_ap.rearrange("(c p) e -> p c e", p=128))
        gw_tok = tg
    else:
        traw = P.dma("pool", STR[0:16, GWR:GWR + 1024], gate_ap[:, :])
        P.wait("pe", traw, ctx.init_tok)
        tcp = None
        for c in range(NCH):
            P.wait("pe", tcp)
            t = P.op("pe", lambda E_, c=c: E_.matmul(
                pD[:, 0:128], STR[0:16, GWR + c * 128:GWR + (c + 1) * 128],
                ctx.ident[0:16, :], start=True, stop=True))
            P.wait("act", t)
            tcp = P.op("act", lambda E_, c=c: E_.copy(gwt[c][:, :], pD[:, 0:E]))
        gw_tok = P.settle("act")

    if MOE_DBG:
        P.wait("dve", gw_tok)
        tdbg = P.op("dve", lambda E_: E_.tensor_copy(SF[:, 1792:1792 + NCH * E], STR[:, GWT:GWT + NCH * E]))
        P.wait("dve", tdbg)
    P.wait("pe", gw_tok)
    for i in range(NT):
        for c in range(NCH):
            tl = P.op("pe", lambda E_, i=i, c=c: E_.matmul(
                pC[:, i * E:(i + 1) * E], XN[c][:, i * 128:(i + 1) * 128],
                gwt[c][:, :], start=(c == 0), stop=(c == NCH - 1)),
                sem=None if c < NCH - 1 else "pe")
    logits_tok = tl
    if MOE_DBG:
        P.wait("dve", logits_tok)
        tdb = P.op("dve", lambda E_: E_.tensor_copy(SF[:, 1536:1536 + 16 * E], pC[:, 0:16 * E]))
        P.wait("dve", tdb)

    ex = SF[:, 0:16 * E]
    work = SF[:, 512:512 + 16 * E]
    msk = SF[:, 1024:1024 + 16 * E]
    mx = SF2[:, 0:16]
    Ssum = SF2[:, 16:32]
    den = SF2[:, 48:64]

    def v3(ap_):
        return ap_.rearrange("p (t e) -> p t e", e=E)

    def b3(ap16):
        return ap16.rearrange("p (t o) -> p t o", o=1).broadcast_to([128, 16, E])

    P.wait("dve", logits_tok)
    t = P.op("dve", lambda E_: E_.reduce_max(mx.rearrange("p (t o) -> p t o", o=1),
                                             v3(pC[:, 0:16 * E]), axis=AX))
    P.wait("dve", t)
    t = P.op("dve", lambda E_: E_.tensor_tensor(v3(ex), v3(pC[:, 0:16 * E]), b3(mx), op=ALU.subtract))
    P.wait("act", t)
    t = P.op("act", lambda E_: E_.activation(ex, ex, AF.Exp))
    P.wait("dve", t)
    t = P.op("dve", lambda E_: E_.reduce_sum(Ssum.rearrange("p (t o) -> p t o", o=1), v3(ex), axis=AX))
    P.wait("dve", t)
    t = P.op("dve", lambda E_: E_.tensor_copy(work, ex))
    for r in range(topk - 1):
        P.wait("dve", t)
        t = P.op("dve", lambda E_: E_.reduce_max(mx.rearrange("p (t o) -> p t o", o=1), v3(work), axis=AX))
        P.wait("dve", t)
        t = P.op("dve", lambda E_: E_.tensor_tensor(v3(msk), v3(work), b3(mx), op=ALU.is_lt))
        P.wait("dve", t)
        t = P.op("dve", lambda E_: E_.tensor_tensor(work, work, msk, op=ALU.mult))
    P.wait("dve", t)
    t = P.op("dve", lambda E_: E_.reduce_max(mx.rearrange("p (t o) -> p t o", o=1), v3(work), axis=AX))
    P.wait("dve", t)
    t = P.op("dve", lambda E_: E_.tensor_tensor(v3(msk), v3(ex), b3(mx), op=ALU.is_ge))
    P.wait("dve", t)
    t = P.op("dve", lambda E_: E_.tensor_tensor(ex, ex, msk, op=ALU.mult))
    P.wait("dve", t)
    t = P.op("dve", lambda E_: E_.reduce_sum(den.rearrange("p (t o) -> p t o", o=1), v3(ex), axis=AX))
    P.wait("dve", t)
    if eps20:
        t = P.op("dve", lambda E_: E_.scalar_tensor_tensor(den, Ssum, 1e-20, den, ALU.mult, ALU.add))
        P.wait("dve", t)
    t = P.op("dve", lambda E_: E_.reciprocal(den, den))
    P.wait("dve", t)
    t = P.op("dve", lambda E_: E_.tensor_tensor(v3(STR[:, SCO:SCO + 16 * E]), v3(ex), b3(den), op=ALU.mult))
    scores_tok = t

    experts = [(e, w1_ap, w3_ap, w2_ap, True) for e in range(E)]
    if shared is not None:
        experts.append((0, shared[0], shared[1], shared[2], False))

    dma_tok = {}
    use_tok = {}
    halfsl = [slice(0, 1024), slice(1024, 2048)]
    prev_silu = None
    prev_umul = None
    prev_tact = None
    prev_down = None
    prev_xadd = {}
    bcast_read_tok = None

    for half in range(2):
        hs = halfsl[half]
        for ei, (e, a1, a3, a2, scaled) in enumerate(experts):
            slot = ei % 2
            if scaled:
                tts = []
                P.wait("pe", scores_tok)
                if bcast_read_tok is not None:
                    P.wait("pe", bcast_read_tok)
                for il in range(8):
                    i = half * 8 + il
                    tt = P.op("pe", lambda E_, i=i, il=il, e=e: E_.matmul(
                        pD[0:1, il * 128:(il + 1) * 128],
                        STR[:, SCO + i * E + e: SCO + i * E + e + 1],
                        ctx.ident[:, :], start=True, stop=True))
                    tts.append(tt)
                P.wait("dve", tts[-1])
                trow = P.op("dve", lambda E_: E_.tensor_copy(ctx.ROWB[0:1, 0:1024], pD[0:1, 0:1024]))
                P.wait("dve", trow)
                trow = P.settle("dve")
                P.wait("pe", trow)
                for j in range(2):
                    tb = P.op("pe", lambda E_, j=j: E_.matmul(
                        pD[:, j * 512:(j + 1) * 512], ctx.ones_row[0:1, :],
                        ctx.ROWB[0:1, j * 512:(j + 1) * 512], start=True, stop=True))
                bcast_tok = tb
            else:
                bcast_tok = None

            act_toks = []

            def _issue_w13(icx):
                sl2 = icx % 2
                for kind, base, wap in (("w1", W1B, a1), ("w3", W3B, a3)):
                    key = (kind, sl2)
                    P.wait("pool", use_tok.get(key))
                    src = wap[e * 1024:(e + 1) * 1024, icx * 128:(icx + 1) * 128] if scaled else \
                        wap[:, icx * 128:(icx + 1) * 128]
                    dma_tok[key] = P.dma("pool", STR[:, base[sl2]:base[sl2] + 1024]
                                         .rearrange("p (c i) -> p c i", c=NCH),
                                         src.rearrange("(c p) i -> p c i", p=128))

            _issue_w13(0)
            for ic in range(NIC):
                wslot = ic % 2
                if ic + 1 < NIC:
                    _issue_w13(ic + 1)
                P.wait("pe", dma_tok[("w1", wslot)], dma_tok[("w3", wslot)])
                P.wait("pe", prev_silu)
                P.wait("pe", prev_umul)
                for c in range(NCH):
                    for j in range(2):
                        tg_ = P.op("pe", lambda E_, c=c, j=j, base=W1B[wslot], hs=hs: E_.matmul(
                            pB[:, j * 512:(j + 1) * 512],
                            STR[:, base + c * 128:base + (c + 1) * 128],
                            XN[c][:, hs][:, j * 512:(j + 1) * 512],
                            start=(c == 0), stop=(c == NCH - 1)),
                            sem=None if c < NCH - 1 else "pe")
                for c in range(NCH):
                    for j in range(2):
                        tu_ = P.op("pe", lambda E_, c=c, j=j, base=W3B[wslot], hs=hs: E_.matmul(
                            pC[:, j * 512:(j + 1) * 512],
                            STR[:, base + c * 128:base + (c + 1) * 128],
                            XN[c][:, hs][:, j * 512:(j + 1) * 512],
                            start=(c == 0), stop=(c == NCH - 1)),
                            sem=None if c < NCH - 1 else "pe")
                use_tok[("w1", wslot)] = tg_
                use_tok[("w3", wslot)] = tu_
                P.wait("act", tg_, prev_tact)
                tsil = P.op("act", lambda E_: E_.activation(SF2[:, 0:1024], pB[:, :], AF.Silu))
                prev_silu = tsil
                P.wait("dve", tsil, tu_)
                tum = P.op("dve", lambda E_: E_.tensor_tensor(SF2[:, 0:1024], SF2[:, 0:1024], pC[:, :], op=ALU.mult))
                prev_umul = tum
                P.wait("dve", tum, prev_down)
                if scaled:
                    P.wait("dve", bcast_tok)
                    tact = P.op("dve", lambda E_, ic=ic: E_.tensor_tensor(
                        ctx.BIG[:, ic * 1024:(ic + 1) * 1024], SF2[:, 0:1024], pD[:, :], op=ALU.mult))
                    bcast_read_tok = tact
                else:
                    tact = P.op("dve", lambda E_, ic=ic: E_.tensor_copy(
                        ctx.BIG[:, ic * 1024:(ic + 1) * 1024], SF2[:, 0:1024]))
                P.wait("dve", tact)
                prev_tact = tact
                act_toks.append(tact)

            w2tok = {}

            def _issue_w2(hcx):
                sl2 = hcx % 2
                key = ("w2", sl2)
                P.wait("pool", use_tok.get(key))
                src2 = a2[e * INTER:(e + 1) * INTER, hcx * 128:(hcx + 1) * 128] if scaled else \
                    a2[:, hcx * 128:(hcx + 1) * 128]
                w2tok[hcx] = P.dma("pool", STR[:, W2B[sl2]:W2B[sl2] + NIC * 128]
                                   .rearrange("p (c i) -> p c i", c=NIC),
                                   src2.rearrange("(c p) i -> p c i", p=128))

            _issue_w2(0)
            for hc in range(NCH):
                wslot = hc % 2
                if hc + 1 < NCH:
                    _issue_w2(hc + 1)
                P.wait("pe", w2tok[hc], act_toks[-1])
                P.wait("pe", prev_xadd.get("last"))
                for ic in range(NIC):
                    for j in range(2):
                        td_ = P.op("pe", lambda E_, ic=ic, j=j, base=W2B[wslot]: E_.matmul(
                            pA[:, j * 512:(j + 1) * 512],
                            STR[:, base + ic * 128:base + (ic + 1) * 128],
                            ctx.BIG[:, ic * 1024 + j * 512: ic * 1024 + j * 512 + 512],
                            start=(ic == 0), stop=(ic == NIC - 1)),
                            sem=None if ic < NIC - 1 else "pe")
                use_tok[("w2", wslot)] = td_
                prev_down = td_
                P.wait("dve", td_)
                tx = P.op("dve", lambda E_, hc=hc, hs=hs: E_.tensor_tensor(
                    X[hc][:, hs], X[hc][:, hs], pA[:, :], op=ALU.add))
                prev_xadd["last"] = tx
    return tx


def _load_wT(ctx, w_ap, tag, n=1024):
    P = ctx.P
    STR = ctx.STR
    raw = STR[:, 0:8192]
    wt = STR[:, 8192:16384]
    t_ld = P.dma("pool", raw.rearrange("p (c i) -> p c i", c=8),
                 w_ap.rearrange("(c p) i -> p c i", p=128))
    P.wait("pe", t_ld, ctx.init_tok)
    tcp = None
    for r in range(8):
        P.wait("pe", tcp)
        for c in range(8):
            tt = P.op("pe", lambda E_, r=r, c=c: E_.matmul(
                ctx.pD[:, c * 128:(c + 1) * 128],
                raw.rearrange("p (c i) -> p c i", c=8)[:, c, r * 128:(r + 1) * 128],
                ctx.ident[:, :], start=True, stop=True),
                sem=None if c < 7 else "pe")
        P.wait("act", tt)
        tcp = P.op("act", lambda E_, r=r: E_.copy(wt[:, r * 1024:(r + 1) * 1024], ctx.pD[:, :]))
    tok = P.settle("act")
    chunks = [wt[:, r * 1024:(r + 1) * 1024] for r in range(8)]
    return tok, chunks


def _load_w_direct(ctx, w_ap, tag, dout=1024, off=0):
    P = ctx.P
    n_el = 8 * dout
    buf = ctx.STR[:, off:off + n_el]
    t_ld = P.dma("pool", buf.rearrange("p (c i) -> p c i", c=8),
                 w_ap.rearrange("(c p) i -> p c i", p=128))
    chunks = [buf[:, c * dout:(c + 1) * dout] for c in range(8)]
    return t_ld, chunks


def emit_fm_proj(ctx, w_chunks, w_tok, r_list, scale=None, bias_col=None,
                 dram_out=None, dram_row0=0, norm_units=None, umax=None,
                 norm_accum=False, extra_wait=None, src_chunks=None,
                 dst_chunks=None):
    P = ctx.P
    src = src_chunks or ctx.XN
    stage = ctx.BIG[:, 0:2048]
    sq = ctx.BIG[:, 2048:4096]
    last_dma = None
    tnorm = None
    prev_ev = None
    nr_total = len(r_list)
    for ri, r in enumerate(r_list):
        P.wait("pe", w_tok, extra_wait)
        P.wait("pe", prev_ev)
        for c in range(NCH):
            for j in range(4):
                psl = ctx.pA if j < 2 else ctx.pB
                po = (j % 2) * 512
                tmm = P.op("pe", lambda E_, r=r, c=c, j=j, psl=psl, po=po, src=src: E_.matmul(
                    psl[:, po:po + 512], w_chunks[c][:, r * 128:(r + 1) * 128],
                    src[c][:, j * 512:(j + 1) * 512],
                    start=(c == 0), stop=(c == NCH - 1)),
                    sem=None if (c < NCH - 1) else "pe")
        P.wait("act", tmm, last_dma)
        if dst_chunks is not None:
            stage = dst_chunks[r]
        for j in range(4):
            psl = ctx.pA if j < 2 else ctx.pB
            po = (j % 2) * 512
            if bias_col is not None:
                tev = P.op("act", lambda E_, r=r, j=j, psl=psl, po=po, stage=stage: E_.activation(
                    stage[:, j * 512:(j + 1) * 512], psl[:, po:po + 512], AF.Identity,
                    bias=bias_col[:, r:r + 1], scale=(scale if scale is not None else 1.0)))
            elif scale is not None:
                tev = P.op("act", lambda E_, r=r, j=j, psl=psl, po=po, stage=stage: E_.activation(
                    stage[:, j * 512:(j + 1) * 512], psl[:, po:po + 512], AF.Copy,
                    scale=scale))
            else:
                tev = P.op("act", lambda E_, r=r, j=j, psl=psl, po=po, stage=stage: E_.copy(
                    stage[:, j * 512:(j + 1) * 512], psl[:, po:po + 512]))
        if dram_out is not None:
            P.wait("sp", tev)
            last_dma = P.dma("sp", dram_out[dram_row0 + r * 128:dram_row0 + (r + 1) * 128, :],
                             stage[:, :])
        units = []
        if norm_units is not None:
            units = norm_units.get(r, [])
        if units or norm_accum:
            P.wait("act", tev, tnorm)
            tsq = P.op("act", lambda E_, stage=stage: E_.activation(sq[:, :], stage[:, :], AF.Square))
            if norm_accum:
                units = [(0, 0, 128)] if not units else units
            for (u, base, nrows) in units:
                P.wait("pe", tsq, tnorm)
                st_flag = (ri == 0) if norm_accum else True
                sp_flag = (ri == nr_total - 1) if norm_accum else True
                for j in range(4):
                    prow = ctx.pD[0:1, :] if j < 2 else ctx.pD[32:33, :]
                    po = (j % 2) * 512
                    tpn = P.op("pe", lambda E_, base=base, nrows=nrows, j=j, prow=prow,
                               po=po, st=st_flag, sp=sp_flag: E_.matmul(
                        prow[0:1, po:po + 512], ctx.ones_col[base:base + nrows, 0:1],
                        sq[base:base + nrows, j * 512:(j + 1) * 512],
                        start=st, stop=sp), sem=None if j < 3 else "pe")
                if norm_accum and ri < nr_total - 1:
                    continue
                P.wait("dve", tpn)
                t1 = P.op("dve", lambda E_, u=u: E_.reduce_max(
                    ctx.SF2[0:1, 40:41], ctx.pD[0:1, 0:1024], axis=AX))
                P.wait("dve", t1)
                t2 = P.op("dve", lambda E_, u=u: E_.reduce_max(
                    ctx.SF2[0:1, 41:42], ctx.pD[32:33, 0:1024], axis=AX))
                P.wait("dve", t2)
                t3 = P.op("dve", lambda E_, u=u, umax=umax: E_.tensor_tensor(
                    umax[0:1, u:u + 1], ctx.SF2[0:1, 40:41], ctx.SF2[0:1, 41:42],
                    op=ALU.max))
                P.wait("dve", t3)
                tnorm = t3
        prev_ev = tev
    return last_dma, tnorm


def emit_v_proj(ctx, w_chunks, w_tok, dram_out, dh1=None, bias_row=None,
                scale=None, extra_wait=None):
    P = ctx.P
    stage = ctx.BIG[:, 4096:6144]
    last_dma = None
    prev_ev = None
    for tp in range(NT // 2):
        P.wait("pe", w_tok, extra_wait, prev_ev, last_dma)
        for half in range(2):
            tt = tp * 2 + half
            psl = ctx.pA if half == 0 else ctx.pB
            for c in range(NCH):
                for j in range(2):
                    lastmm = (c == NCH - 1 and j == 1 and bias_row is None)
                    tmm = P.op("pe", lambda E_, tt=tt, c=c, j=j, psl=psl: E_.matmul(
                        psl[:, j * 512:(j + 1) * 512],
                        ctx.XN[c][:, tt * 128:(tt + 1) * 128],
                        w_chunks[c][:, j * 512:(j + 1) * 512],
                        start=(c == 0), stop=(c == NCH - 1 and bias_row is None)),
                        sem="pe" if lastmm else None)
            if bias_row is not None:
                for j in range(2):
                    tmm = P.op("pe", lambda E_, j=j, psl=psl: E_.matmul(
                        psl[:, j * 512:(j + 1) * 512], ctx.ones_row[0:1, :],
                        bias_row[0:1, j * 512:(j + 1) * 512],
                        start=False, stop=True),
                        sem=None if j < 1 else "pe")
        P.wait("act", tmm)
        for half in range(2):
            psl = ctx.pA if half == 0 else ctx.pB
            if scale is not None:
                tev = P.op("act", lambda E_, half=half, psl=psl: E_.activation(
                    stage[:, half * 1024:(half + 1) * 1024], psl[:, :], AF.Copy, scale=scale))
            else:
                tev = P.op("act", lambda E_, half=half, psl=psl: E_.copy(
                    stage[:, half * 1024:(half + 1) * 1024], psl[:, :]))
        prev_ev = tev
        P.wait("sp", tev)
        last_dma = P.dma("sp",
                         dram_out[tp * 256:(tp + 1) * 256, :].rearrange("(h p) e -> p h e", p=128),
                         stage.rearrange("p (h e) -> p h e", h=2))
    return last_dma


def emit_attn_mha(ctx: Ctx, tag, H, dh, wq_ap, wk_ap, wv_ap, transposed,
                  w_out_ap=None, b_out_ap=None, gamma_ap=None, beta_ap=None,
                  core_id_pairs=None):
    P = ctx.P
    STR, BIG = ctx.STR, ctx.BIG
    scale = float(dh) ** -0.5
    U = 16
    G = H // 2
    grows = 2 * dh

    Qd = P.dram(f"{tag}_Qd", [1024, TPC], BF16).ap()
    Kd = P.dram(f"{tag}_Kd", [1024, TPC], BF16).ap()
    Vd = P.dram(f"{tag}_Vd", [TPC, 1024], BF16).ap()
    Kg = P.dram(f"{tag}_Kg", [2048, TPC], BF16).ap()
    Vg = P.dram(f"{tag}_Vg", [2 * TPC, 1024], BF16).ap()
    kmd = P.dram(f"{tag}_kmd", [1, U], F32).ap()
    kmg = P.dram(f"{tag}_kmg", [1, U], F32).ap()

    qmax = ctx.SF2[0:1, 0:U]
    kmax = ctx.SF2[0:1, 32:32 + U]
    norm_units = {r: [(2 * r, 0, 64), (2 * r + 1, 64, 64)] for r in range(NCH)}

    if transposed:
        wtok, wch = _load_wT(ctx, wq_ap, tag + "q")
    else:
        wtok, wch = _load_w_direct(ctx, wq_ap, tag + "q")
    tq, tnq = emit_fm_proj(ctx, wch, wtok, list(range(NCH)), scale=scale,
                           dram_out=Qd, norm_units=norm_units, umax=qmax)
    P.barrier()
    if transposed:
        wtok, wch = _load_wT(ctx, wk_ap, tag + "k")
    else:
        wtok, wch = _load_w_direct(ctx, wk_ap, tag + "k")
    tk, tnk = emit_fm_proj(ctx, wch, wtok, list(range(NCH)),
                           dram_out=Kd, norm_units=norm_units, umax=kmax)
    P.barrier()
    if transposed:
        wtok, wch = _load_wT(ctx, wv_ap, tag + "v")
    else:
        wtok, wch = _load_w_direct(ctx, wv_ap, tag + "v")
    tv = emit_v_proj(ctx, wch, wtok, Vd)
    P.wait("sp", tnk)
    tkm = P.dma("sp", kmd[0:1, :], kmax)
    P.barrier()

    c1 = P.cc("AllGather", ALU.bypass, [Kd.tensor.ap().opt()], [Kg.tensor.ap().opt()], ctx.pairs)
    c2 = P.cc("AllGather", ALU.bypass, [Vd.tensor.ap().opt()], [Vg.tensor.ap().opt()], ctx.pairs)
    c3 = P.cc("AllReduce", ALU.max, [kmd.tensor.ap().opt()], [kmg.tensor.ap().opt()], ctx.pairs)
    P.wait("sp", c3)
    tkg = P.dma("sp", kmax, kmg[0:1, :])

    if MHA_STOP == "proj":
        P.barrier()
        return (Qd, Kd, Vd, Kg, Vg, kmg)
    P.wait("dve", tkg, tnq)
    t = P.op("dve", lambda E_: E_.tensor_tensor(ctx.SF2[0:1, 64:64 + U], qmax, kmax, op=ALU.mult))
    P.wait("act", t)
    t = P.op("act", lambda E_: E_.activation(ctx.SF2[0:1, 64:64 + U], ctx.SF2[0:1, 64:64 + U], AF.Sqrt))
    P.wait("dve", t)
    t = P.op("dve", lambda E_: E_.tensor_scalar(ctx.ROWB[0:1, 0:U], ctx.SF2[0:1, 64:64 + U],
                                                -1.0, None, ALU.mult))
    P.wait("dve", t)
    brow_tok = P.settle("dve")

    KB = STR[:, 0:4096]
    QB = STR[:, 4096:6144]
    W_ = 2 * dh + 2
    VPW = 32 * W_
    VP = STR[:, 6144:6144 + VPW]
    VPv = VP.rearrange("p (ki w) -> p ki w", w=W_)
    EB = STR[:, 10400:14496]

    if w_out_ap is not None:
        O_all = [BIG[:, c * 2048:(c + 1) * 2048] for c in range(NCH)]
        OS = STR[:, 14496:15520]

    ep_toks = []
    prev_core = {}
    exp_war = {}
    for g in range(G):
        if dh == 64:
            ua, ub = 2 * g, 2 * g + 1
        else:
            ua = ub = g
        P.wait("sp", c1, c2)
        pk = prev_core.get("kb")
        P.wait("sp", pk)
        d1 = P.dma("sp", KB[0:grows, 0:2048], Kg[grows * g:grows * (g + 1), :])
        d2 = P.dma("sp", KB[0:grows, 2048:4096], Kg[1024 + grows * g:1024 + grows * (g + 1), :])
        d3 = P.dma("sp", QB[0:grows, :], Qd[grows * g:grows * (g + 1), :])
        d4 = P.dma("sp", VPv[:, :, 0:dh],
                   Vg[:, grows * g:grows * g + dh].rearrange("(ki p) e -> p ki e", p=128))
        d5 = P.dma("sp", VPv[:, :, dh + 1:2 * dh + 1],
                   Vg[:, grows * g + dh:grows * (g + 1)].rearrange("(ki p) e -> p ki e", p=128))
        P.wait("dve", d5)
        tm1 = P.op("dve", lambda E_: E_.memset(VPv[:, :, dh:dh + 1], 1.0))
        P.wait("dve", tm1)
        tm2 = P.op("dve", lambda E_: E_.memset(VPv[:, :, 2 * dh + 1:2 * dh + 2], 1.0))
        P.wait("pe", brow_tok, d5, tm2, exp_war.get("a"), exp_war.get("b"), prev_core.get("kb"))
        tb1 = P.op("pe", lambda E_, ua=ua: E_.matmul(
            ctx.pA[:, 0:1], ctx.ones_row[0:1, :], ctx.ROWB[0:1, ua:ua + 1],
            start=True, stop=True))
        tb2 = P.op("pe", lambda E_, ub=ub: E_.matmul(
            ctx.pA[:, 1:2], ctx.ones_row[0:1, :], ctx.ROWB[0:1, ub:ub + 1],
            start=True, stop=True))
        P.wait("dve", tb2, exp_war.get("a"), exp_war.get("b"))
        tb3 = P.op("dve", lambda E_: E_.tensor_copy(ctx.bias128[:, 0:2], ctx.pA[:, 0:2]))
        P.wait("dve", tb3)
        tb4 = P.settle("dve")

        exp_tok = {"a": None, "b": None}
        pv_tok = {}
        st_tok = {}
        ep = None
        for qc in range(2):
            qsl = slice(qc * 1024, (qc + 1) * 1024)
            P.wait("pe", ep)
            for ki in range(32):
                for hi, hname in ((0, "a"), (1, "b")):
                    rows = slice(dh * hi, dh * (hi + 1))
                    ps = ctx.pA if hi == 0 else ctx.pB
                    pvp = ctx.pC if hi == 0 else ctx.pD
                    voff = (dh + 1) * hi
                    P.wait("pe", exp_tok[hname], tb4)
                    for j in range(2):
                        tst = P.op("pe", lambda E_, ki=ki, j=j, rows=rows, ps=ps, qsl=qsl: E_.matmul(
                            ps[:, j * 512:(j + 1) * 512],
                            KB[rows, ki * 128:(ki + 1) * 128],
                            QB[rows, qsl][:, j * 512:(j + 1) * 512],
                            start=True, stop=True),
                            sem=None if j < 1 else "pe")
                    st_tok[hname] = tst
                    eoff = 1024 * (2 * (ki % 2) + hi)
                    P.wait("act", tst, pv_tok.get((hname, ki % 2)), tb4)
                    tex = P.op("act", lambda E_, ps=ps, eoff=eoff, hi=hi: E_.activation(
                        EB[:, eoff:eoff + 1024], ps[:, :], AF.Exp,
                        bias=ctx.bias128[:, hi:hi + 1]))
                    exp_tok[hname] = tex
                    exp_war[hname] = tex
                    P.wait("pe", tex)
                    for j in range(2):
                        tpv = P.op("pe", lambda E_, ki=ki, j=j, pvp=pvp, voff=voff, eoff=eoff: E_.matmul(
                            pvp[0:dh + 1, j * 512:(j + 1) * 512],
                            VPv[:, ki, voff:voff + dh + 1],
                            EB[:, eoff + j * 512:eoff + (j + 1) * 512],
                            start=(ki == 0), stop=(ki == 31)),
                            sem=None if j < 1 else "pe")
                    pv_tok[(hname, ki % 2)] = tpv
            for hi, hname in ((0, "a"), (1, "b")):
                pvp = ctx.pC if hi == 0 else ctx.pD
                ps = ctx.pA if hi == 0 else ctx.pB
                P.wait("dve", pv_tok[(hname, 0)], pv_tok[(hname, 1)])
                td = P.op("dve", lambda E_, pvp=pvp: E_.reciprocal(
                    ctx.ROWS[0:1, 0:1024], pvp[dh:dh + 1, 0:1024]))
                P.wait("dve", td)
                td = P.op("dve", lambda E_: E_.tensor_copy(ctx.ROWB[0:1, 1024:2048],
                                                           ctx.ROWS[0:1, 0:1024]))
                P.wait("dve", td)
                td = P.settle("dve")
                P.wait("pe", td, exp_tok[hname])
                for j in range(2):
                    tbc = P.op("pe", lambda E_, j=j, ps=ps: E_.matmul(
                        ps[:, j * 512:(j + 1) * 512], ctx.ones_row[0:1, :],
                        ctx.ROWB[0:1, 1024 + j * 512:1024 + (j + 1) * 512],
                        start=True, stop=True),
                        sem=None if j < 1 else "pe")
                P.wait("act", tbc)
                tnb = P.op("act", lambda E_, ps=ps: E_.copy(ctx.SF2[:, 0:1024], ps[0:128, :]))
                P.wait("dve", tnb)
                if w_out_ap is None:
                    t1 = P.op("dve", lambda E_, pvp=pvp, hi=hi: E_.tensor_tensor(
                        ctx.SF[dh * hi:dh * (hi + 1), 0:1024], pvp[0:dh, :],
                        ctx.SF2[0:dh, 0:1024], op=ALU.mult))
                    P.wait("dve", t1)
                    t2 = P.op("dve", lambda E_, g=g, hi=hi, qsl=qsl: E_.tensor_tensor(
                        ctx.X[g][dh * hi:dh * (hi + 1), qsl],
                        ctx.X[g][dh * hi:dh * (hi + 1), qsl],
                        ctx.SF[dh * hi:dh * (hi + 1), 0:1024], op=ALU.add))
                else:
                    t2 = P.op("dve", lambda E_, pvp=pvp, hi=hi: E_.tensor_tensor(
                        OS[32 * hi:32 * (hi + 1), 0:1024], pvp[0:dh, :],
                        ctx.SF2[0:dh, 0:1024], op=ALU.mult))
                P.wait("dve", t2)
                ep = t2
            if w_out_ap is not None:
                cch, rbase = g // 2, 64 * (g % 2)
                t3 = P.op("dve", lambda E_, cch=cch, rbase=rbase, qsl=qsl: E_.tensor_copy(
                    O_all[cch][rbase:rbase + 64, qsl], OS[0:64, 0:1024]))
                P.wait("dve", t3)
                ep = t3
            prev_core["kb"] = ep
            ep_toks.append(ep)

    P.barrier()
    if MHA_STOP == "core" or w_out_ap is None:
        return ep_toks[-1]

    wtok, wch = _load_w_direct(ctx, w_out_ap, tag + "o")
    bo = ctx.BT[:, 0:8]
    tbo = P.dma("pool", bo, b_out_ap.rearrange("(c p) -> p c", p=128), allow_slow_non_contiguous=True)
    P.wait("pe", tbo)
    _, _ = emit_fm_proj(ctx, wch, wtok, list(range(NCH)), bias_col=bo,
                        src_chunks=O_all, dst_chunks=ctx.XN)
    P.barrier()
    if MHA_STOP == "oproj":
        return ep_toks[-1]
    gm = ctx.BT[:, 8:16]
    bt = ctx.BT[:, 16:24]
    tg1 = P.dma("pool", gm, gamma_ap.rearrange("(c p) -> p c", p=128), allow_slow_non_contiguous=True)
    tg2 = P.dma("pool", bt, beta_ap.rearrange("(c p) -> p c", p=128), allow_slow_non_contiguous=True)
    P.wait("dve", tg1, tg2)
    P.wait("act", tg2)
    P.wait("pe", tg2)
    if MHA_STOP == "noaffine":
        tln = emit_ln(ctx, eps=1e-6, src=ctx.XN, dst=ctx.XN)
    else:
        tln = emit_ln(ctx, eps=1e-6, src=ctx.XN, dst=ctx.XN, gamma=gm, beta=bt)
    P.barrier()
    if MHA_STOP == "ln":
        return tln
    for c in range(NCH):
        tx = P.op("dve", lambda E_, c=c: E_.tensor_tensor(
            ctx.X[c][:, :], ctx.X[c][:, :], ctx.XN[c][:, :], op=ALU.add))
        P.wait("dve", tx)
    return tx


def emit_attn_causal(ctx: Ctx, tag, wq_ap, bq_ap, wk_ap, bk_ap, wv_ap, bv_ap):
    P = ctx.P
    STR, BIG = ctx.STR, ctx.BIG
    d = 1024
    scale = 1.0 / 32.0

    Qd = P.dram(f"{tag}_Qd", [1024, TPC], BF16).ap()
    Kd = P.dram(f"{tag}_Kd", [1024, TPC], BF16).ap()
    Vd = P.dram(f"{tag}_Vd", [TPC, 1024], BF16).ap()
    Kg = P.dram(f"{tag}_Kg", [2048, TPC], BF16).ap()
    Vg = P.dram(f"{tag}_Vg", [2 * TPC, 1024], BF16).ap()
    kmd = P.dram(f"{tag}_kmd", [1, 16], F32).ap()
    kmg = P.dram(f"{tag}_kmg", [1, 16], F32).ap()

    qmax = ctx.SF2[0:1, 0:16]
    kmax = ctx.SF2[0:1, 32:48]

    bqt = ctx.BT[:, 0:8]
    bkt = ctx.BT[:, 8:16]
    tb1 = P.dma("pool", bqt, bq_ap.rearrange("(c p) -> p c", p=128), allow_slow_non_contiguous=True)
    tb2 = P.dma("pool", bkt, bk_ap.rearrange("(c p) -> p c", p=128), allow_slow_non_contiguous=True)
    bvr = ctx.ROWB[0:1, 1024:2048]
    tb3 = P.dma("pool", bvr, bv_ap.rearrange("e -> 1 e" if False else "(o e) -> o e", o=1))
    P.wait("dve", tb1)
    tb4 = P.op("dve", lambda E_: E_.tensor_scalar(ctx.BT[:, 16:24], bqt, scale, None, ALU.mult))
    P.wait("dve", tb4)
    tb5 = P.settle("dve")
    bq32 = ctx.BT[:, 16:24]

    wtok, wch = _load_wT(ctx, wq_ap, tag + "q")
    P.wait("act", tb5)
    _, tnq = emit_fm_proj(ctx, wch, wtok, list(range(NCH)), scale=scale, bias_col=bq32,
                          dram_out=Qd, norm_accum=True, umax=qmax)
    P.barrier()
    wtok, wch = _load_wT(ctx, wk_ap, tag + "k")
    P.wait("act", tb2)
    _, tnk = emit_fm_proj(ctx, wch, wtok, list(range(NCH)), bias_col=bkt,
                          dram_out=Kd, norm_accum=True, umax=kmax)
    P.barrier()
    wtok, wch = _load_wT(ctx, wv_ap, tag + "v")
    P.wait("pool", tb3)
    tv = emit_v_proj(ctx, wch, wtok, Vd, bias_row=bvr)
    P.wait("sp", tnk)
    tkm = P.dma("sp", kmd[0:1, :], kmax)
    P.barrier()

    c1 = P.cc("AllGather", ALU.bypass, [Kd.tensor.ap().opt()], [Kg.tensor.ap().opt()], ctx.pairs)
    c2 = P.cc("AllGather", ALU.bypass, [Vd.tensor.ap().opt()], [Vg.tensor.ap().opt()], ctx.pairs)
    c3 = P.cc("AllReduce", ALU.max, [kmd.tensor.ap().opt()], [kmg.tensor.ap().opt()], ctx.pairs)
    P.wait("sp", c3)
    tkg = P.dma("sp", kmax, kmg[0:1, :])

    P.wait("dve", tkg, tnq)
    t = P.op("dve", lambda E_: E_.tensor_tensor(ctx.SF2[0:1, 64:65], qmax[0:1, 0:1], kmax[0:1, 0:1], op=ALU.mult))
    P.wait("act", t)
    t = P.op("act", lambda E_: E_.activation(ctx.SF2[0:1, 64:65], ctx.SF2[0:1, 64:65], AF.Sqrt))
    P.wait("dve", t)
    t = P.op("dve", lambda E_: E_.tensor_scalar(ctx.ROWB[0:1, 0:1], ctx.SF2[0:1, 64:65], -1.0, None, ALU.mult))
    P.wait("dve", t)
    t = P.settle("dve")
    P.wait("pe", t)
    tbb = P.op("pe", lambda E_: E_.matmul(ctx.pA[:, 0:1], ctx.ones_row[0:1, :],
                                          ctx.ROWB[0:1, 0:1], start=True, stop=True))
    P.wait("dve", tbb)
    tbc = P.op("dve", lambda E_: E_.tensor_copy(ctx.bias128[:, 0:1], ctx.pA[:, 0:1]))
    P.wait("dve", tbc)
    bias_tok = P.settle("dve")

    KT = [STR[:, 0:1024], STR[:, 1024:2048]]
    VT = [STR[:, 2048:3072], STR[:, 3072:4096]]
    QB = STR[:, 4096:8192]
    MB = STR[:, 8192:8704]
    EX = STR[:, 8704:9216]
    EE = [BIG[:, i * 512:(i + 1) * 512] for i in range(32)]

    kdma = {}
    kuse = {}
    prev_ph2 = None
    prev_recip = None
    for qc in range(4):
        qsl = slice(qc * 512, (qc + 1) * 512)
        P.wait("sp", prev_ph2)
        dq = P.dma("sp", QB.rearrange("p (c k) -> p c k", c=8),
                   Qd[:, qsl].rearrange("(c p) k -> p c k", p=128))
        tden = None
        texp_prev = None
        tmask_prev = None
        for ki in range(32):
            sl_ = ki % 2
            ps = ctx.pA if sl_ == 0 else ctx.pB
            rank, kloc = ki // 16, ki % 16
            P.wait("sp", kuse.get(("k", sl_)), c1)
            dk = P.dma("sp", KT[sl_].rearrange("p (c k) -> p c k", c=8),
                       Kg[rank * 1024:(rank + 1) * 1024, kloc * 128:(kloc + 1) * 128]
                       .rearrange("(c p) k -> p c k", p=128))
            kdma[("k", sl_)] = dk
            P.wait("pe", dk, dq, texp_prev, bias_tok)
            for c in range(NCH):
                tst = P.op("pe", lambda E_, c=c, ps=ps, sl_=sl_: E_.matmul(
                    ps[:, 0:512], KT[sl_][:, c * 128:(c + 1) * 128],
                    QB[:, c * 512:(c + 1) * 512],
                    start=(c == 0), stop=(c == NCH - 1)),
                    sem=None if c < NCH - 1 else "pe")
            kuse[("k", sl_)] = tst
            P.wait("act", tst, tmask_prev)
            tex = P.op("act", lambda E_, ps=ps: E_.activation(
                EX[:, :], ps[:, :512], AF.Exp, bias=ctx.bias128[:, 0:1]))
            texp_prev = tex
            P.wait("dve", tex, prev_ph2)
            tmk = P.op("dve", lambda E_, ki=ki, qsl=qsl: E_.tensor_scalar(
                MB[:, :], ctx.qg[:, qsl], ctx.kg[:, ki:ki + 1], None, ALU.is_ge))
            P.wait("dve", tmk)
            tmm = P.op("dve", lambda E_, ki=ki: E_.tensor_tensor(
                EE[ki][:, :], EX[:, :], MB[:, :], op=ALU.mult))
            tmask_prev = tmm
            P.wait("pe", tmm, prev_recip)
            tdn = P.op("pe", lambda E_, ki=ki: E_.matmul(
                ctx.pC[0:1, 0:512], ctx.ones_col[:, 0:1], EE[ki][:, :],
                start=(ki == 0), stop=(ki == 31)))
            tden = tdn
        P.wait("dve", tden)
        td = P.op("dve", lambda E_: E_.reciprocal(ctx.ROWS[0:1, 0:512], ctx.pC[0:1, 0:512]))
        P.wait("dve", td)
        td = P.op("dve", lambda E_: E_.tensor_copy(ctx.ROWB[0:1, 512:1024], ctx.ROWS[0:1, 0:512]))
        P.wait("dve", td)
        td2 = P.settle("dve")
        prev_recip = td2
        P.wait("pe", td2)
        tbc2 = P.op("pe", lambda E_: E_.matmul(ctx.pD[:, 0:512], ctx.ones_row[0:1, :],
                                               ctx.ROWB[0:1, 512:1024], start=True, stop=True))
        P.wait("act", tbc2)
        tnb = P.op("act", lambda E_: E_.copy(ctx.SF[:, 0:512], ctx.pD[:, 0:512]))
        pslots = [ctx.pA[:, 0:512], ctx.pA[:, 512:1024], ctx.pB[:, 0:512], ctx.pB[:, 512:1024],
                  ctx.pC[:, 0:512], ctx.pC[:, 512:1024], ctx.pD[:, 0:512], ctx.pD[:, 512:1024]]
        tpv = {}
        for ki in range(32):
            sl_ = ki % 2
            P.wait("sp", kuse.get(("v", sl_)), c2)
            dv = P.dma("sp", VT[sl_].rearrange("p (c k) -> p c k", c=8) if False else VT[sl_],
                       Vg[ki * 128:(ki + 1) * 128, :])
            P.wait("pe", dv)
            if ki < 2:
                P.wait("pe", tnb, texp_prev)
            for mc in range(NCH):
                tp = P.op("pe", lambda E_, ki=ki, mc=mc, sl_=sl_, pslots=pslots: E_.matmul(
                    pslots[mc], VT[sl_][:, mc * 128:(mc + 1) * 128], EE[ki][:, :],
                    start=(ki == 0), stop=(ki == 31)),
                    sem=None if mc < NCH - 1 else "pe")
            kuse[("v", sl_)] = tp
            tpv[ki] = tp
        P.wait("dve", tpv[31], tpv[30])
        tlast = None
        for mc in range(NCH):
            P.wait("dve", tlast)
            t1 = P.op("dve", lambda E_, mc=mc, pslots=pslots: E_.tensor_tensor(
                ctx.SF[:, 512:1024], pslots[mc], ctx.SF[:, 0:512], op=ALU.mult))
            P.wait("dve", t1)
            t2 = P.op("dve", lambda E_, mc=mc, qsl=qsl: E_.tensor_tensor(
                ctx.X[mc][:, qsl], ctx.X[mc][:, qsl], ctx.SF[:, 512:1024], op=ALU.add))
            tlast = t2
        prev_ph2 = tlast
    P.barrier()
    return tlast


def emit_performer(ctx: Ctx, tag, wqkv_ap, bqkv_ap, wout_ap, bout_ap, omega_ap):
    P = ctx.P
    STR, BIG = ctx.STR, ctx.BIG
    H, dh, M = 16, 64, 256

    qkvd = P.dram(f"{tag}_qkv", [3072, TPC], BF16).ap()
    kvd = P.dram(f"{tag}_kv", [H * 256, 65], F32).ap()
    kvs = P.dram(f"{tag}_kvs", [H * 256, 65], F32).ap()

    bqkvt = ctx.BT[:, 0:24]
    tb = P.dma("pool", bqkvt, bqkv_ap.rearrange("(c p) -> p c", p=128),
               allow_slow_non_contiguous=True)
    om = STR[0:65, 16384:16640]
    tom = P.dma("pool", om[0:64, :], omega_ap[:, :])
    P.wait("dve", tom)
    tom2 = P.op("dve", lambda E_: E_.memset(om[64:65, :], 1.0))
    P.wait("dve", tom2)
    tom3 = P.settle("dve")

    for grp in range(3):
        wtok, wch = _load_w_direct(ctx, wqkv_ap[:, grp * 1024:(grp + 1) * 1024], f"{tag}w{grp}")
        P.wait("pe", tb)
        tq, _ = emit_fm_proj(ctx, wch, wtok, list(range(NCH)),
                             bias_col=bqkvt[:, grp * 8:(grp + 1) * 8],
                             dram_out=qkvd, dram_row0=grp * 1024)
        P.barrier()

    if MHA_STOP == "pqkv":
        P.barrier()
        return (qkvd, kvd, kvs)
    kh = STR[0:64, 0:2048]
    k2 = STR[0:64, 2048:4096]
    kp = STR[:, 4096:8192]
    vh = STR[0:64, 8192:10240]
    vt = STR[:, 10240:11280]
    vtv = STR[:, 10240:11280].rearrange("p (t w) -> p t w", w=65) if False else None
    prev_kv = None
    for h in range(H):
        dk = P.dma("sp", kh[:, :], qkvd[1024 + h * 64:1024 + (h + 1) * 64, :])
        dv = P.dma("sp", vh[:, :], qkvd[2048 + h * 64:2048 + (h + 1) * 64, :])
        P.wait("pe", prev_kv, dv)
        P.wait("act", dk, prev_kv)
        tk2 = P.op("act", lambda E_: E_.activation(k2[:, :], kh[:, :], AF.Square))
        P.wait("pe", tk2, tom3)
        for tt in range(NT):
            tn = P.op("pe", lambda E_, tt=tt: E_.matmul(
                ctx.pD[0:128, tt:tt + 1] if False else ctx.pC[:, tt:tt + 1],
                k2[:, tt * 128:(tt + 1) * 128], ctx.ones_col[0:64, 0:1],
                start=True, stop=True),
                sem=None if tt < NT - 1 else "pe")
        P.wait("dve", tn)
        tsc = P.op("dve", lambda E_: E_.tensor_scalar(ctx.SF2[:, 0:16], ctx.pC[:, 0:16],
                                                      -0.5, None, ALU.mult))
        P.wait("dve", tsc)
        tsc2 = P.settle("dve")
        prev_exp = None
        for tt in range(NT):
            ps = ctx.pA if tt % 2 == 0 else ctx.pB
            P.wait("pe", prev_exp, tsc2)
            tmm = P.op("pe", lambda E_, tt=tt, ps=ps: E_.matmul(
                ps[:, 0:256], kh[:, tt * 128:(tt + 1) * 128], om[0:64, :],
                start=True, stop=True))
            P.wait("act", tmm)
            tex = P.op("act", lambda E_, tt=tt, ps=ps: E_.activation(
                kp[:, tt * 256:(tt + 1) * 256], ps[:, 0:256], AF.Exp,
                bias=ctx.SF2[:, tt:tt + 1]))
            prev_exp = tex
        prev_vt = None
        for tt in range(NT):
            P.wait("pe", prev_vt)
            tvt = P.op("pe", lambda E_, tt=tt: E_.matmul(
                ctx.pC[:, 256:384], vh[:, tt * 128:(tt + 1) * 128], ctx.ident[0:64, :],
                start=True, stop=True))
            P.wait("dve", tvt)
            tcv = P.op("dve", lambda E_: E_.tensor_copy(vt[:, 0:64], ctx.pC[:, 256:320]))
            P.wait("dve", tcv)
            tcv2 = P.op("dve", lambda E_: E_.memset(vt[:, 64:65], 1.0))
            P.wait("dve", tcv2)
            tcv3 = P.settle("dve")
            P.wait("pe", tcv3, prev_exp)
            for mc in range(2):
                tkv = P.op("pe", lambda E_, tt=tt, mc=mc: E_.matmul(
                    ctx.pD[:, mc * 512:mc * 512 + 65],
                    kp[:, tt * 256 + mc * 128:tt * 256 + (mc + 1) * 128],
                    vt[:, 0:65], start=(tt == 0), stop=(tt == NT - 1)),
                    sem=None if (mc < 1 or tt < NT - 1) else "pe")
            prev_vt = tkv if tt == NT - 1 else tvt
        P.wait("act", tkv)
        te1 = P.op("act", lambda E_: E_.copy(ctx.SF[:, 0:65], ctx.pD[:, 0:65]))
        P.wait("act", te1)
        te2 = P.op("act", lambda E_: E_.copy(ctx.SF[:, 512:577], ctx.pD[:, 512:577]))
        P.wait("sp", te2)
        s1 = P.dma("sp", kvd[h * 256:h * 256 + 128, :], ctx.SF[:, 0:65])
        s2 = P.dma("sp", kvd[h * 256 + 128:h * 256 + 256, :], ctx.SF[:, 512:577])
        prev_kv = s2
    P.barrier()

    if MHA_STOP == "pkv":
        P.barrier()
        c1 = P.cc("AllReduce", ALU.add, [kvd.tensor.ap().opt()], [kvs.tensor.ap().opt()], ctx.pairs)
        P.wait("pool", c1)
        return (qkvd, kvd, kvs)
    c1 = P.cc("AllReduce", ALU.add, [kvd.tensor.ap().opt()], [kvs.tensor.ap().opt()], ctx.pairs)
    P.wait("pool", c1)

    qa = STR[0:65, 0:2048]
    q2 = STR[0:64, 2048:4096]
    qp = STR[:, 4096:8192]
    kvb = STR[:, 16640:16770]
    O_all = [BIG[:, c * 2048:(c + 1) * 2048] for c in range(NCH)]
    prev_num = None
    prev_nb = None
    for h in range(H):
        P.wait("sp", prev_num)
        dq = P.dma("sp", qa[0:64, :], qkvd[h * 64:(h + 1) * 64, :])
        P.wait("pool", prev_num)
        dkv = P.dma("pool", kvb.rearrange("p (c w) -> p c w", c=2),
                    kvs[h * 256:(h + 1) * 256, :].rearrange("(c p) w -> p c w", p=128))
        P.wait("act", dq, prev_num)
        tq2 = P.op("act", lambda E_: E_.activation(q2[:, :], qa[0:64, :], AF.Square))
        P.wait("pe", tq2, prev_nb)
        for j in range(4):
            prow = ctx.pC[0:1, :] if j < 2 else ctx.pC[32:33, :]
            po = (j % 2) * 512
            tqn = P.op("pe", lambda E_, j=j, prow=prow, po=po: E_.matmul(
                prow[0:1, po:po + 512], ctx.ones_col[0:64, 0:1],
                q2[:, j * 512:(j + 1) * 512], start=True, stop=True),
                sem=None if j < 3 else "pe")
        P.wait("dve", tqn)
        tr1 = P.op("dve", lambda E_: E_.tensor_scalar(ctx.ROWS[0:1, 0:1024], ctx.pC[0:1, 0:1024],
                                                      -0.5, None, ALU.mult))
        P.wait("dve", tr1)
        tr2 = P.op("dve", lambda E_: E_.tensor_copy(qa[64:65, 0:1024], ctx.ROWS[0:1, 0:1024]))
        P.wait("dve", tr2)
        tr3 = P.op("dve", lambda E_: E_.tensor_scalar(ctx.ROWS[0:1, 0:1024], ctx.pC[32:33, 0:1024],
                                                      -0.5, None, ALU.mult))
        P.wait("dve", tr3)
        tr4 = P.op("dve", lambda E_: E_.tensor_copy(qa[64:65, 1024:2048], ctx.ROWS[0:1, 0:1024]))
        P.wait("dve", tr4)
        tr5 = P.settle("dve")
        prev_exq = None
        for mc in range(2):
            P.wait("pe", tr5, prev_exq)
            for j in range(4):
                psl = ctx.pA if j < 2 else ctx.pB
                po = (j % 2) * 512
                tmm = P.op("pe", lambda E_, mc=mc, j=j, psl=psl, po=po: E_.matmul(
                    psl[:, po:po + 512],
                    om[0:65, mc * 128:(mc + 1) * 128],
                    qa[0:65, j * 512:(j + 1) * 512],
                    start=True, stop=True),
                    sem=None if j < 3 else "pe")
            P.wait("act", tmm)
            tex = P.op("act", lambda E_, mc=mc: E_.activation(
                qp[:, mc * 2048:mc * 2048 + 1024], ctx.pA[:, :], AF.Exp))
            P.wait("act", tex)
            tex = P.op("act", lambda E_, mc=mc: E_.activation(
                qp[:, mc * 2048 + 1024:(mc + 1) * 2048], ctx.pB[:, :], AF.Exp))
            prev_exq = tex
        P.wait("pe", dkv, prev_exq)
        for mc in range(2):
            for j in range(4):
                tnm = P.op("pe", lambda E_, mc=mc, j=j: E_.matmul(
                    (ctx.pC if j < 2 else ctx.pD)[0:65, (j % 2) * 512:(j % 2) * 512 + 512],
                    kvb[:, mc * 65:(mc + 1) * 65],
                    qp[:, mc * 2048 + j * 512:mc * 2048 + (j + 1) * 512],
                    start=(mc == 0), stop=(mc == 1)),
                    sem=None if (mc < 1 or j < 3) else "pe")
        prev_num = tnm
        P.wait("dve", tnm)
        td = P.op("dve", lambda E_: E_.tensor_scalar(ctx.ROWS[0:1, 0:1024], ctx.pC[64:65, 0:1024],
                                                     1e-6, None, ALU.add))
        P.wait("dve", td)
        td = P.op("dve", lambda E_: E_.reciprocal(ctx.ROWS[0:1, 0:1024], ctx.ROWS[0:1, 0:1024]))
        P.wait("dve", td)
        td = P.op("dve", lambda E_: E_.tensor_copy(ctx.ROWB[0:1, 0:1024], ctx.ROWS[0:1, 0:1024]))
        P.wait("dve", td)
        td2 = P.op("dve", lambda E_: E_.tensor_scalar(ctx.ROWS[0:1, 0:1024], ctx.pD[64:65, 0:1024],
                                                      1e-6, None, ALU.add))
        P.wait("dve", td2)
        td2 = P.op("dve", lambda E_: E_.reciprocal(ctx.ROWS[0:1, 0:1024], ctx.ROWS[0:1, 0:1024]))
        P.wait("dve", td2)
        td2 = P.op("dve", lambda E_: E_.tensor_copy(ctx.ROWB[0:1, 1024:2048], ctx.ROWS[0:1, 0:1024]))
        P.wait("dve", td2)
        td3 = P.settle("dve")
        P.wait("pe", td3)
        for j in range(4):
            tbc = P.op("pe", lambda E_, j=j: E_.matmul(
                (ctx.pA if j < 2 else ctx.pB)[0:64, (j % 2) * 512:(j % 2) * 512 + 512],
                ctx.ones_row[0:1, 0:64],
                ctx.ROWB[0:1, j * 512:(j + 1) * 512], start=True, stop=True),
                sem=None if j < 3 else "pe")
        P.wait("act", tbc)
        tnb1 = P.op("act", lambda E_: E_.copy(ctx.SF2[0:64, 0:1024], ctx.pA[0:64, :]))
        P.wait("act", tnb1)
        tnb2 = P.op("act", lambda E_: E_.copy(ctx.SF[0:64, 0:1024], ctx.pB[0:64, :]))
        prev_nb = tnb2
        cch, rb = h // 2, 64 * (h % 2)
        P.wait("dve", tnb1, tnb2)
        to1 = P.op("dve", lambda E_, cch=cch, rb=rb: E_.tensor_tensor(
            O_all[cch][rb:rb + 64, 0:1024], ctx.pC[0:64, :], ctx.SF2[0:64, 0:1024], op=ALU.mult))
        P.wait("dve", to1)
        to2 = P.op("dve", lambda E_, cch=cch, rb=rb: E_.tensor_tensor(
            O_all[cch][rb:rb + 64, 1024:2048], ctx.pD[0:64, :], ctx.SF[0:64, 0:1024], op=ALU.mult))
        P.wait("dve", to2)
        prev_num = to2
    P.barrier()

    wtok, wch = _load_w_direct(ctx, wout_ap, tag + "o")
    bo = ctx.BT[:, 0:8]
    tbo = P.dma("pool", bo, bout_ap.rearrange("(c p) -> p c", p=128),
                allow_slow_non_contiguous=True)
    prev_ev = None
    for r in range(NCH):
        P.wait("pe", wtok, tbo, prev_ev)
        for c in range(NCH):
            for j in range(4):
                psl = ctx.pA if j < 2 else ctx.pB
                po = (j % 2) * 512
                tmm = P.op("pe", lambda E_, r=r, c=c, j=j, psl=psl, po=po: E_.matmul(
                    psl[:, po:po + 512], wch[c][:, r * 128:(r + 1) * 128],
                    O_all[c][:, j * 512:(j + 1) * 512],
                    start=(c == 0), stop=(c == NCH - 1)),
                    sem=None if (c < NCH - 1) else "pe")
        P.wait("dve", tmm)
        for j in range(4):
            psl = ctx.pA if j < 2 else ctx.pB
            po = (j % 2) * 512
            tev = P.op("dve", lambda E_, r=r, j=j, psl=psl, po=po: E_.scalar_tensor_tensor(
                ctx.X[r][:, j * 512:(j + 1) * 512], psl[:, po:po + 512], bo[:, r:r + 1],
                ctx.X[r][:, j * 512:(j + 1) * 512], ALU.add, ALU.add))
            P.wait("dve", tev)
        prev_ev = tev
    P.barrier()
    return prev_ev


SUBLAYERS = list(range(8))
GROUPS = [SUBLAYERS]


def build_group(subs, tag):
    P = Prog()
    F = lambda n, shp: P.dram(n, shp, F32, kind="ExternalInput").ap()
    x_in = F("x_in", [1024, TPC])
    ident_in = P.dram("identity_in", [128, 128], BF16, kind="ExternalInput").ap()
    qg_in = F("qgb_in", [128, TPC])
    kg_in = F("kg_in", [128, 32])
    y_out = P.dram("y_out", [1024, TPC], F32, kind="ExternalOutput").ap()

    w = {}
    w["l0aa_wqkv"] = F("L0_attn_a_Wqkv", [1024, 3072])
    w["l0aa_bqkv"] = F("L0_attn_a_bqkv", [3072])
    w["l0aa_wout"] = F("L0_attn_a_Wout", [1024, 1024])
    w["l0aa_bout"] = F("L0_attn_a_bout", [1024])
    w["l0aa_om"] = F("L0_attn_a_omega", [64, 256])
    w["l0ab_wq"] = F("L0_attn_b_Wq", [1024, 1024])
    w["l0ab_wk"] = F("L0_attn_b_Wk", [1024, 1024])
    w["l0ab_wv"] = F("L0_attn_b_Wv", [1024, 1024])
    w["l0f_g"] = F("L0_ffn_gate_w", [16, 1024])
    w["l0f_w1"] = F("L0_ffn_w1", [16 * 1024, 512])
    w["l0f_w3"] = F("L0_ffn_w3", [16 * 1024, 512])
    w["l0f_w2"] = F("L0_ffn_w2", [16 * 512, 1024])
    w["l1aa_wqkv"] = F("L1_attn_a_w_qkv", [1024, 3072])
    w["l1aa_wo"] = F("L1_attn_a_w_out", [1024, 1024])
    w["l1aa_bo"] = F("L1_attn_a_b_out", [1024])
    w["l1aa_g"] = F("L1_attn_a_ln_gamma", [1024])
    w["l1aa_b"] = F("L1_attn_a_ln_beta", [1024])
    w["l1ab_wq"] = F("L1_attn_b_Wq", [1024, 1024])
    w["l1ab_bq"] = F("L1_attn_b_bq", [1024])
    w["l1ab_wk"] = F("L1_attn_b_Wk", [1024, 1024])
    w["l1ab_bk"] = F("L1_attn_b_bk", [1024])
    w["l1ab_wv"] = F("L1_attn_b_Wv", [1024, 1024])
    w["l1ab_bv"] = F("L1_attn_b_bv", [1024])
    w["l1f_g"] = F("L1_ffn_gate_w", [1024, 8])
    w["l1f_wg"] = F("L1_ffn_w_gate", [8 * 1024, 2048])
    w["l1f_wu"] = F("L1_ffn_w_up", [8 * 1024, 2048])
    w["l1f_wd"] = F("L1_ffn_w_down", [8 * 2048, 1024])
    w["l1f_sg"] = F("L1_ffn_ws_gate", [1024, 2048])
    w["l1f_su"] = F("L1_ffn_ws_up", [1024, 2048])
    w["l1f_sd"] = F("L1_ffn_ws_down", [2048, 1024])

    ctx = Ctx(P)
    ctx.init_consts(x_in, ident_in, qg_in, kg_in)
    P.wait("act", ctx.init_tok)
    P.wait("pe", ctx.init_tok)
    P.wait("dve", ctx.init_tok)

    for s in subs:
        emit_ln(ctx)
        P.barrier()
        if s == 0:
            emit_performer(ctx, "p0", w["l0aa_wqkv"], w["l0aa_bqkv"],
                           w["l0aa_wout"], w["l0aa_bout"], w["l0aa_om"])
        elif s in (1, 3):
            emit_moe(ctx, w["l0f_g"], "eh", 16, 512, 4,
                     w["l0f_w1"], w["l0f_w3"], w["l0f_w2"])
        elif s == 2:
            emit_attn_mha(ctx, f"a0_{s}", 16, 64, w["l0ab_wq"], w["l0ab_wk"],
                          w["l0ab_wv"], transposed=True)
        elif s == 4:
            emit_attn_mha(ctx, "a1", 32, 32,
                          w["l1aa_wqkv"][:, 0:1024], w["l1aa_wqkv"][:, 1024:2048],
                          w["l1aa_wqkv"][:, 2048:3072], transposed=False,
                          w_out_ap=w["l1aa_wo"], b_out_ap=w["l1aa_bo"],
                          gamma_ap=w["l1aa_g"], beta_ap=w["l1aa_b"])
        elif s in (5, 7):
            emit_moe(ctx, w["l1f_g"], "he", 8, 2048, 2,
                     w["l1f_wg"], w["l1f_wu"], w["l1f_wd"],
                     shared=(w["l1f_sg"], w["l1f_su"], w["l1f_sd"]), eps20=True)
        elif s == 6:
            emit_attn_causal(ctx, "c0", w["l1ab_wq"], w["l1ab_bq"],
                             w["l1ab_wk"], w["l1ab_bk"], w["l1ab_wv"], w["l1ab_bv"])
        P.barrier()

    st = None
    for c in range(NCH):
        t = P.op("dve", lambda E_, c=c: E_.tensor_copy(ctx.SF[:, :], ctx.X[c][:, :]))
        P.wait("sp", t)
        st = P.dma("sp", y_out[c * 128:(c + 1) * 128, :], ctx.SF[:, :])
        P.wait("dve", st)
    P.wait("sp", st)
    return P.build()


_IDENT = None


def kernel(**inputs):
    global _IDENT
    import numpy as _np
    xf = inputs["L0_attn_a_x"]
    ident = _np.eye(128, dtype=ml_dtypes.bfloat16)
    kg = _np.ascontiguousarray(
        _np.arange(4096, dtype=_np.float32).reshape(32, 128).T)

    base = {
        "L0_attn_a_Wqkv": inputs["L0_attn_a_Wqkv"],
        "L0_attn_a_bqkv": inputs["L0_attn_a_bqkv"],
        "L0_attn_a_Wout": inputs["L0_attn_a_Wout"],
        "L0_attn_a_bout": inputs["L0_attn_a_bout"],
        "L0_attn_a_omega": inputs["L0_attn_a_omega"],
        "L0_attn_b_Wq": inputs["L0_attn_b_Wq"],
        "L0_attn_b_Wk": inputs["L0_attn_b_Wk"],
        "L0_attn_b_Wv": inputs["L0_attn_b_Wv"],
        "L0_ffn_gate_w": inputs["L0_ffn_gate_w"],
        "L0_ffn_w1": _np.ascontiguousarray(inputs["L0_ffn_w1"].reshape(16 * 1024, 512)),
        "L0_ffn_w3": _np.ascontiguousarray(inputs["L0_ffn_w3"].reshape(16 * 1024, 512)),
        "L0_ffn_w2": _np.ascontiguousarray(inputs["L0_ffn_w2"].reshape(16 * 512, 1024)),
        "L1_attn_a_w_qkv": inputs["L1_attn_a_w_qkv"],
        "L1_attn_a_w_out": inputs["L1_attn_a_w_out"],
        "L1_attn_a_b_out": inputs["L1_attn_a_b_out"],
        "L1_attn_a_ln_gamma": inputs["L1_attn_a_ln_gamma"],
        "L1_attn_a_ln_beta": inputs["L1_attn_a_ln_beta"],
        "L1_attn_b_Wq": inputs["L1_attn_b_Wq"],
        "L1_attn_b_bq": inputs["L1_attn_b_bq"],
        "L1_attn_b_Wk": inputs["L1_attn_b_Wk"],
        "L1_attn_b_bk": inputs["L1_attn_b_bk"],
        "L1_attn_b_Wv": inputs["L1_attn_b_Wv"],
        "L1_attn_b_bv": inputs["L1_attn_b_bv"],
        "L1_ffn_gate_w": inputs["L1_ffn_gate_w"],
        "L1_ffn_w_gate": _np.ascontiguousarray(inputs["L1_ffn_w_gate"].reshape(8 * 1024, 2048)),
        "L1_ffn_w_up": _np.ascontiguousarray(inputs["L1_ffn_w_up"].reshape(8 * 1024, 2048)),
        "L1_ffn_w_down": _np.ascontiguousarray(inputs["L1_ffn_w_down"].reshape(8 * 2048, 1024)),
        "L1_ffn_ws_gate": inputs["L1_ffn_ws_gate"],
        "L1_ffn_ws_up": inputs["L1_ffn_ws_up"],
        "L1_ffn_ws_down": inputs["L1_ffn_ws_down"],
        "identity_in": ident,
        "kg_in": kg,
    }

    xcur = [None] * CORES
    for c in range(CORES):
        b, h = c // 2, c % 2
        xcur[c] = _np.ascontiguousarray(xf[b, h * TPC:(h + 1) * TPC, :].T)

    for gi, subs in enumerate(GROUPS):
        nc = build_group(subs, f"g{gi}")
        in_maps = []
        for c in range(CORES):
            b, h = c // 2, c % 2
            im = dict(base)
            im["x_in"] = xcur[c]
            im["qgb_in"] = _np.tile(
                _np.arange(h * TPC, (h + 1) * TPC, dtype=_np.float32), (128, 1))
            in_maps.append(im)
        res = run_bass_kernel_spmd(nc, in_maps, core_ids=list(range(CORES)))
        for c in range(CORES):
            xcur[c] = _np.ascontiguousarray(res.results[c]["y_out"])

    out = _np.empty((B, N, D), _np.float32)
    for c in range(CORES):
        b, h = c // 2, c % 2
        out[b, h * TPC:(h + 1) * TPC, :] = xcur[c].T
    return out


# version 04
# speedup vs baseline: 1.4182x; 1.0119x over previous
import numpy as np
import ml_dtypes
import concourse.bass as bass
import concourse.mybir as mybir
from concourse.bass_utils import run_bass_kernel_spmd

F32 = mybir.dt.float32
BF16 = mybir.dt.bfloat16
AF = mybir.ActivationFunctionType
ALU = mybir.AluOpType
AX = mybir.AxisListType.X

B, N, D = 4, 4096, 1024
TPC = 2048
NCH = 8
NT = TPC // 128
CORES = 8

ENG_ATTRS = {"pe": "tensor", "act": "scalar", "dve": "vector", "sp": "sync", "pool": "gpsimd"}
ENG_SEM = {"pe": ("pe", 1), "act": ("act", 1), "dve": ("dve", 1),
           "sp": ("spd", 16), "pool": ("poold", 16)}


class Prog:

    def __init__(self):
        self.nc = bass.Bass("TRN2", target_bir_lowering=False)
        self.q = {e: [] for e in ENG_ATTRS}
        self.counts = {}
        self.waited = {e: {} for e in ENG_ATTRS}
        self._sem_names = []
        self._ctx = []

    def sbuf(self, name, shape, dtype):
        g = self.nc.sbuf_tensor(name, shape, dtype)
        t = g.__enter__()
        self._ctx.append(g)
        return t

    def psum(self, name, shape, dtype=F32):
        g = self.nc.psum_tensor(name, shape, dtype)
        t = g.__enter__()
        self._ctx.append(g)
        return t

    def dram(self, name, shape, dtype, kind="Internal"):
        return self.nc.dram_tensor(name, shape, dtype, kind=kind)

    def _use_sem(self, name):
        if name not in self.counts:
            self.counts[name] = 0
            self._sem_names.append(name)

    def wait(self, eng, *tokens):
        for t in tokens:
            if t is None:
                continue
            sem, val = t
            if val <= 0:
                continue
            self._use_sem(sem)
            if self.waited[eng].get(sem, 0) >= val:
                continue
            self.waited[eng][sem] = val
            self.q[eng].append(("wait", sem, val))

    def op(self, eng, fn, sem="auto", n=None):
        if sem == "auto":
            sem, dn = ENG_SEM[eng]
            n = dn if n is None else n
        if sem is not None:
            self._use_sem(sem)
            self.counts[sem] += (n if n is not None else 1)
            tok = (sem, self.counts[sem])
        else:
            tok = None
        self.q[eng].append(("op", fn, sem, (n if n is not None else 1)))
        return tok

    def dma(self, eng, out, in_, **kw):
        return self.op(eng, lambda E: E.dma_start(out=out, in_=in_, **kw))

    def settle(self, eng):
        j = self._junk
        if eng == "act":
            return self.op("act", lambda E: E.copy(j[0:1, 0:8], j[0:1, 8:16]))
        if eng == "dve":
            return self.op("dve", lambda E: E.tensor_copy(j[0:1, 0:8], j[0:1, 8:16]))
        raise ValueError(eng)

    def cc(self, kind, op_, ins, outs, groups):
        return self.op("pool", lambda E: E.collective_compute(
            kind, op_, replica_groups=groups, ins=ins, outs=outs), sem="cc", n=1)

    def barrier(self):
        toks = []
        for eng in ENG_ATTRS:
            for item in reversed(self.q[eng]):
                if item[0] == "op" and item[2] is not None:
                    toks.append((item[2], self.counts[item[2]]))
                    break
        for eng in ENG_ATTRS:
            self.wait(eng, *toks)
        return toks

    def build(self):
        self._sems = {}
        sem_ctx = []
        for name in self._sem_names:
            g = self.nc.semaphore(name)
            self._sems[name] = g.__enter__()
            sem_ctx.append(g)
        bg = self.nc.Block()
        block = bg.__enter__()
        for eng, attr in ENG_ATTRS.items():
            items = self.q[eng]
            if not items:
                continue

            def body(E, items=items):
                for item in items:
                    if item[0] == "wait":
                        E.wait_ge(self._sems[item[1]], item[2])
                    else:
                        _, fn, sem, n = item
                        instr = fn(E)
                        if sem is not None:
                            instr.then_inc(self._sems[sem], n)
            getattr(block, attr)(body)
        bg.__exit__(None, None, None)
        for g in reversed(sem_ctx):
            g.__exit__(None, None, None)
        for g in reversed(self._ctx):
            g.__exit__(None, None, None)
        return self.nc


class Ctx:
    def __init__(self, P: Prog):
        self.P = P
        self.X = [P.sbuf(f"X{c}", [128, TPC], F32) for c in range(NCH)]
        self.XN = [P.sbuf(f"XN{c}", [128, TPC], BF16) for c in range(NCH)]
        self.BIG = P.sbuf("BIG", [128, 16896], BF16)
        self.STR = P.sbuf("STR", [128, 16896], BF16)
        self.SF = P.sbuf("SF", [128, 2048], F32)
        self.SF2 = P.sbuf("SF2", [128, 1024], F32)
        self.ROWS = P.sbuf("ROWS", [1, 2048], F32)
        self.ROWB = P.sbuf("ROWB", [1, 2048], BF16)
        self.ident = P.sbuf("identsb", [128, 128], BF16)
        self.ones_col = P.sbuf("ones_col", [128, 1], BF16)
        self.ones_row = P.sbuf("ones_row", [1, 128], BF16)
        self.qg = P.sbuf("qgsb", [128, TPC], F32)
        self.kg = P.sbuf("kgsb", [128, 32], F32)
        self.bias128 = P.sbuf("bias128", [128, 4], F32)
        P._junk = P.sbuf("junk", [1, 16], F32)
        self.BT = P.sbuf("BT", [128, 24], F32)
        self.pA = P.psum("pA", [128, 1024])
        self.pB = P.psum("pB", [128, 1024])
        self.pC = P.psum("pC", [128, 1024])
        self.pD = P.psum("pD", [128, 1024])
        self.pairs = [[0, 1], [2, 3], [4, 5], [6, 7]]

    def init_consts(self, x_in, ident_in, qg_in, kg_in):
        P = self.P
        for c in range(NCH):
            tx = P.dma("sp", self.X[c][:, :], x_in[c * 128:(c + 1) * 128, :])
        ti = P.dma("sp", self.ident[:, :], ident_in[:, :])
        tq = P.dma("sp", self.qg[:, :], qg_in[:, :])
        tk = P.dma("sp", self.kg[:, :], kg_in[:, :])
        t2 = P.op("dve", lambda E: E.memset(self.ones_col[:, :], 1.0))
        t2 = P.op("dve", lambda E: E.memset(self.ones_row[:, :], 1.0))
        self.init_tok = tk
        self.init_dve = t2


DEBUG_BARRIER = False
LN_HALVES = 2
MOE_DBG = False
MHA_STOP = None


def _dbg_bar(P):
    if DEBUG_BARRIER:
        P.barrier()


def emit_ln(ctx: Ctx, eps=1e-5, src=None, dst=None, gamma=None, beta=None,
            resid=False):
    P = ctx.P
    X = src or ctx.X
    XN = dst or ctx.XN
    src_bf = X is not ctx.X
    pD = ctx.pD
    x2b = ctx.STR
    last = None
    stats_tok = None
    for half in range(LN_HALVES):
        sl = slice(half * 1024, (half + 1) * 1024)
        if stats_tok is not None:
            P.wait("act", stats_tok)
        copy_toks, sq_toks = [], []
        for c in range(NCH):
            if not src_bf:
                tcopy = P.op("act", lambda E, c=c, sl=sl: E.copy(XN[c][:, sl], X[c][:, sl]))
            else:
                tcopy = None
            tsq = P.op("act", lambda E, c=c, sl=sl: E.activation(
                x2b[:, c * 2048: c * 2048 + 1024], X[c][:, sl], AF.Square))
            copy_toks.append(tcopy)
            sq_toks.append(tsq)
        if last is not None:
            P.wait("pe", last)
        for c in range(NCH):
            P.wait("pe", copy_toks[c], sq_toks[c])
            srcc = XN[c] if not src_bf else X[c]
            for j in range(2):
                jsl = slice(j * 512, (j + 1) * 512)
                P.op("pe", lambda E, c=c, j=j, jsl=jsl, srcc=srcc, sl=sl: E.matmul(
                    pD[0:1, jsl], ctx.ones_col[:, 0:1],
                    srcc[:, sl][:, jsl], start=(c == 0), stop=(c == NCH - 1)),
                    sem=None)
                tok = P.op("pe", lambda E, c=c, j=j, jsl=jsl: E.matmul(
                    pD[32:33, jsl], ctx.ones_col[:, 0:1],
                    x2b[:, c * 2048: c * 2048 + 1024][:, jsl],
                    start=(c == 0), stop=(c == NCH - 1)))
        stats_tok = tok
        _dbg_bar(P)
        R, RB = ctx.ROWS, ctx.ROWB
        m = R[0:1, 0:1024]
        var = R[0:1, 1024:2048]
        P.wait("dve", stats_tok)
        t = P.op("dve", lambda E, m=m: E.tensor_scalar(m, pD[0:1, 0:1024], 1.0 / D, None, ALU.mult))
        P.wait("dve", t)
        t = P.op("dve", lambda E, m=m, var=var: E.tensor_tensor(var, m, m, op=ALU.mult))
        P.wait("dve", t)
        t = P.op("dve", lambda E, var=var: E.scalar_tensor_tensor(
            var, pD[32:33, 0:1024], 1.0 / D, var, ALU.mult, ALU.subtract))
        P.wait("dve", t)
        t = P.op("dve", lambda E, var=var: E.tensor_scalar(var, var, float(eps), None, ALU.add))
        P.wait("act", t)
        t = P.op("act", lambda E, var=var: E.activation(var, var, AF.Sqrt))
        P.wait("dve", t)
        t = P.op("dve", lambda E, var=var: E.reciprocal(var, var))
        P.wait("dve", t)
        t2 = P.op("dve", lambda E, RB=RB, var=var: E.tensor_copy(RB[0:1, 0:1024], var))
        P.wait("dve", t2)
        t1 = P.op("dve", lambda E, m=m, var=var: E.tensor_tensor(m, m, var, op=ALU.mult))
        P.wait("dve", t1)
        t3 = P.op("dve", lambda E, RB=RB, m=m: E.tensor_copy(RB[0:1, 1024:2048], m))
        P.wait("dve", t3)
        t3s = P.settle("dve")
        _dbg_bar(P)
        P.wait("pe", t2, t3, t3s, ctx.init_dve)
        for j in range(2):
            jsl = slice(j * 512, (j + 1) * 512)
            ta = P.op("pe", lambda E, jsl=jsl, RB=RB: E.matmul(
                ctx.pA[:, jsl], ctx.ones_row[0:1, :], RB[0:1, jsl],
                start=True, stop=True))
            tb = P.op("pe", lambda E, jsl=jsl, j=j, RB=RB: E.matmul(
                ctx.pB[:, jsl], ctx.ones_row[0:1, :], RB[0:1, 1024 + j * 512:1024 + (j + 1) * 512],
                start=True, stop=True))
        _dbg_bar(P)
        P.wait("dve", ta, tb)
        for c in range(NCH):
            t = P.op("dve", lambda E, c=c, sl=sl: E.tensor_tensor(
                ctx.SF[:, 0:1024], X[c][:, sl], ctx.pA[:, :], op=ALU.mult))
            P.wait("dve", t)
            if gamma is None:
                t = P.op("dve", lambda E, c=c, sl=sl: E.tensor_tensor(
                    XN[c][:, sl], ctx.SF[:, 0:1024], ctx.pB[:, :], op=ALU.subtract))
            else:
                t = P.op("dve", lambda E, c=c: E.tensor_tensor(
                    ctx.SF2[:, 0:1024], ctx.SF[:, 0:1024], ctx.pB[:, :], op=ALU.subtract))
                P.wait("dve", t)
                t = P.op("dve", lambda E, c=c, sl=sl: E.tensor_scalar(
                    XN[c][:, sl], ctx.SF2[:, 0:1024], gamma[:, c:c + 1], beta[:, c:c + 1],
                    ALU.mult, ALU.add))
            P.wait("dve", t)
        last = t
        _dbg_bar(P)
    return last


def emit_moe(ctx: Ctx, gate_ap, gate_layout, E, INTER, topk,
             w1_ap, w3_ap, w2_ap, shared=None, eps20=False):
    P = ctx.P
    XN, X = ctx.XN, ctx.X
    pA, pB, pC, pD = ctx.pA, ctx.pB, ctx.pC, ctx.pD
    SF, SF2 = ctx.SF, ctx.SF2
    NIC = INTER // 128
    W1B = [0, 1024, 2048]
    W3B = [4096, 5120, 6144]
    W2B = [8192, 10240]
    SCO = 12288
    GWT = 12800
    GWR = 14000
    STR = ctx.STR

    gwt = [STR[:, GWT + c * E: GWT + (c + 1) * E] for c in range(NCH)]
    if gate_layout == "he":
        tg = P.dma("pool", STR[:, GWT:GWT + NCH * E].rearrange("p (c e) -> p c e", c=NCH),
                   gate_ap.rearrange("(c p) e -> p c e", p=128))
        gw_tok = tg
    else:
        traw = P.dma("pool", STR[0:16, GWR:GWR + 1024], gate_ap[:, :])
        P.wait("pe", traw, ctx.init_tok)
        tcp = None
        for c in range(NCH):
            P.wait("pe", tcp)
            t = P.op("pe", lambda E_, c=c: E_.matmul(
                pD[:, 0:128], STR[0:16, GWR + c * 128:GWR + (c + 1) * 128],
                ctx.ident[0:16, :], start=True, stop=True))
            P.wait("act", t)
            tcp = P.op("act", lambda E_, c=c: E_.copy(gwt[c][:, :], pD[:, 0:E]))
        gw_tok = P.settle("act")

    if MOE_DBG:
        P.wait("dve", gw_tok)
        tdbg = P.op("dve", lambda E_: E_.tensor_copy(SF[:, 1792:1792 + NCH * E], STR[:, GWT:GWT + NCH * E]))
        P.wait("dve", tdbg)
    P.wait("pe", gw_tok)
    for i in range(NT):
        for c in range(NCH):
            tl = P.op("pe", lambda E_, i=i, c=c: E_.matmul(
                pC[:, i * E:(i + 1) * E], XN[c][:, i * 128:(i + 1) * 128],
                gwt[c][:, :], start=(c == 0), stop=(c == NCH - 1)),
                sem=None if c < NCH - 1 else "pe")
    logits_tok = tl
    if MOE_DBG:
        P.wait("dve", logits_tok)
        tdb = P.op("dve", lambda E_: E_.tensor_copy(SF[:, 1536:1536 + 16 * E], pC[:, 0:16 * E]))
        P.wait("dve", tdb)

    ex = SF[:, 0:16 * E]
    work = SF[:, 512:512 + 16 * E]
    msk = SF[:, 1024:1024 + 16 * E]
    mx = SF2[:, 0:16]
    Ssum = SF2[:, 16:32]
    den = SF2[:, 48:64]

    def v3(ap_):
        return ap_.rearrange("p (t e) -> p t e", e=E)

    def b3(ap16):
        return ap16.rearrange("p (t o) -> p t o", o=1).broadcast_to([128, 16, E])

    P.wait("dve", logits_tok)
    t = P.op("dve", lambda E_: E_.reduce_max(mx.rearrange("p (t o) -> p t o", o=1),
                                             v3(pC[:, 0:16 * E]), axis=AX))
    P.wait("dve", t)
    t = P.op("dve", lambda E_: E_.tensor_tensor(v3(ex), v3(pC[:, 0:16 * E]), b3(mx), op=ALU.subtract))
    P.wait("act", t)
    t = P.op("act", lambda E_: E_.activation(ex, ex, AF.Exp))
    P.wait("dve", t)
    t = P.op("dve", lambda E_: E_.reduce_sum(Ssum.rearrange("p (t o) -> p t o", o=1), v3(ex), axis=AX))
    P.wait("dve", t)
    t = P.op("dve", lambda E_: E_.tensor_copy(work, ex))
    for r in range(topk - 1):
        P.wait("dve", t)
        t = P.op("dve", lambda E_: E_.reduce_max(mx.rearrange("p (t o) -> p t o", o=1), v3(work), axis=AX))
        P.wait("dve", t)
        t = P.op("dve", lambda E_: E_.tensor_tensor(v3(msk), v3(work), b3(mx), op=ALU.is_lt))
        P.wait("dve", t)
        t = P.op("dve", lambda E_: E_.tensor_tensor(work, work, msk, op=ALU.mult))
    P.wait("dve", t)
    t = P.op("dve", lambda E_: E_.reduce_max(mx.rearrange("p (t o) -> p t o", o=1), v3(work), axis=AX))
    P.wait("dve", t)
    t = P.op("dve", lambda E_: E_.tensor_tensor(v3(msk), v3(ex), b3(mx), op=ALU.is_ge))
    P.wait("dve", t)
    t = P.op("dve", lambda E_: E_.tensor_tensor(ex, ex, msk, op=ALU.mult))
    P.wait("dve", t)
    t = P.op("dve", lambda E_: E_.reduce_sum(den.rearrange("p (t o) -> p t o", o=1), v3(ex), axis=AX))
    P.wait("dve", t)
    if eps20:
        t = P.op("dve", lambda E_: E_.scalar_tensor_tensor(den, Ssum, 1e-20, den, ALU.mult, ALU.add))
        P.wait("dve", t)
    t = P.op("dve", lambda E_: E_.reciprocal(den, den))
    P.wait("dve", t)
    t = P.op("dve", lambda E_: E_.tensor_tensor(v3(STR[:, SCO:SCO + 16 * E]), v3(ex), b3(den), op=ALU.mult))
    scores_tok = t

    experts = [(e, w1_ap, w3_ap, w2_ap, True) for e in range(E)]
    if shared is not None:
        experts.append((0, shared[0], shared[1], shared[2], False))

    dma_tok = {}
    use_tok = {}
    halfsl = [slice(0, 1024), slice(1024, 2048)]
    prev_silu = None
    prev_umul = None
    prev_tact = None
    prev_down = None
    prev_xadd = {}
    bcast_read_tok = None

    for half in range(2):
        hs = halfsl[half]
        for ei, (e, a1, a3, a2, scaled) in enumerate(experts):
            slot = ei % 2
            if scaled:
                tts = []
                P.wait("pe", scores_tok)
                if bcast_read_tok is not None:
                    P.wait("pe", bcast_read_tok)
                for il in range(8):
                    i = half * 8 + il
                    tt = P.op("pe", lambda E_, i=i, il=il, e=e: E_.matmul(
                        pD[0:1, il * 128:(il + 1) * 128],
                        STR[:, SCO + i * E + e: SCO + i * E + e + 1],
                        ctx.ident[:, :], start=True, stop=True))
                    tts.append(tt)
                P.wait("dve", tts[-1])
                trow = P.op("dve", lambda E_: E_.tensor_copy(ctx.ROWB[0:1, 0:1024], pD[0:1, 0:1024]))
                P.wait("dve", trow)
                trow = P.settle("dve")
                P.wait("pe", trow)
                for j in range(2):
                    tb = P.op("pe", lambda E_, j=j: E_.matmul(
                        pD[:, j * 512:(j + 1) * 512], ctx.ones_row[0:1, :],
                        ctx.ROWB[0:1, j * 512:(j + 1) * 512], start=True, stop=True))
                bcast_tok = tb
            else:
                bcast_tok = None

            act_toks = []

            def _issue_w13(icx):
                sl2 = icx % 3
                for kind, base, wap in (("w1", W1B, a1), ("w3", W3B, a3)):
                    key = (kind, sl2)
                    P.wait("pool", use_tok.get(key))
                    src = wap[e * 1024:(e + 1) * 1024, icx * 128:(icx + 1) * 128] if scaled else \
                        wap[:, icx * 128:(icx + 1) * 128]
                    dma_tok[key] = P.dma("pool", STR[:, base[sl2]:base[sl2] + 1024]
                                         .rearrange("p (c i) -> p c i", c=NCH),
                                         src.rearrange("(c p) i -> p c i", p=128))

            _issue_w13(0)
            if NIC > 1:
                _issue_w13(1)
            for ic in range(NIC):
                wslot = ic % 3
                if ic + 2 < NIC:
                    _issue_w13(ic + 2)
                P.wait("pe", dma_tok[("w1", wslot)], dma_tok[("w3", wslot)])
                P.wait("pe", prev_silu)
                P.wait("pe", prev_umul)
                for c in range(NCH):
                    for j in range(2):
                        tg_ = P.op("pe", lambda E_, c=c, j=j, base=W1B[wslot], hs=hs: E_.matmul(
                            pB[:, j * 512:(j + 1) * 512],
                            STR[:, base + c * 128:base + (c + 1) * 128],
                            XN[c][:, hs][:, j * 512:(j + 1) * 512],
                            start=(c == 0), stop=(c == NCH - 1)),
                            sem=None if c < NCH - 1 else "pe")
                for c in range(NCH):
                    for j in range(2):
                        tu_ = P.op("pe", lambda E_, c=c, j=j, base=W3B[wslot], hs=hs: E_.matmul(
                            pC[:, j * 512:(j + 1) * 512],
                            STR[:, base + c * 128:base + (c + 1) * 128],
                            XN[c][:, hs][:, j * 512:(j + 1) * 512],
                            start=(c == 0), stop=(c == NCH - 1)),
                            sem=None if c < NCH - 1 else "pe")
                use_tok[("w1", wslot)] = tg_
                use_tok[("w3", wslot)] = tu_
                P.wait("act", tg_, prev_tact)
                tsil = P.op("act", lambda E_: E_.activation(SF2[:, 0:1024], pB[:, :], AF.Silu))
                prev_silu = tsil
                P.wait("dve", tsil, tu_)
                tum = P.op("dve", lambda E_: E_.tensor_tensor(SF2[:, 0:1024], SF2[:, 0:1024], pC[:, :], op=ALU.mult))
                prev_umul = tum
                P.wait("dve", tum, prev_down)
                if scaled:
                    P.wait("dve", bcast_tok)
                    tact = P.op("dve", lambda E_, ic=ic: E_.tensor_tensor(
                        ctx.BIG[:, ic * 1024:(ic + 1) * 1024], SF2[:, 0:1024], pD[:, :], op=ALU.mult))
                    bcast_read_tok = tact
                else:
                    tact = P.op("dve", lambda E_, ic=ic: E_.tensor_copy(
                        ctx.BIG[:, ic * 1024:(ic + 1) * 1024], SF2[:, 0:1024]))
                P.wait("dve", tact)
                prev_tact = tact
                act_toks.append(tact)

            w2tok = {}

            def _issue_w2(hcx):
                sl2 = hcx % 2
                key = ("w2", sl2)
                P.wait("pool", use_tok.get(key))
                src2 = a2[e * INTER:(e + 1) * INTER, hcx * 128:(hcx + 1) * 128] if scaled else \
                    a2[:, hcx * 128:(hcx + 1) * 128]
                w2tok[hcx] = P.dma("pool", STR[:, W2B[sl2]:W2B[sl2] + NIC * 128]
                                   .rearrange("p (c i) -> p c i", c=NIC),
                                   src2.rearrange("(c p) i -> p c i", p=128))

            _issue_w2(0)
            for hc in range(NCH):
                wslot = hc % 2
                if hc + 1 < NCH:
                    _issue_w2(hc + 1)
                P.wait("pe", w2tok[hc], act_toks[-1])
                P.wait("pe", prev_xadd.get("last"))
                for ic in range(NIC):
                    for j in range(2):
                        td_ = P.op("pe", lambda E_, ic=ic, j=j, base=W2B[wslot]: E_.matmul(
                            pA[:, j * 512:(j + 1) * 512],
                            STR[:, base + ic * 128:base + (ic + 1) * 128],
                            ctx.BIG[:, ic * 1024 + j * 512: ic * 1024 + j * 512 + 512],
                            start=(ic == 0), stop=(ic == NIC - 1)),
                            sem=None if ic < NIC - 1 else "pe")
                use_tok[("w2", wslot)] = td_
                prev_down = td_
                P.wait("dve", td_)
                tx = P.op("dve", lambda E_, hc=hc, hs=hs: E_.tensor_tensor(
                    X[hc][:, hs], X[hc][:, hs], pA[:, :], op=ALU.add))
                prev_xadd["last"] = tx
    return tx


def _load_wT(ctx, w_ap, tag, n=1024):
    P = ctx.P
    STR = ctx.STR
    raw = STR[:, 0:8192]
    wt = STR[:, 8192:16384]
    t_ld = P.dma("pool", raw.rearrange("p (c i) -> p c i", c=8),
                 w_ap.rearrange("(c p) i -> p c i", p=128))
    P.wait("pe", t_ld, ctx.init_tok)
    tcp = None
    for r in range(8):
        P.wait("pe", tcp)
        for c in range(8):
            tt = P.op("pe", lambda E_, r=r, c=c: E_.matmul(
                ctx.pD[:, c * 128:(c + 1) * 128],
                raw.rearrange("p (c i) -> p c i", c=8)[:, c, r * 128:(r + 1) * 128],
                ctx.ident[:, :], start=True, stop=True),
                sem=None if c < 7 else "pe")
        P.wait("act", tt)
        tcp = P.op("act", lambda E_, r=r: E_.copy(wt[:, r * 1024:(r + 1) * 1024], ctx.pD[:, :]))
    tok = P.settle("act")
    chunks = [wt[:, r * 1024:(r + 1) * 1024] for r in range(8)]
    return tok, chunks


def _load_w_direct(ctx, w_ap, tag, dout=1024, off=0):
    P = ctx.P
    n_el = 8 * dout
    buf = ctx.STR[:, off:off + n_el]
    t_ld = P.dma("pool", buf.rearrange("p (c i) -> p c i", c=8),
                 w_ap.rearrange("(c p) i -> p c i", p=128))
    chunks = [buf[:, c * dout:(c + 1) * dout] for c in range(8)]
    return t_ld, chunks


def emit_fm_proj(ctx, w_chunks, w_tok, r_list, scale=None, bias_col=None,
                 dram_out=None, dram_row0=0, norm_units=None, umax=None,
                 norm_accum=False, extra_wait=None, src_chunks=None,
                 dst_chunks=None):
    P = ctx.P
    src = src_chunks or ctx.XN
    stage = ctx.BIG[:, 0:2048]
    sq = ctx.BIG[:, 2048:4096]
    last_dma = None
    tnorm = None
    prev_ev = None
    nr_total = len(r_list)
    for ri, r in enumerate(r_list):
        P.wait("pe", w_tok, extra_wait)
        P.wait("pe", prev_ev)
        for c in range(NCH):
            for j in range(4):
                psl = ctx.pA if j < 2 else ctx.pB
                po = (j % 2) * 512
                tmm = P.op("pe", lambda E_, r=r, c=c, j=j, psl=psl, po=po, src=src: E_.matmul(
                    psl[:, po:po + 512], w_chunks[c][:, r * 128:(r + 1) * 128],
                    src[c][:, j * 512:(j + 1) * 512],
                    start=(c == 0), stop=(c == NCH - 1)),
                    sem=None if (c < NCH - 1) else "pe")
        P.wait("act", tmm, last_dma)
        if dst_chunks is not None:
            stage = dst_chunks[r]
        for j in range(4):
            psl = ctx.pA if j < 2 else ctx.pB
            po = (j % 2) * 512
            if bias_col is not None:
                tev = P.op("act", lambda E_, r=r, j=j, psl=psl, po=po, stage=stage: E_.activation(
                    stage[:, j * 512:(j + 1) * 512], psl[:, po:po + 512], AF.Identity,
                    bias=bias_col[:, r:r + 1], scale=(scale if scale is not None else 1.0)))
            elif scale is not None:
                tev = P.op("act", lambda E_, r=r, j=j, psl=psl, po=po, stage=stage: E_.activation(
                    stage[:, j * 512:(j + 1) * 512], psl[:, po:po + 512], AF.Copy,
                    scale=scale))
            else:
                tev = P.op("act", lambda E_, r=r, j=j, psl=psl, po=po, stage=stage: E_.copy(
                    stage[:, j * 512:(j + 1) * 512], psl[:, po:po + 512]))
        if dram_out is not None:
            P.wait("sp", tev)
            last_dma = P.dma("sp", dram_out[dram_row0 + r * 128:dram_row0 + (r + 1) * 128, :],
                             stage[:, :])
        units = []
        if norm_units is not None:
            units = norm_units.get(r, [])
        if units or norm_accum:
            P.wait("act", tev, tnorm)
            tsq = P.op("act", lambda E_, stage=stage: E_.activation(sq[:, :], stage[:, :], AF.Square))
            if norm_accum:
                units = [(0, 0, 128)] if not units else units
            for (u, base, nrows) in units:
                P.wait("pe", tsq, tnorm)
                st_flag = (ri == 0) if norm_accum else True
                sp_flag = (ri == nr_total - 1) if norm_accum else True
                for j in range(4):
                    prow = ctx.pD[0:1, :] if j < 2 else ctx.pD[32:33, :]
                    po = (j % 2) * 512
                    tpn = P.op("pe", lambda E_, base=base, nrows=nrows, j=j, prow=prow,
                               po=po, st=st_flag, sp=sp_flag: E_.matmul(
                        prow[0:1, po:po + 512], ctx.ones_col[base:base + nrows, 0:1],
                        sq[base:base + nrows, j * 512:(j + 1) * 512],
                        start=st, stop=sp), sem=None if j < 3 else "pe")
                if norm_accum and ri < nr_total - 1:
                    continue
                P.wait("dve", tpn)
                t1 = P.op("dve", lambda E_, u=u: E_.reduce_max(
                    ctx.SF2[0:1, 40:41], ctx.pD[0:1, 0:1024], axis=AX))
                P.wait("dve", t1)
                t2 = P.op("dve", lambda E_, u=u: E_.reduce_max(
                    ctx.SF2[0:1, 41:42], ctx.pD[32:33, 0:1024], axis=AX))
                P.wait("dve", t2)
                t3 = P.op("dve", lambda E_, u=u, umax=umax: E_.tensor_tensor(
                    umax[0:1, u:u + 1], ctx.SF2[0:1, 40:41], ctx.SF2[0:1, 41:42],
                    op=ALU.max))
                P.wait("dve", t3)
                tnorm = t3
        prev_ev = tev
    return last_dma, tnorm


def emit_v_proj(ctx, w_chunks, w_tok, dram_out, dh1=None, bias_row=None,
                scale=None, extra_wait=None):
    P = ctx.P
    stage = ctx.BIG[:, 4096:6144]
    last_dma = None
    prev_ev = None
    for tp in range(NT // 2):
        P.wait("pe", w_tok, extra_wait, prev_ev, last_dma)
        for half in range(2):
            tt = tp * 2 + half
            psl = ctx.pA if half == 0 else ctx.pB
            for c in range(NCH):
                for j in range(2):
                    lastmm = (c == NCH - 1 and j == 1 and bias_row is None)
                    tmm = P.op("pe", lambda E_, tt=tt, c=c, j=j, psl=psl: E_.matmul(
                        psl[:, j * 512:(j + 1) * 512],
                        ctx.XN[c][:, tt * 128:(tt + 1) * 128],
                        w_chunks[c][:, j * 512:(j + 1) * 512],
                        start=(c == 0), stop=(c == NCH - 1 and bias_row is None)),
                        sem="pe" if lastmm else None)
            if bias_row is not None:
                for j in range(2):
                    tmm = P.op("pe", lambda E_, j=j, psl=psl: E_.matmul(
                        psl[:, j * 512:(j + 1) * 512], ctx.ones_row[0:1, :],
                        bias_row[0:1, j * 512:(j + 1) * 512],
                        start=False, stop=True),
                        sem=None if j < 1 else "pe")
        P.wait("act", tmm)
        for half in range(2):
            psl = ctx.pA if half == 0 else ctx.pB
            if scale is not None:
                tev = P.op("act", lambda E_, half=half, psl=psl: E_.activation(
                    stage[:, half * 1024:(half + 1) * 1024], psl[:, :], AF.Copy, scale=scale))
            else:
                tev = P.op("act", lambda E_, half=half, psl=psl: E_.copy(
                    stage[:, half * 1024:(half + 1) * 1024], psl[:, :]))
        prev_ev = tev
        P.wait("sp", tev)
        last_dma = P.dma("sp",
                         dram_out[tp * 256:(tp + 1) * 256, :].rearrange("(h p) e -> p h e", p=128),
                         stage.rearrange("p (h e) -> p h e", h=2))
    return last_dma


def emit_attn_mha(ctx: Ctx, tag, H, dh, wq_ap, wk_ap, wv_ap, transposed,
                  w_out_ap=None, b_out_ap=None, gamma_ap=None, beta_ap=None,
                  core_id_pairs=None):
    P = ctx.P
    STR, BIG = ctx.STR, ctx.BIG
    scale = float(dh) ** -0.5
    U = 16
    G = H // 2
    grows = 2 * dh

    Qd = P.dram(f"{tag}_Qd", [1024, TPC], BF16).ap()
    Kd = P.dram(f"{tag}_Kd", [1024, TPC], BF16).ap()
    Vd = P.dram(f"{tag}_Vd", [TPC, 1024], BF16).ap()
    Kg = P.dram(f"{tag}_Kg", [2048, TPC], BF16).ap()
    Vg = P.dram(f"{tag}_Vg", [2 * TPC, 1024], BF16).ap()
    kmd = P.dram(f"{tag}_kmd", [1, U], F32).ap()
    kmg = P.dram(f"{tag}_kmg", [1, U], F32).ap()

    qmax = ctx.SF2[0:1, 0:U]
    kmax = ctx.SF2[0:1, 32:32 + U]
    norm_units = {r: [(2 * r, 0, 64), (2 * r + 1, 64, 64)] for r in range(NCH)}

    if transposed:
        wtok, wch = _load_wT(ctx, wq_ap, tag + "q")
    else:
        wtok, wch = _load_w_direct(ctx, wq_ap, tag + "q")
    tq, tnq = emit_fm_proj(ctx, wch, wtok, list(range(NCH)), scale=scale,
                           dram_out=Qd, norm_units=norm_units, umax=qmax)
    P.barrier()
    if transposed:
        wtok, wch = _load_wT(ctx, wk_ap, tag + "k")
    else:
        wtok, wch = _load_w_direct(ctx, wk_ap, tag + "k")
    tk, tnk = emit_fm_proj(ctx, wch, wtok, list(range(NCH)),
                           dram_out=Kd, norm_units=norm_units, umax=kmax)
    P.barrier()
    if transposed:
        wtok, wch = _load_wT(ctx, wv_ap, tag + "v")
    else:
        wtok, wch = _load_w_direct(ctx, wv_ap, tag + "v")
    tv = emit_v_proj(ctx, wch, wtok, Vd)
    P.wait("sp", tnk)
    tkm = P.dma("sp", kmd[0:1, :], kmax)
    P.barrier()

    c1 = P.cc("AllGather", ALU.bypass, [Kd.tensor.ap().opt()], [Kg.tensor.ap().opt()], ctx.pairs)
    c2 = P.cc("AllGather", ALU.bypass, [Vd.tensor.ap().opt()], [Vg.tensor.ap().opt()], ctx.pairs)
    c3 = P.cc("AllReduce", ALU.max, [kmd.tensor.ap().opt()], [kmg.tensor.ap().opt()], ctx.pairs)
    P.wait("sp", c3)
    tkg = P.dma("sp", kmax, kmg[0:1, :])

    if MHA_STOP == "proj":
        P.barrier()
        return (Qd, Kd, Vd, Kg, Vg, kmg)
    P.wait("dve", tkg, tnq)
    t = P.op("dve", lambda E_: E_.tensor_tensor(ctx.SF2[0:1, 64:64 + U], qmax, kmax, op=ALU.mult))
    P.wait("act", t)
    t = P.op("act", lambda E_: E_.activation(ctx.SF2[0:1, 64:64 + U], ctx.SF2[0:1, 64:64 + U], AF.Sqrt))
    P.wait("dve", t)
    t = P.op("dve", lambda E_: E_.tensor_scalar(ctx.ROWB[0:1, 0:U], ctx.SF2[0:1, 64:64 + U],
                                                -1.0, None, ALU.mult))
    P.wait("dve", t)
    brow_tok = P.settle("dve")

    KB = STR[:, 0:4096]
    QB = STR[:, 4096:6144]
    W_ = 2 * dh + 2
    VPW = 32 * W_
    VP = STR[:, 6144:6144 + VPW]
    VPv = VP.rearrange("p (ki w) -> p ki w", w=W_)
    EB = STR[:, 10400:14496]

    if w_out_ap is not None:
        O_all = [BIG[:, c * 2048:(c + 1) * 2048] for c in range(NCH)]
        OS = STR[:, 14496:15520]

    ep_toks = []
    prev_core = {}
    exp_war = {}
    for g in range(G):
        if dh == 64:
            ua, ub = 2 * g, 2 * g + 1
        else:
            ua = ub = g
        P.wait("sp", c1, c2)
        pk = prev_core.get("kb")
        P.wait("sp", pk)
        d1 = P.dma("sp", KB[0:grows, 0:2048], Kg[grows * g:grows * (g + 1), :])
        d2 = P.dma("sp", KB[0:grows, 2048:4096], Kg[1024 + grows * g:1024 + grows * (g + 1), :])
        d3 = P.dma("sp", QB[0:grows, :], Qd[grows * g:grows * (g + 1), :])
        d4 = P.dma("sp", VPv[:, :, 0:dh],
                   Vg[:, grows * g:grows * g + dh].rearrange("(ki p) e -> p ki e", p=128))
        d5 = P.dma("sp", VPv[:, :, dh + 1:2 * dh + 1],
                   Vg[:, grows * g + dh:grows * (g + 1)].rearrange("(ki p) e -> p ki e", p=128))
        P.wait("dve", d5)
        tm1 = P.op("dve", lambda E_: E_.memset(VPv[:, :, dh:dh + 1], 1.0))
        P.wait("dve", tm1)
        tm2 = P.op("dve", lambda E_: E_.memset(VPv[:, :, 2 * dh + 1:2 * dh + 2], 1.0))
        P.wait("pe", brow_tok, d5, tm2, exp_war.get("a"), exp_war.get("b"), prev_core.get("kb"))
        tb1 = P.op("pe", lambda E_, ua=ua: E_.matmul(
            ctx.pA[:, 0:1], ctx.ones_row[0:1, :], ctx.ROWB[0:1, ua:ua + 1],
            start=True, stop=True))
        tb2 = P.op("pe", lambda E_, ub=ub: E_.matmul(
            ctx.pA[:, 1:2], ctx.ones_row[0:1, :], ctx.ROWB[0:1, ub:ub + 1],
            start=True, stop=True))
        P.wait("dve", tb2, exp_war.get("a"), exp_war.get("b"))
        tb3 = P.op("dve", lambda E_: E_.tensor_copy(ctx.bias128[:, 0:2], ctx.pA[:, 0:2]))
        P.wait("dve", tb3)
        tb4 = P.settle("dve")

        exp_tok = {"a": None, "b": None}
        pv_tok = {}
        st_tok = {}
        ep = None
        for qc in range(2):
            qsl = slice(qc * 1024, (qc + 1) * 1024)
            P.wait("pe", ep)
            for ki in range(32):
                for hi, hname in ((0, "a"), (1, "b")):
                    rows = slice(dh * hi, dh * (hi + 1))
                    ps = ctx.pA if hi == 0 else ctx.pB
                    pvp = ctx.pC if hi == 0 else ctx.pD
                    voff = (dh + 1) * hi
                    P.wait("pe", exp_tok[hname], tb4)
                    for j in range(2):
                        tst = P.op("pe", lambda E_, ki=ki, j=j, rows=rows, ps=ps, qsl=qsl: E_.matmul(
                            ps[:, j * 512:(j + 1) * 512],
                            KB[rows, ki * 128:(ki + 1) * 128],
                            QB[rows, qsl][:, j * 512:(j + 1) * 512],
                            start=True, stop=True),
                            sem=None if j < 1 else "pe")
                    st_tok[hname] = tst
                    eoff = 1024 * (2 * (ki % 2) + hi)
                    P.wait("act", tst, pv_tok.get((hname, ki % 2)), tb4)
                    tex = P.op("act", lambda E_, ps=ps, eoff=eoff, hi=hi: E_.activation(
                        EB[:, eoff:eoff + 1024], ps[:, :], AF.Exp,
                        bias=ctx.bias128[:, hi:hi + 1]))
                    exp_tok[hname] = tex
                    exp_war[hname] = tex
                    P.wait("pe", tex)
                    for j in range(2):
                        tpv = P.op("pe", lambda E_, ki=ki, j=j, pvp=pvp, voff=voff, eoff=eoff: E_.matmul(
                            pvp[0:dh + 1, j * 512:(j + 1) * 512],
                            VPv[:, ki, voff:voff + dh + 1],
                            EB[:, eoff + j * 512:eoff + (j + 1) * 512],
                            start=(ki == 0), stop=(ki == 31)),
                            sem=None if j < 1 else "pe")
                    pv_tok[(hname, ki % 2)] = tpv
            for hi, hname in ((0, "a"), (1, "b")):
                pvp = ctx.pC if hi == 0 else ctx.pD
                ps = ctx.pA if hi == 0 else ctx.pB
                P.wait("dve", pv_tok[(hname, 0)], pv_tok[(hname, 1)])
                td = P.op("dve", lambda E_, pvp=pvp: E_.reciprocal(
                    ctx.ROWS[0:1, 0:1024], pvp[dh:dh + 1, 0:1024]))
                P.wait("dve", td)
                td = P.op("dve", lambda E_: E_.tensor_copy(ctx.ROWB[0:1, 1024:2048],
                                                           ctx.ROWS[0:1, 0:1024]))
                P.wait("dve", td)
                td = P.settle("dve")
                P.wait("pe", td, exp_tok[hname])
                for j in range(2):
                    tbc = P.op("pe", lambda E_, j=j, ps=ps: E_.matmul(
                        ps[:, j * 512:(j + 1) * 512], ctx.ones_row[0:1, :],
                        ctx.ROWB[0:1, 1024 + j * 512:1024 + (j + 1) * 512],
                        start=True, stop=True),
                        sem=None if j < 1 else "pe")
                P.wait("act", tbc)
                tnb = P.op("act", lambda E_, ps=ps: E_.copy(ctx.SF2[:, 0:1024], ps[0:128, :]))
                P.wait("dve", tnb)
                if w_out_ap is None:
                    t1 = P.op("dve", lambda E_, pvp=pvp, hi=hi: E_.tensor_tensor(
                        ctx.SF[dh * hi:dh * (hi + 1), 0:1024], pvp[0:dh, :],
                        ctx.SF2[0:dh, 0:1024], op=ALU.mult))
                    P.wait("dve", t1)
                    t2 = P.op("dve", lambda E_, g=g, hi=hi, qsl=qsl: E_.tensor_tensor(
                        ctx.X[g][dh * hi:dh * (hi + 1), qsl],
                        ctx.X[g][dh * hi:dh * (hi + 1), qsl],
                        ctx.SF[dh * hi:dh * (hi + 1), 0:1024], op=ALU.add))
                else:
                    t2 = P.op("dve", lambda E_, pvp=pvp, hi=hi: E_.tensor_tensor(
                        OS[32 * hi:32 * (hi + 1), 0:1024], pvp[0:dh, :],
                        ctx.SF2[0:dh, 0:1024], op=ALU.mult))
                P.wait("dve", t2)
                ep = t2
            if w_out_ap is not None:
                cch, rbase = g // 2, 64 * (g % 2)
                t3 = P.op("dve", lambda E_, cch=cch, rbase=rbase, qsl=qsl: E_.tensor_copy(
                    O_all[cch][rbase:rbase + 64, qsl], OS[0:64, 0:1024]))
                P.wait("dve", t3)
                ep = t3
            prev_core["kb"] = ep
            ep_toks.append(ep)

    P.barrier()
    if MHA_STOP == "core" or w_out_ap is None:
        return ep_toks[-1]

    wtok, wch = _load_w_direct(ctx, w_out_ap, tag + "o")
    bo = ctx.BT[:, 0:8]
    tbo = P.dma("pool", bo, b_out_ap.rearrange("(c p) -> p c", p=128), allow_slow_non_contiguous=True)
    P.wait("pe", tbo)
    _, _ = emit_fm_proj(ctx, wch, wtok, list(range(NCH)), bias_col=bo,
                        src_chunks=O_all, dst_chunks=ctx.XN)
    P.barrier()
    if MHA_STOP == "oproj":
        return ep_toks[-1]
    gm = ctx.BT[:, 8:16]
    bt = ctx.BT[:, 16:24]
    tg1 = P.dma("pool", gm, gamma_ap.rearrange("(c p) -> p c", p=128), allow_slow_non_contiguous=True)
    tg2 = P.dma("pool", bt, beta_ap.rearrange("(c p) -> p c", p=128), allow_slow_non_contiguous=True)
    P.wait("dve", tg1, tg2)
    P.wait("act", tg2)
    P.wait("pe", tg2)
    if MHA_STOP == "noaffine":
        tln = emit_ln(ctx, eps=1e-6, src=ctx.XN, dst=ctx.XN)
    else:
        tln = emit_ln(ctx, eps=1e-6, src=ctx.XN, dst=ctx.XN, gamma=gm, beta=bt)
    P.barrier()
    if MHA_STOP == "ln":
        return tln
    for c in range(NCH):
        tx = P.op("dve", lambda E_, c=c: E_.tensor_tensor(
            ctx.X[c][:, :], ctx.X[c][:, :], ctx.XN[c][:, :], op=ALU.add))
        P.wait("dve", tx)
    return tx


def emit_attn_causal(ctx: Ctx, tag, wq_ap, bq_ap, wk_ap, bk_ap, wv_ap, bv_ap):
    P = ctx.P
    STR, BIG = ctx.STR, ctx.BIG
    d = 1024
    scale = 1.0 / 32.0

    Qd = P.dram(f"{tag}_Qd", [1024, TPC], BF16).ap()
    Kd = P.dram(f"{tag}_Kd", [1024, TPC], BF16).ap()
    Vd = P.dram(f"{tag}_Vd", [TPC, 1024], BF16).ap()
    Kg = P.dram(f"{tag}_Kg", [2048, TPC], BF16).ap()
    Vg = P.dram(f"{tag}_Vg", [2 * TPC, 1024], BF16).ap()
    kmd = P.dram(f"{tag}_kmd", [1, 16], F32).ap()
    kmg = P.dram(f"{tag}_kmg", [1, 16], F32).ap()

    qmax = ctx.SF2[0:1, 0:16]
    kmax = ctx.SF2[0:1, 32:48]

    bqt = ctx.BT[:, 0:8]
    bkt = ctx.BT[:, 8:16]
    tb1 = P.dma("pool", bqt, bq_ap.rearrange("(c p) -> p c", p=128), allow_slow_non_contiguous=True)
    tb2 = P.dma("pool", bkt, bk_ap.rearrange("(c p) -> p c", p=128), allow_slow_non_contiguous=True)
    bvr = ctx.ROWB[0:1, 1024:2048]
    tb3 = P.dma("pool", bvr, bv_ap.rearrange("e -> 1 e" if False else "(o e) -> o e", o=1))
    P.wait("dve", tb1)
    tb4 = P.op("dve", lambda E_: E_.tensor_scalar(ctx.BT[:, 16:24], bqt, scale, None, ALU.mult))
    P.wait("dve", tb4)
    tb5 = P.settle("dve")
    bq32 = ctx.BT[:, 16:24]

    wtok, wch = _load_wT(ctx, wq_ap, tag + "q")
    P.wait("act", tb5)
    _, tnq = emit_fm_proj(ctx, wch, wtok, list(range(NCH)), scale=scale, bias_col=bq32,
                          dram_out=Qd, norm_accum=True, umax=qmax)
    P.barrier()
    wtok, wch = _load_wT(ctx, wk_ap, tag + "k")
    P.wait("act", tb2)
    _, tnk = emit_fm_proj(ctx, wch, wtok, list(range(NCH)), bias_col=bkt,
                          dram_out=Kd, norm_accum=True, umax=kmax)
    P.barrier()
    wtok, wch = _load_wT(ctx, wv_ap, tag + "v")
    P.wait("pool", tb3)
    tv = emit_v_proj(ctx, wch, wtok, Vd, bias_row=bvr)
    P.wait("sp", tnk)
    tkm = P.dma("sp", kmd[0:1, :], kmax)
    P.barrier()

    c1 = P.cc("AllGather", ALU.bypass, [Kd.tensor.ap().opt()], [Kg.tensor.ap().opt()], ctx.pairs)
    c2 = P.cc("AllGather", ALU.bypass, [Vd.tensor.ap().opt()], [Vg.tensor.ap().opt()], ctx.pairs)
    c3 = P.cc("AllReduce", ALU.max, [kmd.tensor.ap().opt()], [kmg.tensor.ap().opt()], ctx.pairs)
    P.wait("sp", c3)
    tkg = P.dma("sp", kmax, kmg[0:1, :])

    P.wait("dve", tkg, tnq)
    t = P.op("dve", lambda E_: E_.tensor_tensor(ctx.SF2[0:1, 64:65], qmax[0:1, 0:1], kmax[0:1, 0:1], op=ALU.mult))
    P.wait("act", t)
    t = P.op("act", lambda E_: E_.activation(ctx.SF2[0:1, 64:65], ctx.SF2[0:1, 64:65], AF.Sqrt))
    P.wait("dve", t)
    t = P.op("dve", lambda E_: E_.tensor_scalar(ctx.ROWB[0:1, 0:1], ctx.SF2[0:1, 64:65], -1.0, None, ALU.mult))
    P.wait("dve", t)
    t = P.settle("dve")
    P.wait("pe", t)
    tbb = P.op("pe", lambda E_: E_.matmul(ctx.pA[:, 0:1], ctx.ones_row[0:1, :],
                                          ctx.ROWB[0:1, 0:1], start=True, stop=True))
    P.wait("dve", tbb)
    tbc = P.op("dve", lambda E_: E_.tensor_copy(ctx.bias128[:, 0:1], ctx.pA[:, 0:1]))
    P.wait("dve", tbc)
    bias_tok = P.settle("dve")

    KT = [STR[:, 0:1024], STR[:, 1024:2048]]
    VT = [STR[:, 2048:3072], STR[:, 3072:4096]]
    QB = STR[:, 4096:8192]
    MB = STR[:, 8192:8704]
    EX = STR[:, 8704:9216]
    EE = [BIG[:, i * 512:(i + 1) * 512] for i in range(32)]

    kdma = {}
    kuse = {}
    prev_ph2 = None
    prev_recip = None
    for qc in range(4):
        qsl = slice(qc * 512, (qc + 1) * 512)
        P.wait("sp", prev_ph2)
        dq = P.dma("sp", QB.rearrange("p (c k) -> p c k", c=8),
                   Qd[:, qsl].rearrange("(c p) k -> p c k", p=128))
        tden = None
        texp_prev = None
        tmask_prev = None
        for ki in range(32):
            sl_ = ki % 2
            ps = ctx.pA if sl_ == 0 else ctx.pB
            rank, kloc = ki // 16, ki % 16
            P.wait("sp", kuse.get(("k", sl_)), c1)
            dk = P.dma("sp", KT[sl_].rearrange("p (c k) -> p c k", c=8),
                       Kg[rank * 1024:(rank + 1) * 1024, kloc * 128:(kloc + 1) * 128]
                       .rearrange("(c p) k -> p c k", p=128))
            kdma[("k", sl_)] = dk
            P.wait("pe", dk, dq, texp_prev, bias_tok)
            for c in range(NCH):
                tst = P.op("pe", lambda E_, c=c, ps=ps, sl_=sl_: E_.matmul(
                    ps[:, 0:512], KT[sl_][:, c * 128:(c + 1) * 128],
                    QB[:, c * 512:(c + 1) * 512],
                    start=(c == 0), stop=(c == NCH - 1)),
                    sem=None if c < NCH - 1 else "pe")
            kuse[("k", sl_)] = tst
            P.wait("act", tst, tmask_prev)
            tex = P.op("act", lambda E_, ps=ps: E_.activation(
                EX[:, :], ps[:, :512], AF.Exp, bias=ctx.bias128[:, 0:1]))
            texp_prev = tex
            P.wait("dve", tex, prev_ph2)
            tmk = P.op("dve", lambda E_, ki=ki, qsl=qsl: E_.tensor_scalar(
                MB[:, :], ctx.qg[:, qsl], ctx.kg[:, ki:ki + 1], None, ALU.is_ge))
            P.wait("dve", tmk)
            tmm = P.op("dve", lambda E_, ki=ki: E_.tensor_tensor(
                EE[ki][:, :], EX[:, :], MB[:, :], op=ALU.mult))
            tmask_prev = tmm
            P.wait("pe", tmm, prev_recip)
            tdn = P.op("pe", lambda E_, ki=ki: E_.matmul(
                ctx.pC[0:1, 0:512], ctx.ones_col[:, 0:1], EE[ki][:, :],
                start=(ki == 0), stop=(ki == 31)))
            tden = tdn
        P.wait("dve", tden)
        td = P.op("dve", lambda E_: E_.reciprocal(ctx.ROWS[0:1, 0:512], ctx.pC[0:1, 0:512]))
        P.wait("dve", td)
        td = P.op("dve", lambda E_: E_.tensor_copy(ctx.ROWB[0:1, 512:1024], ctx.ROWS[0:1, 0:512]))
        P.wait("dve", td)
        td2 = P.settle("dve")
        prev_recip = td2
        P.wait("pe", td2)
        tbc2 = P.op("pe", lambda E_: E_.matmul(ctx.pD[:, 0:512], ctx.ones_row[0:1, :],
                                               ctx.ROWB[0:1, 512:1024], start=True, stop=True))
        P.wait("act", tbc2)
        tnb = P.op("act", lambda E_: E_.copy(ctx.SF[:, 0:512], ctx.pD[:, 0:512]))
        pslots = [ctx.pA[:, 0:512], ctx.pA[:, 512:1024], ctx.pB[:, 0:512], ctx.pB[:, 512:1024],
                  ctx.pC[:, 0:512], ctx.pC[:, 512:1024], ctx.pD[:, 0:512], ctx.pD[:, 512:1024]]
        tpv = {}
        for ki in range(32):
            sl_ = ki % 2
            P.wait("sp", kuse.get(("v", sl_)), c2)
            dv = P.dma("sp", VT[sl_].rearrange("p (c k) -> p c k", c=8) if False else VT[sl_],
                       Vg[ki * 128:(ki + 1) * 128, :])
            P.wait("pe", dv)
            if ki < 2:
                P.wait("pe", tnb, texp_prev)
            for mc in range(NCH):
                tp = P.op("pe", lambda E_, ki=ki, mc=mc, sl_=sl_, pslots=pslots: E_.matmul(
                    pslots[mc], VT[sl_][:, mc * 128:(mc + 1) * 128], EE[ki][:, :],
                    start=(ki == 0), stop=(ki == 31)),
                    sem=None if mc < NCH - 1 else "pe")
            kuse[("v", sl_)] = tp
            tpv[ki] = tp
        P.wait("dve", tpv[31], tpv[30])
        tlast = None
        for mc in range(NCH):
            P.wait("dve", tlast)
            t1 = P.op("dve", lambda E_, mc=mc, pslots=pslots: E_.tensor_tensor(
                ctx.SF[:, 512:1024], pslots[mc], ctx.SF[:, 0:512], op=ALU.mult))
            P.wait("dve", t1)
            t2 = P.op("dve", lambda E_, mc=mc, qsl=qsl: E_.tensor_tensor(
                ctx.X[mc][:, qsl], ctx.X[mc][:, qsl], ctx.SF[:, 512:1024], op=ALU.add))
            tlast = t2
        prev_ph2 = tlast
    P.barrier()
    return tlast


def emit_performer(ctx: Ctx, tag, wqkv_ap, bqkv_ap, wout_ap, bout_ap, omega_ap):
    P = ctx.P
    STR, BIG = ctx.STR, ctx.BIG
    H, dh, M = 16, 64, 256

    qkvd = P.dram(f"{tag}_qkv", [3072, TPC], BF16).ap()
    kvd = P.dram(f"{tag}_kv", [H * 256, 65], F32).ap()
    kvs = P.dram(f"{tag}_kvs", [H * 256, 65], F32).ap()

    bqkvt = ctx.BT[:, 0:24]
    tb = P.dma("pool", bqkvt, bqkv_ap.rearrange("(c p) -> p c", p=128),
               allow_slow_non_contiguous=True)
    om = STR[0:65, 16384:16640]
    tom = P.dma("pool", om[0:64, :], omega_ap[:, :])
    P.wait("dve", tom)
    tom2 = P.op("dve", lambda E_: E_.memset(om[64:65, :], 1.0))
    P.wait("dve", tom2)
    tom3 = P.settle("dve")

    for grp in range(3):
        wtok, wch = _load_w_direct(ctx, wqkv_ap[:, grp * 1024:(grp + 1) * 1024], f"{tag}w{grp}")
        P.wait("pe", tb)
        tq, _ = emit_fm_proj(ctx, wch, wtok, list(range(NCH)),
                             bias_col=bqkvt[:, grp * 8:(grp + 1) * 8],
                             dram_out=qkvd, dram_row0=grp * 1024)
        P.barrier()

    if MHA_STOP == "pqkv":
        P.barrier()
        return (qkvd, kvd, kvs)
    kh = STR[0:64, 0:2048]
    k2 = STR[0:64, 2048:4096]
    kp = STR[:, 4096:8192]
    vh = STR[0:64, 8192:10240]
    vt = STR[:, 10240:11280]
    vtv = STR[:, 10240:11280].rearrange("p (t w) -> p t w", w=65) if False else None
    prev_kv = None
    for h in range(H):
        dk = P.dma("sp", kh[:, :], qkvd[1024 + h * 64:1024 + (h + 1) * 64, :])
        dv = P.dma("sp", vh[:, :], qkvd[2048 + h * 64:2048 + (h + 1) * 64, :])
        P.wait("pe", prev_kv, dv)
        P.wait("act", dk, prev_kv)
        tk2 = P.op("act", lambda E_: E_.activation(k2[:, :], kh[:, :], AF.Square))
        P.wait("pe", tk2, tom3)
        for tt in range(NT):
            tn = P.op("pe", lambda E_, tt=tt: E_.matmul(
                ctx.pD[0:128, tt:tt + 1] if False else ctx.pC[:, tt:tt + 1],
                k2[:, tt * 128:(tt + 1) * 128], ctx.ones_col[0:64, 0:1],
                start=True, stop=True),
                sem=None if tt < NT - 1 else "pe")
        P.wait("dve", tn)
        tsc = P.op("dve", lambda E_: E_.tensor_scalar(ctx.SF2[:, 0:16], ctx.pC[:, 0:16],
                                                      -0.5, None, ALU.mult))
        P.wait("dve", tsc)
        tsc2 = P.settle("dve")
        prev_exp = None
        for tt in range(NT):
            ps = ctx.pA if tt % 2 == 0 else ctx.pB
            P.wait("pe", prev_exp, tsc2)
            tmm = P.op("pe", lambda E_, tt=tt, ps=ps: E_.matmul(
                ps[:, 0:256], kh[:, tt * 128:(tt + 1) * 128], om[0:64, :],
                start=True, stop=True))
            P.wait("act", tmm)
            tex = P.op("act", lambda E_, tt=tt, ps=ps: E_.activation(
                kp[:, tt * 256:(tt + 1) * 256], ps[:, 0:256], AF.Exp,
                bias=ctx.SF2[:, tt:tt + 1]))
            prev_exp = tex
        prev_vt = None
        for tt in range(NT):
            P.wait("pe", prev_vt)
            tvt = P.op("pe", lambda E_, tt=tt: E_.matmul(
                ctx.pC[:, 256:384], vh[:, tt * 128:(tt + 1) * 128], ctx.ident[0:64, :],
                start=True, stop=True))
            P.wait("dve", tvt)
            tcv = P.op("dve", lambda E_: E_.tensor_copy(vt[:, 0:64], ctx.pC[:, 256:320]))
            P.wait("dve", tcv)
            tcv2 = P.op("dve", lambda E_: E_.memset(vt[:, 64:65], 1.0))
            P.wait("dve", tcv2)
            tcv3 = P.settle("dve")
            P.wait("pe", tcv3, prev_exp)
            for mc in range(2):
                tkv = P.op("pe", lambda E_, tt=tt, mc=mc: E_.matmul(
                    ctx.pD[:, mc * 512:mc * 512 + 65],
                    kp[:, tt * 256 + mc * 128:tt * 256 + (mc + 1) * 128],
                    vt[:, 0:65], start=(tt == 0), stop=(tt == NT - 1)),
                    sem=None if (mc < 1 or tt < NT - 1) else "pe")
            prev_vt = tkv if tt == NT - 1 else tvt
        P.wait("act", tkv)
        te1 = P.op("act", lambda E_: E_.copy(ctx.SF[:, 0:65], ctx.pD[:, 0:65]))
        P.wait("act", te1)
        te2 = P.op("act", lambda E_: E_.copy(ctx.SF[:, 512:577], ctx.pD[:, 512:577]))
        P.wait("sp", te2)
        s1 = P.dma("sp", kvd[h * 256:h * 256 + 128, :], ctx.SF[:, 0:65])
        s2 = P.dma("sp", kvd[h * 256 + 128:h * 256 + 256, :], ctx.SF[:, 512:577])
        prev_kv = s2
    P.barrier()

    if MHA_STOP == "pkv":
        P.barrier()
        c1 = P.cc("AllReduce", ALU.add, [kvd.tensor.ap().opt()], [kvs.tensor.ap().opt()], ctx.pairs)
        P.wait("pool", c1)
        return (qkvd, kvd, kvs)
    c1 = P.cc("AllReduce", ALU.add, [kvd.tensor.ap().opt()], [kvs.tensor.ap().opt()], ctx.pairs)
    P.wait("pool", c1)

    qa = STR[0:65, 0:2048]
    q2 = STR[0:64, 2048:4096]
    qp = STR[:, 4096:8192]
    kvb = STR[:, 16640:16770]
    O_all = [BIG[:, c * 2048:(c + 1) * 2048] for c in range(NCH)]
    prev_num = None
    prev_nb = None
    for h in range(H):
        P.wait("sp", prev_num)
        dq = P.dma("sp", qa[0:64, :], qkvd[h * 64:(h + 1) * 64, :])
        P.wait("pool", prev_num)
        dkv = P.dma("pool", kvb.rearrange("p (c w) -> p c w", c=2),
                    kvs[h * 256:(h + 1) * 256, :].rearrange("(c p) w -> p c w", p=128))
        P.wait("act", dq, prev_num)
        tq2 = P.op("act", lambda E_: E_.activation(q2[:, :], qa[0:64, :], AF.Square))
        P.wait("pe", tq2, prev_nb)
        for j in range(4):
            prow = ctx.pC[0:1, :] if j < 2 else ctx.pC[32:33, :]
            po = (j % 2) * 512
            tqn = P.op("pe", lambda E_, j=j, prow=prow, po=po: E_.matmul(
                prow[0:1, po:po + 512], ctx.ones_col[0:64, 0:1],
                q2[:, j * 512:(j + 1) * 512], start=True, stop=True),
                sem=None if j < 3 else "pe")
        P.wait("dve", tqn)
        tr1 = P.op("dve", lambda E_: E_.tensor_scalar(ctx.ROWS[0:1, 0:1024], ctx.pC[0:1, 0:1024],
                                                      -0.5, None, ALU.mult))
        P.wait("dve", tr1)
        tr2 = P.op("dve", lambda E_: E_.tensor_copy(qa[64:65, 0:1024], ctx.ROWS[0:1, 0:1024]))
        P.wait("dve", tr2)
        tr3 = P.op("dve", lambda E_: E_.tensor_scalar(ctx.ROWS[0:1, 0:1024], ctx.pC[32:33, 0:1024],
                                                      -0.5, None, ALU.mult))
        P.wait("dve", tr3)
        tr4 = P.op("dve", lambda E_: E_.tensor_copy(qa[64:65, 1024:2048], ctx.ROWS[0:1, 0:1024]))
        P.wait("dve", tr4)
        tr5 = P.settle("dve")
        prev_exq = None
        for mc in range(2):
            P.wait("pe", tr5, prev_exq)
            for j in range(4):
                psl = ctx.pA if j < 2 else ctx.pB
                po = (j % 2) * 512
                tmm = P.op("pe", lambda E_, mc=mc, j=j, psl=psl, po=po: E_.matmul(
                    psl[:, po:po + 512],
                    om[0:65, mc * 128:(mc + 1) * 128],
                    qa[0:65, j * 512:(j + 1) * 512],
                    start=True, stop=True),
                    sem=None if j < 3 else "pe")
            P.wait("act", tmm)
            tex = P.op("act", lambda E_, mc=mc: E_.activation(
                qp[:, mc * 2048:mc * 2048 + 1024], ctx.pA[:, :], AF.Exp))
            P.wait("act", tex)
            tex = P.op("act", lambda E_, mc=mc: E_.activation(
                qp[:, mc * 2048 + 1024:(mc + 1) * 2048], ctx.pB[:, :], AF.Exp))
            prev_exq = tex
        P.wait("pe", dkv, prev_exq)
        for mc in range(2):
            for j in range(4):
                tnm = P.op("pe", lambda E_, mc=mc, j=j: E_.matmul(
                    (ctx.pC if j < 2 else ctx.pD)[0:65, (j % 2) * 512:(j % 2) * 512 + 512],
                    kvb[:, mc * 65:(mc + 1) * 65],
                    qp[:, mc * 2048 + j * 512:mc * 2048 + (j + 1) * 512],
                    start=(mc == 0), stop=(mc == 1)),
                    sem=None if (mc < 1 or j < 3) else "pe")
        prev_num = tnm
        P.wait("dve", tnm)
        td = P.op("dve", lambda E_: E_.tensor_scalar(ctx.ROWS[0:1, 0:1024], ctx.pC[64:65, 0:1024],
                                                     1e-6, None, ALU.add))
        P.wait("dve", td)
        td = P.op("dve", lambda E_: E_.reciprocal(ctx.ROWS[0:1, 0:1024], ctx.ROWS[0:1, 0:1024]))
        P.wait("dve", td)
        td = P.op("dve", lambda E_: E_.tensor_copy(ctx.ROWB[0:1, 0:1024], ctx.ROWS[0:1, 0:1024]))
        P.wait("dve", td)
        td2 = P.op("dve", lambda E_: E_.tensor_scalar(ctx.ROWS[0:1, 0:1024], ctx.pD[64:65, 0:1024],
                                                      1e-6, None, ALU.add))
        P.wait("dve", td2)
        td2 = P.op("dve", lambda E_: E_.reciprocal(ctx.ROWS[0:1, 0:1024], ctx.ROWS[0:1, 0:1024]))
        P.wait("dve", td2)
        td2 = P.op("dve", lambda E_: E_.tensor_copy(ctx.ROWB[0:1, 1024:2048], ctx.ROWS[0:1, 0:1024]))
        P.wait("dve", td2)
        td3 = P.settle("dve")
        P.wait("pe", td3)
        for j in range(4):
            tbc = P.op("pe", lambda E_, j=j: E_.matmul(
                (ctx.pA if j < 2 else ctx.pB)[0:64, (j % 2) * 512:(j % 2) * 512 + 512],
                ctx.ones_row[0:1, 0:64],
                ctx.ROWB[0:1, j * 512:(j + 1) * 512], start=True, stop=True),
                sem=None if j < 3 else "pe")
        P.wait("act", tbc)
        tnb1 = P.op("act", lambda E_: E_.copy(ctx.SF2[0:64, 0:1024], ctx.pA[0:64, :]))
        P.wait("act", tnb1)
        tnb2 = P.op("act", lambda E_: E_.copy(ctx.SF[0:64, 0:1024], ctx.pB[0:64, :]))
        prev_nb = tnb2
        cch, rb = h // 2, 64 * (h % 2)
        P.wait("dve", tnb1, tnb2)
        to1 = P.op("dve", lambda E_, cch=cch, rb=rb: E_.tensor_tensor(
            O_all[cch][rb:rb + 64, 0:1024], ctx.pC[0:64, :], ctx.SF2[0:64, 0:1024], op=ALU.mult))
        P.wait("dve", to1)
        to2 = P.op("dve", lambda E_, cch=cch, rb=rb: E_.tensor_tensor(
            O_all[cch][rb:rb + 64, 1024:2048], ctx.pD[0:64, :], ctx.SF[0:64, 0:1024], op=ALU.mult))
        P.wait("dve", to2)
        prev_num = to2
    P.barrier()

    wtok, wch = _load_w_direct(ctx, wout_ap, tag + "o")
    bo = ctx.BT[:, 0:8]
    tbo = P.dma("pool", bo, bout_ap.rearrange("(c p) -> p c", p=128),
                allow_slow_non_contiguous=True)
    prev_ev = None
    for r in range(NCH):
        P.wait("pe", wtok, tbo, prev_ev)
        for c in range(NCH):
            for j in range(4):
                psl = ctx.pA if j < 2 else ctx.pB
                po = (j % 2) * 512
                tmm = P.op("pe", lambda E_, r=r, c=c, j=j, psl=psl, po=po: E_.matmul(
                    psl[:, po:po + 512], wch[c][:, r * 128:(r + 1) * 128],
                    O_all[c][:, j * 512:(j + 1) * 512],
                    start=(c == 0), stop=(c == NCH - 1)),
                    sem=None if (c < NCH - 1) else "pe")
        P.wait("dve", tmm)
        for j in range(4):
            psl = ctx.pA if j < 2 else ctx.pB
            po = (j % 2) * 512
            tev = P.op("dve", lambda E_, r=r, j=j, psl=psl, po=po: E_.scalar_tensor_tensor(
                ctx.X[r][:, j * 512:(j + 1) * 512], psl[:, po:po + 512], bo[:, r:r + 1],
                ctx.X[r][:, j * 512:(j + 1) * 512], ALU.add, ALU.add))
            P.wait("dve", tev)
        prev_ev = tev
    P.barrier()
    return prev_ev


SUBLAYERS = list(range(8))
GROUPS = [SUBLAYERS]


def build_group(subs, tag):
    P = Prog()
    F = lambda n, shp: P.dram(n, shp, F32, kind="ExternalInput").ap()
    x_in = F("x_in", [1024, TPC])
    ident_in = P.dram("identity_in", [128, 128], BF16, kind="ExternalInput").ap()
    qg_in = F("qgb_in", [128, TPC])
    kg_in = F("kg_in", [128, 32])
    y_out = P.dram("y_out", [1024, TPC], F32, kind="ExternalOutput").ap()

    w = {}
    w["l0aa_wqkv"] = F("L0_attn_a_Wqkv", [1024, 3072])
    w["l0aa_bqkv"] = F("L0_attn_a_bqkv", [3072])
    w["l0aa_wout"] = F("L0_attn_a_Wout", [1024, 1024])
    w["l0aa_bout"] = F("L0_attn_a_bout", [1024])
    w["l0aa_om"] = F("L0_attn_a_omega", [64, 256])
    w["l0ab_wq"] = F("L0_attn_b_Wq", [1024, 1024])
    w["l0ab_wk"] = F("L0_attn_b_Wk", [1024, 1024])
    w["l0ab_wv"] = F("L0_attn_b_Wv", [1024, 1024])
    w["l0f_g"] = F("L0_ffn_gate_w", [16, 1024])
    w["l0f_w1"] = F("L0_ffn_w1", [16 * 1024, 512])
    w["l0f_w3"] = F("L0_ffn_w3", [16 * 1024, 512])
    w["l0f_w2"] = F("L0_ffn_w2", [16 * 512, 1024])
    w["l1aa_wqkv"] = F("L1_attn_a_w_qkv", [1024, 3072])
    w["l1aa_wo"] = F("L1_attn_a_w_out", [1024, 1024])
    w["l1aa_bo"] = F("L1_attn_a_b_out", [1024])
    w["l1aa_g"] = F("L1_attn_a_ln_gamma", [1024])
    w["l1aa_b"] = F("L1_attn_a_ln_beta", [1024])
    w["l1ab_wq"] = F("L1_attn_b_Wq", [1024, 1024])
    w["l1ab_bq"] = F("L1_attn_b_bq", [1024])
    w["l1ab_wk"] = F("L1_attn_b_Wk", [1024, 1024])
    w["l1ab_bk"] = F("L1_attn_b_bk", [1024])
    w["l1ab_wv"] = F("L1_attn_b_Wv", [1024, 1024])
    w["l1ab_bv"] = F("L1_attn_b_bv", [1024])
    w["l1f_g"] = F("L1_ffn_gate_w", [1024, 8])
    w["l1f_wg"] = F("L1_ffn_w_gate", [8 * 1024, 2048])
    w["l1f_wu"] = F("L1_ffn_w_up", [8 * 1024, 2048])
    w["l1f_wd"] = F("L1_ffn_w_down", [8 * 2048, 1024])
    w["l1f_sg"] = F("L1_ffn_ws_gate", [1024, 2048])
    w["l1f_su"] = F("L1_ffn_ws_up", [1024, 2048])
    w["l1f_sd"] = F("L1_ffn_ws_down", [2048, 1024])

    ctx = Ctx(P)
    ctx.init_consts(x_in, ident_in, qg_in, kg_in)
    P.wait("act", ctx.init_tok)
    P.wait("pe", ctx.init_tok)
    P.wait("dve", ctx.init_tok)

    for s in subs:
        emit_ln(ctx)
        P.barrier()
        if s == 0:
            emit_performer(ctx, "p0", w["l0aa_wqkv"], w["l0aa_bqkv"],
                           w["l0aa_wout"], w["l0aa_bout"], w["l0aa_om"])
        elif s in (1, 3):
            emit_moe(ctx, w["l0f_g"], "eh", 16, 512, 4,
                     w["l0f_w1"], w["l0f_w3"], w["l0f_w2"])
        elif s == 2:
            emit_attn_mha(ctx, f"a0_{s}", 16, 64, w["l0ab_wq"], w["l0ab_wk"],
                          w["l0ab_wv"], transposed=True)
        elif s == 4:
            emit_attn_mha(ctx, "a1", 32, 32,
                          w["l1aa_wqkv"][:, 0:1024], w["l1aa_wqkv"][:, 1024:2048],
                          w["l1aa_wqkv"][:, 2048:3072], transposed=False,
                          w_out_ap=w["l1aa_wo"], b_out_ap=w["l1aa_bo"],
                          gamma_ap=w["l1aa_g"], beta_ap=w["l1aa_b"])
        elif s in (5, 7):
            emit_moe(ctx, w["l1f_g"], "he", 8, 2048, 2,
                     w["l1f_wg"], w["l1f_wu"], w["l1f_wd"],
                     shared=(w["l1f_sg"], w["l1f_su"], w["l1f_sd"]), eps20=True)
        elif s == 6:
            emit_attn_causal(ctx, "c0", w["l1ab_wq"], w["l1ab_bq"],
                             w["l1ab_wk"], w["l1ab_bk"], w["l1ab_wv"], w["l1ab_bv"])
        P.barrier()

    st = None
    for c in range(NCH):
        t = P.op("dve", lambda E_, c=c: E_.tensor_copy(ctx.SF[:, :], ctx.X[c][:, :]))
        P.wait("sp", t)
        st = P.dma("sp", y_out[c * 128:(c + 1) * 128, :], ctx.SF[:, :])
        P.wait("dve", st)
    P.wait("sp", st)
    return P.build()


_IDENT = None


def kernel(**inputs):
    global _IDENT
    import numpy as _np
    xf = inputs["L0_attn_a_x"]
    ident = _np.eye(128, dtype=ml_dtypes.bfloat16)
    kg = _np.ascontiguousarray(
        _np.arange(4096, dtype=_np.float32).reshape(32, 128).T)

    base = {
        "L0_attn_a_Wqkv": inputs["L0_attn_a_Wqkv"],
        "L0_attn_a_bqkv": inputs["L0_attn_a_bqkv"],
        "L0_attn_a_Wout": inputs["L0_attn_a_Wout"],
        "L0_attn_a_bout": inputs["L0_attn_a_bout"],
        "L0_attn_a_omega": inputs["L0_attn_a_omega"],
        "L0_attn_b_Wq": inputs["L0_attn_b_Wq"],
        "L0_attn_b_Wk": inputs["L0_attn_b_Wk"],
        "L0_attn_b_Wv": inputs["L0_attn_b_Wv"],
        "L0_ffn_gate_w": inputs["L0_ffn_gate_w"],
        "L0_ffn_w1": _np.ascontiguousarray(inputs["L0_ffn_w1"].reshape(16 * 1024, 512)),
        "L0_ffn_w3": _np.ascontiguousarray(inputs["L0_ffn_w3"].reshape(16 * 1024, 512)),
        "L0_ffn_w2": _np.ascontiguousarray(inputs["L0_ffn_w2"].reshape(16 * 512, 1024)),
        "L1_attn_a_w_qkv": inputs["L1_attn_a_w_qkv"],
        "L1_attn_a_w_out": inputs["L1_attn_a_w_out"],
        "L1_attn_a_b_out": inputs["L1_attn_a_b_out"],
        "L1_attn_a_ln_gamma": inputs["L1_attn_a_ln_gamma"],
        "L1_attn_a_ln_beta": inputs["L1_attn_a_ln_beta"],
        "L1_attn_b_Wq": inputs["L1_attn_b_Wq"],
        "L1_attn_b_bq": inputs["L1_attn_b_bq"],
        "L1_attn_b_Wk": inputs["L1_attn_b_Wk"],
        "L1_attn_b_bk": inputs["L1_attn_b_bk"],
        "L1_attn_b_Wv": inputs["L1_attn_b_Wv"],
        "L1_attn_b_bv": inputs["L1_attn_b_bv"],
        "L1_ffn_gate_w": inputs["L1_ffn_gate_w"],
        "L1_ffn_w_gate": _np.ascontiguousarray(inputs["L1_ffn_w_gate"].reshape(8 * 1024, 2048)),
        "L1_ffn_w_up": _np.ascontiguousarray(inputs["L1_ffn_w_up"].reshape(8 * 1024, 2048)),
        "L1_ffn_w_down": _np.ascontiguousarray(inputs["L1_ffn_w_down"].reshape(8 * 2048, 1024)),
        "L1_ffn_ws_gate": inputs["L1_ffn_ws_gate"],
        "L1_ffn_ws_up": inputs["L1_ffn_ws_up"],
        "L1_ffn_ws_down": inputs["L1_ffn_ws_down"],
        "identity_in": ident,
        "kg_in": kg,
    }

    xcur = [None] * CORES
    for c in range(CORES):
        b, h = c // 2, c % 2
        xcur[c] = _np.ascontiguousarray(xf[b, h * TPC:(h + 1) * TPC, :].T)

    for gi, subs in enumerate(GROUPS):
        nc = build_group(subs, f"g{gi}")
        in_maps = []
        for c in range(CORES):
            b, h = c // 2, c % 2
            im = dict(base)
            im["x_in"] = xcur[c]
            im["qgb_in"] = _np.tile(
                _np.arange(h * TPC, (h + 1) * TPC, dtype=_np.float32), (128, 1))
            in_maps.append(im)
        res = run_bass_kernel_spmd(nc, in_maps, core_ids=list(range(CORES)))
        for c in range(CORES):
            xcur[c] = _np.ascontiguousarray(res.results[c]["y_out"])

    out = _np.empty((B, N, D), _np.float32)
    for c in range(CORES):
        b, h = c // 2, c % 2
        out[b, h * TPC:(h + 1) * TPC, :] = xcur[c].T
    return out


# version 05 (final)
# speedup vs baseline: 1.5556x; 1.0969x over previous
import numpy as np
import ml_dtypes
import concourse.bass as bass
import concourse.mybir as mybir
from concourse.bass_utils import run_bass_kernel_spmd

F32 = mybir.dt.float32
BF16 = mybir.dt.bfloat16
AF = mybir.ActivationFunctionType
ALU = mybir.AluOpType
AX = mybir.AxisListType.X

B, N, D = 4, 4096, 1024
TPC = 2048
NCH = 8
NT = TPC // 128
CORES = 8

ENG_ATTRS = {"pe": "tensor", "act": "scalar", "dve": "vector", "sp": "sync", "pool": "gpsimd"}
ENG_SEM = {"pe": ("pe", 1), "act": ("act", 1), "dve": ("dve", 1),
           "sp": ("spd", 16), "pool": ("poold", 16)}


class Prog:

    def __init__(self):
        self.nc = bass.Bass("TRN2", target_bir_lowering=False)
        self.q = {e: [] for e in ENG_ATTRS}
        self.counts = {}
        self.waited = {e: {} for e in ENG_ATTRS}
        self._sem_names = []
        self._ctx = []

    def sbuf(self, name, shape, dtype):
        g = self.nc.sbuf_tensor(name, shape, dtype)
        t = g.__enter__()
        self._ctx.append(g)
        return t

    def psum(self, name, shape, dtype=F32):
        g = self.nc.psum_tensor(name, shape, dtype)
        t = g.__enter__()
        self._ctx.append(g)
        return t

    def dram(self, name, shape, dtype, kind="Internal"):
        return self.nc.dram_tensor(name, shape, dtype, kind=kind)

    def _use_sem(self, name):
        if name not in self.counts:
            self.counts[name] = 0
            self._sem_names.append(name)

    def wait(self, eng, *tokens):
        for t in tokens:
            if t is None:
                continue
            sem, val = t
            if val <= 0:
                continue
            self._use_sem(sem)
            if self.waited[eng].get(sem, 0) >= val:
                continue
            self.waited[eng][sem] = val
            self.q[eng].append(("wait", sem, val))

    def op(self, eng, fn, sem="auto", n=None):
        if sem == "auto":
            sem, dn = ENG_SEM[eng]
            n = dn if n is None else n
        if sem is not None:
            self._use_sem(sem)
            self.counts[sem] += (n if n is not None else 1)
            tok = (sem, self.counts[sem])
        else:
            tok = None
        self.q[eng].append(("op", fn, sem, (n if n is not None else 1)))
        return tok

    def dma(self, eng, out, in_, **kw):
        return self.op(eng, lambda E: E.dma_start(out=out, in_=in_, **kw))

    def settle(self, eng):
        j = self._junk
        if eng == "act":
            return self.op("act", lambda E: E.copy(j[0:1, 0:8], j[0:1, 8:16]))
        if eng == "dve":
            return self.op("dve", lambda E: E.tensor_copy(j[0:1, 0:8], j[0:1, 8:16]))
        raise ValueError(eng)

    def cc(self, kind, op_, ins, outs, groups):
        return self.op("pool", lambda E: E.collective_compute(
            kind, op_, replica_groups=groups, ins=ins, outs=outs), sem="cc", n=1)

    def barrier(self):
        toks = []
        for eng in ENG_ATTRS:
            for item in reversed(self.q[eng]):
                if item[0] == "op" and item[2] is not None:
                    toks.append((item[2], self.counts[item[2]]))
                    break
        for eng in ENG_ATTRS:
            self.wait(eng, *toks)
        return toks

    def build(self):
        self._sems = {}
        sem_ctx = []
        for name in self._sem_names:
            g = self.nc.semaphore(name)
            self._sems[name] = g.__enter__()
            sem_ctx.append(g)
        bg = self.nc.Block()
        block = bg.__enter__()
        for eng, attr in ENG_ATTRS.items():
            items = self.q[eng]
            if not items:
                continue

            def body(E, items=items):
                for item in items:
                    if item[0] == "wait":
                        E.wait_ge(self._sems[item[1]], item[2])
                    else:
                        _, fn, sem, n = item
                        instr = fn(E)
                        if sem is not None:
                            instr.then_inc(self._sems[sem], n)
            getattr(block, attr)(body)
        bg.__exit__(None, None, None)
        for g in reversed(sem_ctx):
            g.__exit__(None, None, None)
        for g in reversed(self._ctx):
            g.__exit__(None, None, None)
        return self.nc


class Ctx:
    def __init__(self, P: Prog):
        self.P = P
        self.X = [P.sbuf(f"X{c}", [128, TPC], F32) for c in range(NCH)]
        self.XN = [P.sbuf(f"XN{c}", [128, TPC], BF16) for c in range(NCH)]
        self.BIG = P.sbuf("BIG", [128, 16896], BF16)
        self.STR = P.sbuf("STR", [128, 16896], BF16)
        self.SF = P.sbuf("SF", [128, 2048], F32)
        self.SF2 = P.sbuf("SF2", [128, 1024], F32)
        self.ROWS = P.sbuf("ROWS", [1, 2048], F32)
        self.ROWB = P.sbuf("ROWB", [1, 2048], BF16)
        self.ident = P.sbuf("identsb", [128, 128], BF16)
        self.ones_col = P.sbuf("ones_col", [128, 1], BF16)
        self.ones_row = P.sbuf("ones_row", [1, 128], BF16)
        self.qg = P.sbuf("qgsb", [128, TPC], F32)
        self.kg = P.sbuf("kgsb", [128, 32], F32)
        self.bias128 = P.sbuf("bias128", [128, 4], F32)
        P._junk = P.sbuf("junk", [1, 16], F32)
        self.BT = P.sbuf("BT", [128, 24], F32)
        self.pAB = P.psum("pAB", [128, 2048])
        self.pCD = P.psum("pCD", [128, 2048])
        self.pA = self.pAB.ap()[:, 0:1024]
        self.pB = self.pAB.ap()[:, 1024:2048]
        self.pC = self.pCD.ap()[:, 0:1024]
        self.pD = self.pCD.ap()[:, 1024:2048]
        self.pairs = [[0, 1], [2, 3], [4, 5], [6, 7]]

    def init_consts(self, x_in, ident_in, qg_in, kg_in):
        P = self.P
        for c in range(NCH):
            tx = P.dma("sp", self.X[c][:, :], x_in[c * 128:(c + 1) * 128, :])
        ti = P.dma("sp", self.ident[:, :], ident_in[:, :])
        tq = P.dma("sp", self.qg[:, :], qg_in[:, :])
        tk = P.dma("sp", self.kg[:, :], kg_in[:, :])
        t2 = P.op("dve", lambda E: E.memset(self.ones_col[:, :], 1.0))
        t2 = P.op("dve", lambda E: E.memset(self.ones_row[:, :], 1.0))
        self.init_tok = tk
        self.init_dve = t2


DEBUG_BARRIER = False
LN_HALVES = 2
MOE_DBG = False
MHA_STOP = None


def _dbg_bar(P):
    if DEBUG_BARRIER:
        P.barrier()


def emit_ln(ctx: Ctx, eps=1e-5, src=None, dst=None, gamma=None, beta=None,
            resid=False):
    P = ctx.P
    X = src or ctx.X
    XN = dst or ctx.XN
    src_bf = X is not ctx.X
    pD = ctx.pD
    x2b = ctx.STR
    last = None
    stats_tok = None
    for half in range(LN_HALVES):
        sl = slice(half * 1024, (half + 1) * 1024)
        if stats_tok is not None:
            P.wait("act", stats_tok)
        copy_toks, sq_toks = [], []
        for c in range(NCH):
            if not src_bf:
                tcopy = P.op("act", lambda E, c=c, sl=sl: E.copy(XN[c][:, sl], X[c][:, sl]))
            else:
                tcopy = None
            tsq = P.op("act", lambda E, c=c, sl=sl: E.activation(
                x2b[:, c * 2048: c * 2048 + 1024], X[c][:, sl], AF.Square))
            copy_toks.append(tcopy)
            sq_toks.append(tsq)
        if last is not None:
            P.wait("pe", last)
        for c in range(NCH):
            P.wait("pe", copy_toks[c], sq_toks[c])
            srcc = XN[c] if not src_bf else X[c]
            for j in range(2):
                jsl = slice(j * 512, (j + 1) * 512)
                P.op("pe", lambda E, c=c, j=j, jsl=jsl, srcc=srcc, sl=sl: E.matmul(
                    pD[0:1, jsl], ctx.ones_col[:, 0:1],
                    srcc[:, sl][:, jsl], start=(c == 0), stop=(c == NCH - 1)),
                    sem=None)
                tok = P.op("pe", lambda E, c=c, j=j, jsl=jsl: E.matmul(
                    pD[32:33, jsl], ctx.ones_col[:, 0:1],
                    x2b[:, c * 2048: c * 2048 + 1024][:, jsl],
                    start=(c == 0), stop=(c == NCH - 1)))
        stats_tok = tok
        _dbg_bar(P)
        R, RB = ctx.ROWS, ctx.ROWB
        m = R[0:1, 0:1024]
        var = R[0:1, 1024:2048]
        P.wait("dve", stats_tok)
        t = P.op("dve", lambda E, m=m: E.tensor_scalar(m, pD[0:1, 0:1024], 1.0 / D, None, ALU.mult))
        P.wait("dve", t)
        t = P.op("dve", lambda E, m=m, var=var: E.tensor_tensor(var, m, m, op=ALU.mult))
        P.wait("dve", t)
        t = P.op("dve", lambda E, var=var: E.scalar_tensor_tensor(
            var, pD[32:33, 0:1024], 1.0 / D, var, ALU.mult, ALU.subtract))
        P.wait("dve", t)
        t = P.op("dve", lambda E, var=var: E.tensor_scalar(var, var, float(eps), None, ALU.add))
        P.wait("act", t)
        t = P.op("act", lambda E, var=var: E.activation(var, var, AF.Sqrt))
        P.wait("dve", t)
        t = P.op("dve", lambda E, var=var: E.reciprocal(var, var))
        P.wait("dve", t)
        t2 = P.op("dve", lambda E, RB=RB, var=var: E.tensor_copy(RB[0:1, 0:1024], var))
        P.wait("dve", t2)
        t1 = P.op("dve", lambda E, m=m, var=var: E.tensor_tensor(m, m, var, op=ALU.mult))
        P.wait("dve", t1)
        t3 = P.op("dve", lambda E, RB=RB, m=m: E.tensor_copy(RB[0:1, 1024:2048], m))
        P.wait("dve", t3)
        t3s = P.settle("dve")
        _dbg_bar(P)
        P.wait("pe", t2, t3, t3s, ctx.init_dve)
        for j in range(2):
            jsl = slice(j * 512, (j + 1) * 512)
            ta = P.op("pe", lambda E, jsl=jsl, RB=RB: E.matmul(
                ctx.pA[:, jsl], ctx.ones_row[0:1, :], RB[0:1, jsl],
                start=True, stop=True))
            tb = P.op("pe", lambda E, jsl=jsl, j=j, RB=RB: E.matmul(
                ctx.pB[:, jsl], ctx.ones_row[0:1, :], RB[0:1, 1024 + j * 512:1024 + (j + 1) * 512],
                start=True, stop=True))
        _dbg_bar(P)
        P.wait("dve", ta, tb)
        for c in range(NCH):
            t = P.op("dve", lambda E, c=c, sl=sl: E.tensor_tensor(
                ctx.SF[:, 0:1024], X[c][:, sl], ctx.pA[:, :], op=ALU.mult))
            P.wait("dve", t)
            if gamma is None:
                t = P.op("dve", lambda E, c=c, sl=sl: E.tensor_tensor(
                    XN[c][:, sl], ctx.SF[:, 0:1024], ctx.pB[:, :], op=ALU.subtract))
            else:
                t = P.op("dve", lambda E, c=c: E.tensor_tensor(
                    ctx.SF2[:, 0:1024], ctx.SF[:, 0:1024], ctx.pB[:, :], op=ALU.subtract))
                P.wait("dve", t)
                t = P.op("dve", lambda E, c=c, sl=sl: E.tensor_scalar(
                    XN[c][:, sl], ctx.SF2[:, 0:1024], gamma[:, c:c + 1], beta[:, c:c + 1],
                    ALU.mult, ALU.add))
            P.wait("dve", t)
        last = t
        _dbg_bar(P)
    return last


def emit_moe(ctx: Ctx, gate_ap, gate_layout, E, INTER, topk,
             w1_ap, w3_ap, w2_ap, shared=None, eps20=False):
    P = ctx.P
    XN, X = ctx.XN, ctx.X
    pA, pB, pC, pD = ctx.pA, ctx.pB, ctx.pC, ctx.pD
    SF, SF2 = ctx.SF, ctx.SF2
    NIC = INTER // 128
    W1B = [0, 1024, 2048]
    W3B = [4096, 5120, 6144]
    W2B = [8192, 10240]
    SCO = 12288
    GWT = 12800
    GWR = 14000
    STR = ctx.STR

    gwt = [STR[:, GWT + c * E: GWT + (c + 1) * E] for c in range(NCH)]
    if gate_layout == "he":
        tg = P.dma("pool", STR[:, GWT:GWT + NCH * E].rearrange("p (c e) -> p c e", c=NCH),
                   gate_ap.rearrange("(c p) e -> p c e", p=128))
        gw_tok = tg
    else:
        traw = P.dma("pool", STR[0:16, GWR:GWR + 1024], gate_ap[:, :])
        P.wait("pe", traw, ctx.init_tok)
        tcp = None
        for c in range(NCH):
            P.wait("pe", tcp)
            t = P.op("pe", lambda E_, c=c: E_.matmul(
                pD[:, 0:128], STR[0:16, GWR + c * 128:GWR + (c + 1) * 128],
                ctx.ident[0:16, :], start=True, stop=True))
            P.wait("act", t)
            tcp = P.op("act", lambda E_, c=c: E_.copy(gwt[c][:, :], pD[:, 0:E]))
        gw_tok = P.settle("act")

    if MOE_DBG:
        P.wait("dve", gw_tok)
        tdbg = P.op("dve", lambda E_: E_.tensor_copy(SF[:, 1792:1792 + NCH * E], STR[:, GWT:GWT + NCH * E]))
        P.wait("dve", tdbg)
    P.wait("pe", gw_tok)
    for i in range(NT):
        for c in range(NCH):
            tl = P.op("pe", lambda E_, i=i, c=c: E_.matmul(
                pC[:, i * E:(i + 1) * E], XN[c][:, i * 128:(i + 1) * 128],
                gwt[c][:, :], start=(c == 0), stop=(c == NCH - 1)),
                sem=None if c < NCH - 1 else "pe")
    logits_tok = tl
    if MOE_DBG:
        P.wait("dve", logits_tok)
        tdb = P.op("dve", lambda E_: E_.tensor_copy(SF[:, 1536:1536 + 16 * E], pC[:, 0:16 * E]))
        P.wait("dve", tdb)

    ex = SF[:, 0:16 * E]
    work = SF[:, 512:512 + 16 * E]
    msk = SF[:, 1024:1024 + 16 * E]
    mx = SF2[:, 0:16]
    Ssum = SF2[:, 16:32]
    den = SF2[:, 48:64]

    def v3(ap_):
        return ap_.rearrange("p (t e) -> p t e", e=E)

    def b3(ap16):
        return ap16.rearrange("p (t o) -> p t o", o=1).broadcast_to([128, 16, E])

    P.wait("dve", logits_tok)
    t = P.op("dve", lambda E_: E_.reduce_max(mx.rearrange("p (t o) -> p t o", o=1),
                                             v3(pC[:, 0:16 * E]), axis=AX))
    P.wait("dve", t)
    t = P.op("dve", lambda E_: E_.tensor_tensor(v3(ex), v3(pC[:, 0:16 * E]), b3(mx), op=ALU.subtract))
    P.wait("act", t)
    t = P.op("act", lambda E_: E_.activation(ex, ex, AF.Exp))
    P.wait("dve", t)
    t = P.op("dve", lambda E_: E_.reduce_sum(Ssum.rearrange("p (t o) -> p t o", o=1), v3(ex), axis=AX))
    P.wait("dve", t)
    t = P.op("dve", lambda E_: E_.tensor_copy(work, ex))
    for r in range(topk - 1):
        P.wait("dve", t)
        t = P.op("dve", lambda E_: E_.reduce_max(mx.rearrange("p (t o) -> p t o", o=1), v3(work), axis=AX))
        P.wait("dve", t)
        t = P.op("dve", lambda E_: E_.tensor_tensor(v3(msk), v3(work), b3(mx), op=ALU.is_lt))
        P.wait("dve", t)
        t = P.op("dve", lambda E_: E_.tensor_tensor(work, work, msk, op=ALU.mult))
    P.wait("dve", t)
    t = P.op("dve", lambda E_: E_.reduce_max(mx.rearrange("p (t o) -> p t o", o=1), v3(work), axis=AX))
    P.wait("dve", t)
    t = P.op("dve", lambda E_: E_.tensor_tensor(v3(msk), v3(ex), b3(mx), op=ALU.is_ge))
    P.wait("dve", t)
    t = P.op("dve", lambda E_: E_.tensor_tensor(ex, ex, msk, op=ALU.mult))
    P.wait("dve", t)
    t = P.op("dve", lambda E_: E_.reduce_sum(den.rearrange("p (t o) -> p t o", o=1), v3(ex), axis=AX))
    P.wait("dve", t)
    if eps20:
        t = P.op("dve", lambda E_: E_.scalar_tensor_tensor(den, Ssum, 1e-20, den, ALU.mult, ALU.add))
        P.wait("dve", t)
    t = P.op("dve", lambda E_: E_.reciprocal(den, den))
    P.wait("dve", t)
    t = P.op("dve", lambda E_: E_.tensor_tensor(v3(STR[:, SCO:SCO + 16 * E]), v3(ex), b3(den), op=ALU.mult))
    scores_tok = t

    experts = [(e, w1_ap, w3_ap, w2_ap, True) for e in range(E)]
    if shared is not None:
        experts.append((0, shared[0], shared[1], shared[2], False))

    dma_tok = {}
    use_tok = {}
    halfsl = [slice(0, 1024), slice(1024, 2048)]
    prev_silu = None
    prev_umul = None
    prev_tact = None
    prev_down = None
    prev_xadd = {}
    bcast_read_tok = None

    for half in range(2):
        hs = halfsl[half]
        for ei, (e, a1, a3, a2, scaled) in enumerate(experts):
            slot = ei % 2
            if scaled:
                tts = []
                P.wait("pe", scores_tok)
                if bcast_read_tok is not None:
                    P.wait("pe", bcast_read_tok)
                for il in range(8):
                    i = half * 8 + il
                    tt = P.op("pe", lambda E_, i=i, il=il, e=e: E_.matmul(
                        pD[0:1, il * 128:(il + 1) * 128],
                        STR[:, SCO + i * E + e: SCO + i * E + e + 1],
                        ctx.ident[:, :], start=True, stop=True))
                    tts.append(tt)
                P.wait("dve", tts[-1])
                trow = P.op("dve", lambda E_: E_.tensor_copy(ctx.ROWB[0:1, 0:1024], pD[0:1, 0:1024]))
                P.wait("dve", trow)
                trow = P.settle("dve")
                P.wait("pe", trow)
                for j in range(2):
                    tb = P.op("pe", lambda E_, j=j: E_.matmul(
                        pD[:, j * 512:(j + 1) * 512], ctx.ones_row[0:1, :],
                        ctx.ROWB[0:1, j * 512:(j + 1) * 512], start=True, stop=True))
                bcast_tok = tb
            else:
                bcast_tok = None

            act_toks = []

            def _issue_w13(icx):
                sl2 = icx % 3
                for kind, base, wap in (("w1", W1B, a1), ("w3", W3B, a3)):
                    key = (kind, sl2)
                    P.wait("pool", use_tok.get(key))
                    src = wap[e * 1024:(e + 1) * 1024, icx * 128:(icx + 1) * 128] if scaled else \
                        wap[:, icx * 128:(icx + 1) * 128]
                    dma_tok[key] = P.dma("pool", STR[:, base[sl2]:base[sl2] + 1024]
                                         .rearrange("p (c i) -> p c i", c=NCH),
                                         src.rearrange("(c p) i -> p c i", p=128))

            _issue_w13(0)
            if NIC > 1:
                _issue_w13(1)
            for ic in range(NIC):
                wslot = ic % 3
                if ic + 2 < NIC:
                    _issue_w13(ic + 2)
                P.wait("pe", dma_tok[("w1", wslot)], dma_tok[("w3", wslot)])
                P.wait("pe", prev_silu)
                P.wait("pe", prev_umul)
                for c in range(NCH):
                    for j in range(2):
                        tg_ = P.op("pe", lambda E_, c=c, j=j, base=W1B[wslot], hs=hs: E_.matmul(
                            pB[:, j * 512:(j + 1) * 512],
                            STR[:, base + c * 128:base + (c + 1) * 128],
                            XN[c][:, hs][:, j * 512:(j + 1) * 512],
                            start=(c == 0), stop=(c == NCH - 1)),
                            sem=None if c < NCH - 1 else "pe")
                for c in range(NCH):
                    for j in range(2):
                        tu_ = P.op("pe", lambda E_, c=c, j=j, base=W3B[wslot], hs=hs: E_.matmul(
                            pC[:, j * 512:(j + 1) * 512],
                            STR[:, base + c * 128:base + (c + 1) * 128],
                            XN[c][:, hs][:, j * 512:(j + 1) * 512],
                            start=(c == 0), stop=(c == NCH - 1)),
                            sem=None if c < NCH - 1 else "pe")
                use_tok[("w1", wslot)] = tg_
                use_tok[("w3", wslot)] = tu_
                P.wait("act", tg_, prev_tact)
                tsil = P.op("act", lambda E_: E_.activation(SF2[:, 0:1024], pB[:, :], AF.Silu))
                prev_silu = tsil
                P.wait("dve", tsil, tu_)
                tum = P.op("dve", lambda E_: E_.tensor_tensor(SF2[:, 0:1024], SF2[:, 0:1024], pC[:, :], op=ALU.mult))
                prev_umul = tum
                P.wait("dve", tum, prev_down)
                if scaled:
                    P.wait("dve", bcast_tok)
                    tact = P.op("dve", lambda E_, ic=ic: E_.tensor_tensor(
                        ctx.BIG[:, ic * 1024:(ic + 1) * 1024], SF2[:, 0:1024], pD[:, :], op=ALU.mult))
                    bcast_read_tok = tact
                else:
                    tact = P.op("dve", lambda E_, ic=ic: E_.tensor_copy(
                        ctx.BIG[:, ic * 1024:(ic + 1) * 1024], SF2[:, 0:1024]))
                P.wait("dve", tact)
                prev_tact = tact
                act_toks.append(tact)

            w2tok = {}

            def _issue_w2(hcx):
                sl2 = hcx % 2
                key = ("w2", sl2)
                P.wait("pool", use_tok.get(key))
                src2 = a2[e * INTER:(e + 1) * INTER, hcx * 128:(hcx + 1) * 128] if scaled else \
                    a2[:, hcx * 128:(hcx + 1) * 128]
                w2tok[hcx] = P.dma("pool", STR[:, W2B[sl2]:W2B[sl2] + NIC * 128]
                                   .rearrange("p (c i) -> p c i", c=NIC),
                                   src2.rearrange("(c p) i -> p c i", p=128))

            _issue_w2(0)
            for hc in range(NCH):
                wslot = hc % 2
                if hc + 1 < NCH:
                    _issue_w2(hc + 1)
                P.wait("pe", w2tok[hc], act_toks[-1])
                P.wait("pe", prev_xadd.get("last"))
                for ic in range(NIC):
                    for j in range(2):
                        td_ = P.op("pe", lambda E_, ic=ic, j=j, base=W2B[wslot]: E_.matmul(
                            pA[:, j * 512:(j + 1) * 512],
                            STR[:, base + ic * 128:base + (ic + 1) * 128],
                            ctx.BIG[:, ic * 1024 + j * 512: ic * 1024 + j * 512 + 512],
                            start=(ic == 0), stop=(ic == NIC - 1)),
                            sem=None if ic < NIC - 1 else "pe")
                use_tok[("w2", wslot)] = td_
                prev_down = td_
                P.wait("dve", td_)
                tx = P.op("dve", lambda E_, hc=hc, hs=hs: E_.tensor_tensor(
                    X[hc][:, hs], X[hc][:, hs], pA[:, :], op=ALU.add))
                prev_xadd["last"] = tx
    return tx


def _load_wT(ctx, w_ap, tag, n=1024):
    P = ctx.P
    STR = ctx.STR
    raw = STR[:, 0:8192]
    wt = STR[:, 8192:16384]
    t_ld = P.dma("pool", raw.rearrange("p (c i) -> p c i", c=8),
                 w_ap.rearrange("(c p) i -> p c i", p=128))
    P.wait("pe", t_ld, ctx.init_tok)
    tcp = None
    for r in range(8):
        P.wait("pe", tcp)
        for c in range(8):
            tt = P.op("pe", lambda E_, r=r, c=c: E_.matmul(
                ctx.pD[:, c * 128:(c + 1) * 128],
                raw.rearrange("p (c i) -> p c i", c=8)[:, c, r * 128:(r + 1) * 128],
                ctx.ident[:, :], start=True, stop=True),
                sem=None if c < 7 else "pe")
        P.wait("act", tt)
        tcp = P.op("act", lambda E_, r=r: E_.copy(wt[:, r * 1024:(r + 1) * 1024], ctx.pD[:, :]))
    tok = P.settle("act")
    chunks = [wt[:, r * 1024:(r + 1) * 1024] for r in range(8)]
    return tok, chunks


def _load_w_direct(ctx, w_ap, tag, dout=1024, off=0):
    P = ctx.P
    n_el = 8 * dout
    buf = ctx.STR[:, off:off + n_el]
    t_ld = P.dma("pool", buf.rearrange("p (c i) -> p c i", c=8),
                 w_ap.rearrange("(c p) i -> p c i", p=128))
    chunks = [buf[:, c * dout:(c + 1) * dout] for c in range(8)]
    return t_ld, chunks


def emit_fm_proj(ctx, w_chunks, w_tok, r_list, scale=None, bias_col=None,
                 dram_out=None, dram_row0=0, norm_units=None, umax=None,
                 norm_accum=False, extra_wait=None, src_chunks=None,
                 dst_chunks=None):
    P = ctx.P
    src = src_chunks or ctx.XN
    stage = ctx.BIG[:, 0:2048]
    sq = ctx.BIG[:, 2048:4096]
    last_dma = None
    tnorm = None
    prev_ev = None
    nr_total = len(r_list)
    for ri, r in enumerate(r_list):
        P.wait("pe", w_tok, extra_wait)
        P.wait("pe", prev_ev)
        for c in range(NCH):
            for j in range(4):
                psl = ctx.pA if j < 2 else ctx.pB
                po = (j % 2) * 512
                tmm = P.op("pe", lambda E_, r=r, c=c, j=j, psl=psl, po=po, src=src: E_.matmul(
                    psl[:, po:po + 512], w_chunks[c][:, r * 128:(r + 1) * 128],
                    src[c][:, j * 512:(j + 1) * 512],
                    start=(c == 0), stop=(c == NCH - 1)),
                    sem=None if (c < NCH - 1) else "pe")
        P.wait("act", tmm, last_dma)
        if dst_chunks is not None:
            stage = dst_chunks[r]
        for j in range(4):
            psl = ctx.pA if j < 2 else ctx.pB
            po = (j % 2) * 512
            if bias_col is not None:
                tev = P.op("act", lambda E_, r=r, j=j, psl=psl, po=po, stage=stage: E_.activation(
                    stage[:, j * 512:(j + 1) * 512], psl[:, po:po + 512], AF.Identity,
                    bias=bias_col[:, r:r + 1], scale=(scale if scale is not None else 1.0)))
            elif scale is not None:
                tev = P.op("act", lambda E_, r=r, j=j, psl=psl, po=po, stage=stage: E_.activation(
                    stage[:, j * 512:(j + 1) * 512], psl[:, po:po + 512], AF.Copy,
                    scale=scale))
            else:
                tev = P.op("act", lambda E_, r=r, j=j, psl=psl, po=po, stage=stage: E_.copy(
                    stage[:, j * 512:(j + 1) * 512], psl[:, po:po + 512]))
        if dram_out is not None:
            P.wait("sp", tev)
            last_dma = P.dma("sp", dram_out[dram_row0 + r * 128:dram_row0 + (r + 1) * 128, :],
                             stage[:, :])
        units = []
        if norm_units is not None:
            units = norm_units.get(r, [])
        if units or norm_accum:
            P.wait("act", tev, tnorm)
            tsq = P.op("act", lambda E_, stage=stage: E_.activation(sq[:, :], stage[:, :], AF.Square))
            if norm_accum:
                units = [(0, 0, 128)] if not units else units
            for (u, base, nrows) in units:
                P.wait("pe", tsq, tnorm)
                st_flag = (ri == 0) if norm_accum else True
                sp_flag = (ri == nr_total - 1) if norm_accum else True
                for j in range(4):
                    prow = ctx.pD[0:1, :] if j < 2 else ctx.pD[32:33, :]
                    po = (j % 2) * 512
                    tpn = P.op("pe", lambda E_, base=base, nrows=nrows, j=j, prow=prow,
                               po=po, st=st_flag, sp=sp_flag: E_.matmul(
                        prow[0:1, po:po + 512], ctx.ones_col[base:base + nrows, 0:1],
                        sq[base:base + nrows, j * 512:(j + 1) * 512],
                        start=st, stop=sp), sem=None if j < 3 else "pe")
                if norm_accum and ri < nr_total - 1:
                    continue
                P.wait("dve", tpn)
                t1 = P.op("dve", lambda E_, u=u: E_.reduce_max(
                    ctx.SF2[0:1, 40:41], ctx.pD[0:1, 0:1024], axis=AX))
                P.wait("dve", t1)
                t2 = P.op("dve", lambda E_, u=u: E_.reduce_max(
                    ctx.SF2[0:1, 41:42], ctx.pD[32:33, 0:1024], axis=AX))
                P.wait("dve", t2)
                t3 = P.op("dve", lambda E_, u=u, umax=umax: E_.tensor_tensor(
                    umax[0:1, u:u + 1], ctx.SF2[0:1, 40:41], ctx.SF2[0:1, 41:42],
                    op=ALU.max))
                P.wait("dve", t3)
                tnorm = t3
        prev_ev = tev
    return last_dma, tnorm


def emit_v_proj(ctx, w_chunks, w_tok, dram_out, dh1=None, bias_row=None,
                scale=None, extra_wait=None):
    P = ctx.P
    stage = ctx.BIG[:, 4096:6144]
    last_dma = None
    prev_ev = None
    for tp in range(NT // 2):
        P.wait("pe", w_tok, extra_wait, prev_ev, last_dma)
        for half in range(2):
            tt = tp * 2 + half
            psl = ctx.pA if half == 0 else ctx.pB
            for c in range(NCH):
                for j in range(2):
                    lastmm = (c == NCH - 1 and j == 1 and bias_row is None)
                    tmm = P.op("pe", lambda E_, tt=tt, c=c, j=j, psl=psl: E_.matmul(
                        psl[:, j * 512:(j + 1) * 512],
                        ctx.XN[c][:, tt * 128:(tt + 1) * 128],
                        w_chunks[c][:, j * 512:(j + 1) * 512],
                        start=(c == 0), stop=(c == NCH - 1 and bias_row is None)),
                        sem="pe" if lastmm else None)
            if bias_row is not None:
                for j in range(2):
                    tmm = P.op("pe", lambda E_, j=j, psl=psl: E_.matmul(
                        psl[:, j * 512:(j + 1) * 512], ctx.ones_row[0:1, :],
                        bias_row[0:1, j * 512:(j + 1) * 512],
                        start=False, stop=True),
                        sem=None if j < 1 else "pe")
        P.wait("act", tmm)
        for half in range(2):
            psl = ctx.pA if half == 0 else ctx.pB
            if scale is not None:
                tev = P.op("act", lambda E_, half=half, psl=psl: E_.activation(
                    stage[:, half * 1024:(half + 1) * 1024], psl[:, :], AF.Copy, scale=scale))
            else:
                tev = P.op("act", lambda E_, half=half, psl=psl: E_.copy(
                    stage[:, half * 1024:(half + 1) * 1024], psl[:, :]))
        prev_ev = tev
        P.wait("sp", tev)
        last_dma = P.dma("sp",
                         dram_out[tp * 256:(tp + 1) * 256, :].rearrange("(h p) e -> p h e", p=128),
                         stage.rearrange("p (h e) -> p h e", h=2))
    return last_dma


def emit_attn_mha(ctx: Ctx, tag, H, dh, wq_ap, wk_ap, wv_ap, transposed,
                  w_out_ap=None, b_out_ap=None, gamma_ap=None, beta_ap=None,
                  core_id_pairs=None):
    P = ctx.P
    STR, BIG = ctx.STR, ctx.BIG
    scale = float(dh) ** -0.5
    U = 16
    G = H // 2
    grows = 2 * dh

    Qd = P.dram(f"{tag}_Qd", [1024, TPC], BF16).ap()
    Kd = P.dram(f"{tag}_Kd", [1024, TPC], BF16).ap()
    Vd = P.dram(f"{tag}_Vd", [TPC, 1024], BF16).ap()
    Kg = P.dram(f"{tag}_Kg", [2048, TPC], BF16).ap()
    Vg = P.dram(f"{tag}_Vg", [2 * TPC, 1024], BF16).ap()
    kmd = P.dram(f"{tag}_kmd", [1, U], F32).ap()
    kmg = P.dram(f"{tag}_kmg", [1, U], F32).ap()

    qmax = ctx.SF2[0:1, 0:U]
    kmax = ctx.SF2[0:1, 32:32 + U]
    norm_units = {r: [(2 * r, 0, 64), (2 * r + 1, 64, 64)] for r in range(NCH)}

    if transposed:
        wtok, wch = _load_wT(ctx, wq_ap, tag + "q")
    else:
        wtok, wch = _load_w_direct(ctx, wq_ap, tag + "q")
    tq, tnq = emit_fm_proj(ctx, wch, wtok, list(range(NCH)), scale=scale,
                           dram_out=Qd, norm_units=norm_units, umax=qmax)
    P.barrier()
    if transposed:
        wtok, wch = _load_wT(ctx, wk_ap, tag + "k")
    else:
        wtok, wch = _load_w_direct(ctx, wk_ap, tag + "k")
    tk, tnk = emit_fm_proj(ctx, wch, wtok, list(range(NCH)),
                           dram_out=Kd, norm_units=norm_units, umax=kmax)
    P.barrier()
    if transposed:
        wtok, wch = _load_wT(ctx, wv_ap, tag + "v")
    else:
        wtok, wch = _load_w_direct(ctx, wv_ap, tag + "v")
    tv = emit_v_proj(ctx, wch, wtok, Vd)
    P.wait("sp", tnk)
    tkm = P.dma("sp", kmd[0:1, :], kmax)
    P.barrier()

    c1 = P.cc("AllGather", ALU.bypass, [Kd.tensor.ap().opt()], [Kg.tensor.ap().opt()], ctx.pairs)
    c2 = P.cc("AllGather", ALU.bypass, [Vd.tensor.ap().opt()], [Vg.tensor.ap().opt()], ctx.pairs)
    c3 = P.cc("AllReduce", ALU.max, [kmd.tensor.ap().opt()], [kmg.tensor.ap().opt()], ctx.pairs)
    P.wait("sp", c3)
    tkg = P.dma("sp", kmax, kmg[0:1, :])

    if MHA_STOP == "proj":
        P.barrier()
        return (Qd, Kd, Vd, Kg, Vg, kmg)
    P.wait("dve", tkg, tnq)
    t = P.op("dve", lambda E_: E_.tensor_tensor(ctx.SF2[0:1, 64:64 + U], qmax, kmax, op=ALU.mult))
    P.wait("act", t)
    t = P.op("act", lambda E_: E_.activation(ctx.SF2[0:1, 64:64 + U], ctx.SF2[0:1, 64:64 + U], AF.Sqrt))
    P.wait("dve", t)
    t = P.op("dve", lambda E_: E_.tensor_scalar(ctx.ROWB[0:1, 0:U], ctx.SF2[0:1, 64:64 + U],
                                                -1.0, None, ALU.mult))
    P.wait("dve", t)
    brow_tok = P.settle("dve")

    KB = STR[:, 0:4096]
    QB = STR[:, 4096:6144]
    W_ = 2 * dh + 2
    VPW = 32 * W_
    VP = STR[:, 6144:6144 + VPW]
    VPv = VP.rearrange("p (ki w) -> p ki w", w=W_)
    EB = STR[:, 10400:14496]

    if w_out_ap is not None:
        O_all = [BIG[:, c * 2048:(c + 1) * 2048] for c in range(NCH)]
        OS = STR[:, 14496:15520]

    ep_toks = []
    prev_core = {}
    exp_war = {}
    for g in range(G):
        if dh == 64:
            ua, ub = 2 * g, 2 * g + 1
        else:
            ua = ub = g
        P.wait("sp", c1, c2)
        P.wait("sp", prev_core.get("st"), prev_core.get("pva"), prev_core.get("pvb"))
        d1 = P.dma("sp", KB[0:grows, 0:2048], Kg[grows * g:grows * (g + 1), :])
        d2 = P.dma("sp", KB[0:grows, 2048:4096], Kg[1024 + grows * g:1024 + grows * (g + 1), :])
        d3 = P.dma("sp", QB[0:grows, :], Qd[grows * g:grows * (g + 1), :])
        d4 = P.dma("sp", VPv[:, :, 0:dh],
                   Vg[:, grows * g:grows * g + dh].rearrange("(ki p) e -> p ki e", p=128))
        d5 = P.dma("sp", VPv[:, :, dh + 1:2 * dh + 1],
                   Vg[:, grows * g + dh:grows * (g + 1)].rearrange("(ki p) e -> p ki e", p=128))
        P.wait("dve", d5)
        tm1 = P.op("dve", lambda E_: E_.memset(VPv[:, :, dh:dh + 1], 1.0))
        P.wait("dve", tm1)
        tm2 = P.op("dve", lambda E_: E_.memset(VPv[:, :, 2 * dh + 1:2 * dh + 2], 1.0))
        P.wait("dve", brow_tok)
        tbm = P.op("dve", lambda E_, ua=ua, ub=ub: E_.tensor_tensor(
            ctx.ROWB[0:1, 20:21], ctx.ROWB[0:1, ua:ua + 1], ctx.ROWB[0:1, ub:ub + 1],
            op=ALU.min))
        P.wait("dve", tbm)
        tbm2 = P.settle("dve")
        P.wait("pe", tbm2, d5, tm2, exp_war.get("a"), exp_war.get("b"), prev_core.get("kb"))
        tb1 = P.op("pe", lambda E_: E_.matmul(
            ctx.pA[:, 0:1], ctx.ones_row[0:1, :], ctx.ROWB[0:1, 20:21],
            start=True, stop=True))
        P.wait("dve", tb1, exp_war.get("a"), exp_war.get("b"))
        tb3 = P.op("dve", lambda E_: E_.tensor_copy(ctx.bias128[:, 0:1], ctx.pA[:, 0:1]))
        P.wait("dve", tb3)
        tb4 = P.settle("dve")

        exp_tok = {"a": None, "b": None}
        pv_tok = {}
        st_tok = {}
        ep = None
        for qc in range(2):
            qsl = slice(qc * 1024, (qc + 1) * 1024)
            P.wait("pe", ep)
            for ki in range(32):
                P.wait("pe", exp_tok["a"], tb4)
                for hi in range(2):
                    rows = slice(dh * hi, dh * (hi + 1))
                    ps = ctx.pA if hi == 0 else ctx.pB
                    for j in range(2):
                        tst = P.op("pe", lambda E_, ki=ki, j=j, rows=rows, ps=ps, qsl=qsl: E_.matmul(
                            ps[:, j * 512:(j + 1) * 512],
                            KB[rows, ki * 128:(ki + 1) * 128],
                            QB[rows, qsl][:, j * 512:(j + 1) * 512],
                            start=True, stop=True),
                            sem=None if (hi < 1 or j < 1) else "pe")
                st_tok["a"] = tst
                eoff = 2048 * (ki % 2)
                P.wait("act", tst, pv_tok.get(("a", ki % 2)), pv_tok.get(("b", ki % 2)), tb4)
                tex = P.op("act", lambda E_, eoff=eoff: E_.activation(
                    EB[:, eoff:eoff + 2048], ctx.pAB.ap()[:, :], AF.Exp,
                    bias=ctx.bias128[:, 0:1]))
                exp_tok["a"] = tex
                exp_war["a"] = tex
                P.wait("pe", tex)
                for hi, hname in ((0, "a"), (1, "b")):
                    pvp = ctx.pC if hi == 0 else ctx.pD
                    voff = (dh + 1) * hi
                    for j in range(2):
                        tpv = P.op("pe", lambda E_, ki=ki, j=j, pvp=pvp, voff=voff, eoff=eoff, hi=hi: E_.matmul(
                            pvp[0:dh + 1, j * 512:(j + 1) * 512],
                            VPv[:, ki, voff:voff + dh + 1],
                            EB[:, eoff + 1024 * hi + j * 512:eoff + 1024 * hi + (j + 1) * 512],
                            start=(ki == 0), stop=(ki == 31)),
                            sem=None if j < 1 else "pe")
                    pv_tok[(hname, ki % 2)] = tpv
            for hi, hname in ((0, "a"), (1, "b")):
                pvp = ctx.pC if hi == 0 else ctx.pD
                ps = ctx.pA if hi == 0 else ctx.pB
                P.wait("dve", pv_tok[(hname, 0)], pv_tok[(hname, 1)])
                td = P.op("dve", lambda E_, pvp=pvp: E_.reciprocal(
                    ctx.ROWS[0:1, 0:1024], pvp[dh:dh + 1, 0:1024]))
                P.wait("dve", td)
                td = P.op("dve", lambda E_: E_.tensor_copy(ctx.ROWB[0:1, 1024:2048],
                                                           ctx.ROWS[0:1, 0:1024]))
                P.wait("dve", td)
                td = P.settle("dve")
                P.wait("pe", td, exp_tok["a"])
                for j in range(2):
                    tbc = P.op("pe", lambda E_, j=j, ps=ps: E_.matmul(
                        ps[:, j * 512:(j + 1) * 512], ctx.ones_row[0:1, :],
                        ctx.ROWB[0:1, 1024 + j * 512:1024 + (j + 1) * 512],
                        start=True, stop=True),
                        sem=None if j < 1 else "pe")
                P.wait("act", tbc)
                tnb = P.op("act", lambda E_, ps=ps: E_.copy(ctx.SF2[:, 0:1024], ps[0:128, :]))
                P.wait("dve", tnb)
                if w_out_ap is None:
                    t1 = P.op("dve", lambda E_, pvp=pvp, hi=hi: E_.tensor_tensor(
                        ctx.SF[dh * hi:dh * (hi + 1), 0:1024], pvp[0:dh, :],
                        ctx.SF2[0:dh, 0:1024], op=ALU.mult))
                    P.wait("dve", t1)
                    t2 = P.op("dve", lambda E_, g=g, hi=hi, qsl=qsl: E_.tensor_tensor(
                        ctx.X[g][dh * hi:dh * (hi + 1), qsl],
                        ctx.X[g][dh * hi:dh * (hi + 1), qsl],
                        ctx.SF[dh * hi:dh * (hi + 1), 0:1024], op=ALU.add))
                else:
                    t2 = P.op("dve", lambda E_, pvp=pvp, hi=hi: E_.tensor_tensor(
                        OS[32 * hi:32 * (hi + 1), 0:1024], pvp[0:dh, :],
                        ctx.SF2[0:dh, 0:1024], op=ALU.mult))
                P.wait("dve", t2)
                ep = t2
            if w_out_ap is not None:
                cch, rbase = g // 2, 64 * (g % 2)
                t3 = P.op("dve", lambda E_, cch=cch, rbase=rbase, qsl=qsl: E_.tensor_copy(
                    O_all[cch][rbase:rbase + 64, qsl], OS[0:64, 0:1024]))
                P.wait("dve", t3)
                ep = t3
            prev_core["kb"] = ep
            prev_core["st"] = st_tok.get("a")
            prev_core["pva"] = pv_tok.get(("a", 1))
            prev_core["pvb"] = pv_tok.get(("b", 1))
            ep_toks.append(ep)

    P.barrier()
    if MHA_STOP == "core" or w_out_ap is None:
        return ep_toks[-1]

    wtok, wch = _load_w_direct(ctx, w_out_ap, tag + "o")
    bo = ctx.BT[:, 0:8]
    tbo = P.dma("pool", bo, b_out_ap.rearrange("(c p) -> p c", p=128), allow_slow_non_contiguous=True)
    P.wait("pe", tbo)
    _, _ = emit_fm_proj(ctx, wch, wtok, list(range(NCH)), bias_col=bo,
                        src_chunks=O_all, dst_chunks=ctx.XN)
    P.barrier()
    if MHA_STOP == "oproj":
        return ep_toks[-1]
    gm = ctx.BT[:, 8:16]
    bt = ctx.BT[:, 16:24]
    tg1 = P.dma("pool", gm, gamma_ap.rearrange("(c p) -> p c", p=128), allow_slow_non_contiguous=True)
    tg2 = P.dma("pool", bt, beta_ap.rearrange("(c p) -> p c", p=128), allow_slow_non_contiguous=True)
    P.wait("dve", tg1, tg2)
    P.wait("act", tg2)
    P.wait("pe", tg2)
    if MHA_STOP == "noaffine":
        tln = emit_ln(ctx, eps=1e-6, src=ctx.XN, dst=ctx.XN)
    else:
        tln = emit_ln(ctx, eps=1e-6, src=ctx.XN, dst=ctx.XN, gamma=gm, beta=bt)
    P.barrier()
    if MHA_STOP == "ln":
        return tln
    for c in range(NCH):
        tx = P.op("dve", lambda E_, c=c: E_.tensor_tensor(
            ctx.X[c][:, :], ctx.X[c][:, :], ctx.XN[c][:, :], op=ALU.add))
        P.wait("dve", tx)
    return tx


def emit_attn_causal(ctx: Ctx, tag, wq_ap, bq_ap, wk_ap, bk_ap, wv_ap, bv_ap):
    P = ctx.P
    STR, BIG = ctx.STR, ctx.BIG
    d = 1024
    scale = 1.0 / 32.0

    Qd = P.dram(f"{tag}_Qd", [1024, TPC], BF16).ap()
    Kd = P.dram(f"{tag}_Kd", [1024, TPC], BF16).ap()
    Vd = P.dram(f"{tag}_Vd", [TPC, 1024], BF16).ap()
    Kg = P.dram(f"{tag}_Kg", [2048, TPC], BF16).ap()
    Vg = P.dram(f"{tag}_Vg", [2 * TPC, 1024], BF16).ap()
    kmd = P.dram(f"{tag}_kmd", [1, 16], F32).ap()
    kmg = P.dram(f"{tag}_kmg", [1, 16], F32).ap()

    qmax = ctx.SF2[0:1, 0:16]
    kmax = ctx.SF2[0:1, 32:48]

    bqt = ctx.BT[:, 0:8]
    bkt = ctx.BT[:, 8:16]
    tb1 = P.dma("pool", bqt, bq_ap.rearrange("(c p) -> p c", p=128), allow_slow_non_contiguous=True)
    tb2 = P.dma("pool", bkt, bk_ap.rearrange("(c p) -> p c", p=128), allow_slow_non_contiguous=True)
    bvr = ctx.ROWB[0:1, 1024:2048]
    tb3 = P.dma("pool", bvr, bv_ap.rearrange("e -> 1 e" if False else "(o e) -> o e", o=1))
    P.wait("dve", tb1)
    tb4 = P.op("dve", lambda E_: E_.tensor_scalar(ctx.BT[:, 16:24], bqt, scale, None, ALU.mult))
    P.wait("dve", tb4)
    tb5 = P.settle("dve")
    bq32 = ctx.BT[:, 16:24]

    wtok, wch = _load_wT(ctx, wq_ap, tag + "q")
    P.wait("act", tb5)
    _, tnq = emit_fm_proj(ctx, wch, wtok, list(range(NCH)), scale=scale, bias_col=bq32,
                          dram_out=Qd, norm_accum=True, umax=qmax)
    P.barrier()
    wtok, wch = _load_wT(ctx, wk_ap, tag + "k")
    P.wait("act", tb2)
    _, tnk = emit_fm_proj(ctx, wch, wtok, list(range(NCH)), bias_col=bkt,
                          dram_out=Kd, norm_accum=True, umax=kmax)
    P.barrier()
    wtok, wch = _load_wT(ctx, wv_ap, tag + "v")
    P.wait("pool", tb3)
    tv = emit_v_proj(ctx, wch, wtok, Vd, bias_row=bvr)
    P.wait("sp", tnk)
    tkm = P.dma("sp", kmd[0:1, :], kmax)
    P.barrier()

    c1 = P.cc("AllGather", ALU.bypass, [Kd.tensor.ap().opt()], [Kg.tensor.ap().opt()], ctx.pairs)
    c2 = P.cc("AllGather", ALU.bypass, [Vd.tensor.ap().opt()], [Vg.tensor.ap().opt()], ctx.pairs)
    c3 = P.cc("AllReduce", ALU.max, [kmd.tensor.ap().opt()], [kmg.tensor.ap().opt()], ctx.pairs)
    P.wait("sp", c3)
    tkg = P.dma("sp", kmax, kmg[0:1, :])

    P.wait("dve", tkg, tnq)
    t = P.op("dve", lambda E_: E_.tensor_tensor(ctx.SF2[0:1, 64:65], qmax[0:1, 0:1], kmax[0:1, 0:1], op=ALU.mult))
    P.wait("act", t)
    t = P.op("act", lambda E_: E_.activation(ctx.SF2[0:1, 64:65], ctx.SF2[0:1, 64:65], AF.Sqrt))
    P.wait("dve", t)
    t = P.op("dve", lambda E_: E_.tensor_scalar(ctx.ROWB[0:1, 0:1], ctx.SF2[0:1, 64:65], -1.0, None, ALU.mult))
    P.wait("dve", t)
    t = P.settle("dve")
    P.wait("pe", t)
    tbb = P.op("pe", lambda E_: E_.matmul(ctx.pA[:, 0:1], ctx.ones_row[0:1, :],
                                          ctx.ROWB[0:1, 0:1], start=True, stop=True))
    P.wait("dve", tbb)
    tbc = P.op("dve", lambda E_: E_.tensor_copy(ctx.bias128[:, 0:1], ctx.pA[:, 0:1]))
    P.wait("dve", tbc)
    bias_tok = P.settle("dve")

    KT = [STR[:, 0:1024], STR[:, 1024:2048]]
    VT = [STR[:, 2048:3072], STR[:, 3072:4096]]
    QB = STR[:, 4096:8192]
    MB = STR[:, 8192:8704]
    EX = STR[:, 8704:9216]
    EE = [BIG[:, i * 512:(i + 1) * 512] for i in range(32)]

    kdma = {}
    kuse = {}
    prev_ph2 = None
    prev_recip = None
    for qc in range(4):
        qsl = slice(qc * 512, (qc + 1) * 512)
        P.wait("sp", prev_ph2)
        dq = P.dma("sp", QB.rearrange("p (c k) -> p c k", c=8),
                   Qd[:, qsl].rearrange("(c p) k -> p c k", p=128))
        tden = None
        texp_prev = None
        tmask_prev = None
        for ki in range(32):
            sl_ = ki % 2
            ps = ctx.pA if sl_ == 0 else ctx.pB
            rank, kloc = ki // 16, ki % 16
            P.wait("sp", kuse.get(("k", sl_)), c1)
            dk = P.dma("sp", KT[sl_].rearrange("p (c k) -> p c k", c=8),
                       Kg[rank * 1024:(rank + 1) * 1024, kloc * 128:(kloc + 1) * 128]
                       .rearrange("(c p) k -> p c k", p=128))
            kdma[("k", sl_)] = dk
            P.wait("pe", dk, dq, texp_prev, bias_tok)
            for c in range(NCH):
                tst = P.op("pe", lambda E_, c=c, ps=ps, sl_=sl_: E_.matmul(
                    ps[:, 0:512], KT[sl_][:, c * 128:(c + 1) * 128],
                    QB[:, c * 512:(c + 1) * 512],
                    start=(c == 0), stop=(c == NCH - 1)),
                    sem=None if c < NCH - 1 else "pe")
            kuse[("k", sl_)] = tst
            P.wait("act", tst, tmask_prev)
            tex = P.op("act", lambda E_, ps=ps: E_.activation(
                EX[:, :], ps[:, :512], AF.Exp, bias=ctx.bias128[:, 0:1]))
            texp_prev = tex
            P.wait("dve", tex, prev_ph2)
            tmk = P.op("dve", lambda E_, ki=ki, qsl=qsl: E_.tensor_scalar(
                MB[:, :], ctx.qg[:, qsl], ctx.kg[:, ki:ki + 1], None, ALU.is_ge))
            P.wait("dve", tmk)
            tmm = P.op("dve", lambda E_, ki=ki: E_.tensor_tensor(
                EE[ki][:, :], EX[:, :], MB[:, :], op=ALU.mult))
            tmask_prev = tmm
            P.wait("pe", tmm, prev_recip)
            tdn = P.op("pe", lambda E_, ki=ki: E_.matmul(
                ctx.pC[0:1, 0:512], ctx.ones_col[:, 0:1], EE[ki][:, :],
                start=(ki == 0), stop=(ki == 31)))
            tden = tdn
        P.wait("dve", tden)
        td = P.op("dve", lambda E_: E_.reciprocal(ctx.ROWS[0:1, 0:512], ctx.pC[0:1, 0:512]))
        P.wait("dve", td)
        td = P.op("dve", lambda E_: E_.tensor_copy(ctx.ROWB[0:1, 512:1024], ctx.ROWS[0:1, 0:512]))
        P.wait("dve", td)
        td2 = P.settle("dve")
        prev_recip = td2
        P.wait("pe", td2)
        tbc2 = P.op("pe", lambda E_: E_.matmul(ctx.pD[:, 0:512], ctx.ones_row[0:1, :],
                                               ctx.ROWB[0:1, 512:1024], start=True, stop=True))
        P.wait("act", tbc2)
        tnb = P.op("act", lambda E_: E_.copy(ctx.SF[:, 0:512], ctx.pD[:, 0:512]))
        pslots = [ctx.pA[:, 0:512], ctx.pA[:, 512:1024], ctx.pB[:, 0:512], ctx.pB[:, 512:1024],
                  ctx.pC[:, 0:512], ctx.pC[:, 512:1024], ctx.pD[:, 0:512], ctx.pD[:, 512:1024]]
        tpv = {}
        for ki in range(32):
            sl_ = ki % 2
            P.wait("sp", kuse.get(("v", sl_)), c2)
            dv = P.dma("sp", VT[sl_].rearrange("p (c k) -> p c k", c=8) if False else VT[sl_],
                       Vg[ki * 128:(ki + 1) * 128, :])
            P.wait("pe", dv)
            if ki < 2:
                P.wait("pe", tnb, texp_prev)
            for mc in range(NCH):
                tp = P.op("pe", lambda E_, ki=ki, mc=mc, sl_=sl_, pslots=pslots: E_.matmul(
                    pslots[mc], VT[sl_][:, mc * 128:(mc + 1) * 128], EE[ki][:, :],
                    start=(ki == 0), stop=(ki == 31)),
                    sem=None if mc < NCH - 1 else "pe")
            kuse[("v", sl_)] = tp
            tpv[ki] = tp
        P.wait("dve", tpv[31], tpv[30])
        tlast = None
        for mc in range(NCH):
            P.wait("dve", tlast)
            t1 = P.op("dve", lambda E_, mc=mc, pslots=pslots: E_.tensor_tensor(
                ctx.SF[:, 512:1024], pslots[mc], ctx.SF[:, 0:512], op=ALU.mult))
            P.wait("dve", t1)
            t2 = P.op("dve", lambda E_, mc=mc, qsl=qsl: E_.tensor_tensor(
                ctx.X[mc][:, qsl], ctx.X[mc][:, qsl], ctx.SF[:, 512:1024], op=ALU.add))
            tlast = t2
        prev_ph2 = tlast
    P.barrier()
    return tlast


def emit_performer(ctx: Ctx, tag, wqkv_ap, bqkv_ap, wout_ap, bout_ap, omega_ap):
    P = ctx.P
    STR, BIG = ctx.STR, ctx.BIG
    H, dh, M = 16, 64, 256

    qkvd = P.dram(f"{tag}_qkv", [3072, TPC], BF16).ap()
    kvd = P.dram(f"{tag}_kv", [H * 256, 65], F32).ap()
    kvs = P.dram(f"{tag}_kvs", [H * 256, 65], F32).ap()

    bqkvt = ctx.BT[:, 0:24]
    tb = P.dma("pool", bqkvt, bqkv_ap.rearrange("(c p) -> p c", p=128),
               allow_slow_non_contiguous=True)
    om = STR[0:65, 16384:16640]
    tom = P.dma("pool", om[0:64, :], omega_ap[:, :])
    P.wait("dve", tom)
    tom2 = P.op("dve", lambda E_: E_.memset(om[64:65, :], 1.0))
    P.wait("dve", tom2)
    tom3 = P.settle("dve")

    for grp in range(3):
        wtok, wch = _load_w_direct(ctx, wqkv_ap[:, grp * 1024:(grp + 1) * 1024], f"{tag}w{grp}")
        P.wait("pe", tb)
        tq, _ = emit_fm_proj(ctx, wch, wtok, list(range(NCH)),
                             bias_col=bqkvt[:, grp * 8:(grp + 1) * 8],
                             dram_out=qkvd, dram_row0=grp * 1024)
        P.barrier()

    if MHA_STOP == "pqkv":
        P.barrier()
        return (qkvd, kvd, kvs)
    kh = STR[0:64, 0:2048]
    k2 = STR[0:64, 2048:4096]
    kp = STR[:, 4096:8192]
    vh = STR[0:64, 8192:10240]
    vt = STR[:, 10240:11280]
    vtv = STR[:, 10240:11280].rearrange("p (t w) -> p t w", w=65) if False else None
    prev_kv = None
    for h in range(H):
        dk = P.dma("sp", kh[:, :], qkvd[1024 + h * 64:1024 + (h + 1) * 64, :])
        dv = P.dma("sp", vh[:, :], qkvd[2048 + h * 64:2048 + (h + 1) * 64, :])
        P.wait("pe", prev_kv, dv)
        P.wait("act", dk, prev_kv)
        tk2 = P.op("act", lambda E_: E_.activation(k2[:, :], kh[:, :], AF.Square))
        P.wait("pe", tk2, tom3)
        for tt in range(NT):
            tn = P.op("pe", lambda E_, tt=tt: E_.matmul(
                ctx.pD[0:128, tt:tt + 1] if False else ctx.pC[:, tt:tt + 1],
                k2[:, tt * 128:(tt + 1) * 128], ctx.ones_col[0:64, 0:1],
                start=True, stop=True),
                sem=None if tt < NT - 1 else "pe")
        P.wait("dve", tn)
        tsc = P.op("dve", lambda E_: E_.tensor_scalar(ctx.SF2[:, 0:16], ctx.pC[:, 0:16],
                                                      -0.5, None, ALU.mult))
        P.wait("dve", tsc)
        tsc2 = P.settle("dve")
        prev_exp = None
        for tt in range(NT):
            ps = ctx.pA if tt % 2 == 0 else ctx.pB
            P.wait("pe", prev_exp, tsc2)
            tmm = P.op("pe", lambda E_, tt=tt, ps=ps: E_.matmul(
                ps[:, 0:256], kh[:, tt * 128:(tt + 1) * 128], om[0:64, :],
                start=True, stop=True))
            P.wait("act", tmm)
            tex = P.op("act", lambda E_, tt=tt, ps=ps: E_.activation(
                kp[:, tt * 256:(tt + 1) * 256], ps[:, 0:256], AF.Exp,
                bias=ctx.SF2[:, tt:tt + 1]))
            prev_exp = tex
        prev_vt = None
        for tt in range(NT):
            P.wait("pe", prev_vt)
            tvt = P.op("pe", lambda E_, tt=tt: E_.matmul(
                ctx.pC[:, 256:384], vh[:, tt * 128:(tt + 1) * 128], ctx.ident[0:64, :],
                start=True, stop=True))
            P.wait("dve", tvt)
            tcv = P.op("dve", lambda E_: E_.tensor_copy(vt[:, 0:64], ctx.pC[:, 256:320]))
            P.wait("dve", tcv)
            tcv2 = P.op("dve", lambda E_: E_.memset(vt[:, 64:65], 1.0))
            P.wait("dve", tcv2)
            tcv3 = P.settle("dve")
            P.wait("pe", tcv3, prev_exp)
            for mc in range(2):
                tkv = P.op("pe", lambda E_, tt=tt, mc=mc: E_.matmul(
                    ctx.pD[:, mc * 512:mc * 512 + 65],
                    kp[:, tt * 256 + mc * 128:tt * 256 + (mc + 1) * 128],
                    vt[:, 0:65], start=(tt == 0), stop=(tt == NT - 1)),
                    sem=None if (mc < 1 or tt < NT - 1) else "pe")
            prev_vt = tkv if tt == NT - 1 else tvt
        P.wait("act", tkv)
        te1 = P.op("act", lambda E_: E_.copy(ctx.SF[:, 0:65], ctx.pD[:, 0:65]))
        P.wait("act", te1)
        te2 = P.op("act", lambda E_: E_.copy(ctx.SF[:, 512:577], ctx.pD[:, 512:577]))
        P.wait("sp", te2)
        s1 = P.dma("sp", kvd[h * 256:h * 256 + 128, :], ctx.SF[:, 0:65])
        s2 = P.dma("sp", kvd[h * 256 + 128:h * 256 + 256, :], ctx.SF[:, 512:577])
        prev_kv = s2
    P.barrier()

    if MHA_STOP == "pkv":
        P.barrier()
        c1 = P.cc("AllReduce", ALU.add, [kvd.tensor.ap().opt()], [kvs.tensor.ap().opt()], ctx.pairs)
        P.wait("pool", c1)
        return (qkvd, kvd, kvs)
    c1 = P.cc("AllReduce", ALU.add, [kvd.tensor.ap().opt()], [kvs.tensor.ap().opt()], ctx.pairs)
    P.wait("pool", c1)

    qa = STR[0:65, 0:2048]
    q2 = STR[0:64, 2048:4096]
    qp = STR[:, 4096:8192]
    kvb = STR[:, 16640:16770]
    O_all = [BIG[:, c * 2048:(c + 1) * 2048] for c in range(NCH)]
    prev_num = None
    prev_nb = None
    for h in range(H):
        P.wait("sp", prev_num)
        dq = P.dma("sp", qa[0:64, :], qkvd[h * 64:(h + 1) * 64, :])
        P.wait("pool", prev_num)
        dkv = P.dma("pool", kvb.rearrange("p (c w) -> p c w", c=2),
                    kvs[h * 256:(h + 1) * 256, :].rearrange("(c p) w -> p c w", p=128))
        P.wait("act", dq, prev_num)
        tq2 = P.op("act", lambda E_: E_.activation(q2[:, :], qa[0:64, :], AF.Square))
        P.wait("pe", tq2, prev_nb)
        for j in range(4):
            prow = ctx.pC[0:1, :] if j < 2 else ctx.pC[32:33, :]
            po = (j % 2) * 512
            tqn = P.op("pe", lambda E_, j=j, prow=prow, po=po: E_.matmul(
                prow[0:1, po:po + 512], ctx.ones_col[0:64, 0:1],
                q2[:, j * 512:(j + 1) * 512], start=True, stop=True),
                sem=None if j < 3 else "pe")
        P.wait("dve", tqn)
        tr1 = P.op("dve", lambda E_: E_.tensor_scalar(ctx.ROWS[0:1, 0:1024], ctx.pC[0:1, 0:1024],
                                                      -0.5, None, ALU.mult))
        P.wait("dve", tr1)
        tr2 = P.op("dve", lambda E_: E_.tensor_copy(qa[64:65, 0:1024], ctx.ROWS[0:1, 0:1024]))
        P.wait("dve", tr2)
        tr3 = P.op("dve", lambda E_: E_.tensor_scalar(ctx.ROWS[0:1, 0:1024], ctx.pC[32:33, 0:1024],
                                                      -0.5, None, ALU.mult))
        P.wait("dve", tr3)
        tr4 = P.op("dve", lambda E_: E_.tensor_copy(qa[64:65, 1024:2048], ctx.ROWS[0:1, 0:1024]))
        P.wait("dve", tr4)
        tr5 = P.settle("dve")
        prev_exq = None
        for mc in range(2):
            P.wait("pe", tr5, prev_exq)
            for j in range(4):
                psl = ctx.pA if j < 2 else ctx.pB
                po = (j % 2) * 512
                tmm = P.op("pe", lambda E_, mc=mc, j=j, psl=psl, po=po: E_.matmul(
                    psl[:, po:po + 512],
                    om[0:65, mc * 128:(mc + 1) * 128],
                    qa[0:65, j * 512:(j + 1) * 512],
                    start=True, stop=True),
                    sem=None if j < 3 else "pe")
            P.wait("act", tmm)
            tex = P.op("act", lambda E_, mc=mc: E_.activation(
                qp[:, mc * 2048:mc * 2048 + 1024], ctx.pA[:, :], AF.Exp))
            P.wait("act", tex)
            tex = P.op("act", lambda E_, mc=mc: E_.activation(
                qp[:, mc * 2048 + 1024:(mc + 1) * 2048], ctx.pB[:, :], AF.Exp))
            prev_exq = tex
        P.wait("pe", dkv, prev_exq)
        for mc in range(2):
            for j in range(4):
                tnm = P.op("pe", lambda E_, mc=mc, j=j: E_.matmul(
                    (ctx.pC if j < 2 else ctx.pD)[0:65, (j % 2) * 512:(j % 2) * 512 + 512],
                    kvb[:, mc * 65:(mc + 1) * 65],
                    qp[:, mc * 2048 + j * 512:mc * 2048 + (j + 1) * 512],
                    start=(mc == 0), stop=(mc == 1)),
                    sem=None if (mc < 1 or j < 3) else "pe")
        prev_num = tnm
        P.wait("dve", tnm)
        td = P.op("dve", lambda E_: E_.tensor_scalar(ctx.ROWS[0:1, 0:1024], ctx.pC[64:65, 0:1024],
                                                     1e-6, None, ALU.add))
        P.wait("dve", td)
        td = P.op("dve", lambda E_: E_.reciprocal(ctx.ROWS[0:1, 0:1024], ctx.ROWS[0:1, 0:1024]))
        P.wait("dve", td)
        td = P.op("dve", lambda E_: E_.tensor_copy(ctx.ROWB[0:1, 0:1024], ctx.ROWS[0:1, 0:1024]))
        P.wait("dve", td)
        td2 = P.op("dve", lambda E_: E_.tensor_scalar(ctx.ROWS[0:1, 0:1024], ctx.pD[64:65, 0:1024],
                                                      1e-6, None, ALU.add))
        P.wait("dve", td2)
        td2 = P.op("dve", lambda E_: E_.reciprocal(ctx.ROWS[0:1, 0:1024], ctx.ROWS[0:1, 0:1024]))
        P.wait("dve", td2)
        td2 = P.op("dve", lambda E_: E_.tensor_copy(ctx.ROWB[0:1, 1024:2048], ctx.ROWS[0:1, 0:1024]))
        P.wait("dve", td2)
        td3 = P.settle("dve")
        P.wait("pe", td3)
        for j in range(4):
            tbc = P.op("pe", lambda E_, j=j: E_.matmul(
                (ctx.pA if j < 2 else ctx.pB)[0:64, (j % 2) * 512:(j % 2) * 512 + 512],
                ctx.ones_row[0:1, 0:64],
                ctx.ROWB[0:1, j * 512:(j + 1) * 512], start=True, stop=True),
                sem=None if j < 3 else "pe")
        P.wait("act", tbc)
        tnb1 = P.op("act", lambda E_: E_.copy(ctx.SF2[0:64, 0:1024], ctx.pA[0:64, :]))
        P.wait("act", tnb1)
        tnb2 = P.op("act", lambda E_: E_.copy(ctx.SF[0:64, 0:1024], ctx.pB[0:64, :]))
        prev_nb = tnb2
        cch, rb = h // 2, 64 * (h % 2)
        P.wait("dve", tnb1, tnb2)
        to1 = P.op("dve", lambda E_, cch=cch, rb=rb: E_.tensor_tensor(
            O_all[cch][rb:rb + 64, 0:1024], ctx.pC[0:64, :], ctx.SF2[0:64, 0:1024], op=ALU.mult))
        P.wait("dve", to1)
        to2 = P.op("dve", lambda E_, cch=cch, rb=rb: E_.tensor_tensor(
            O_all[cch][rb:rb + 64, 1024:2048], ctx.pD[0:64, :], ctx.SF[0:64, 0:1024], op=ALU.mult))
        P.wait("dve", to2)
        prev_num = to2
    P.barrier()

    wtok, wch = _load_w_direct(ctx, wout_ap, tag + "o")
    bo = ctx.BT[:, 0:8]
    tbo = P.dma("pool", bo, bout_ap.rearrange("(c p) -> p c", p=128),
                allow_slow_non_contiguous=True)
    prev_ev = None
    for r in range(NCH):
        P.wait("pe", wtok, tbo, prev_ev)
        for c in range(NCH):
            for j in range(4):
                psl = ctx.pA if j < 2 else ctx.pB
                po = (j % 2) * 512
                tmm = P.op("pe", lambda E_, r=r, c=c, j=j, psl=psl, po=po: E_.matmul(
                    psl[:, po:po + 512], wch[c][:, r * 128:(r + 1) * 128],
                    O_all[c][:, j * 512:(j + 1) * 512],
                    start=(c == 0), stop=(c == NCH - 1)),
                    sem=None if (c < NCH - 1) else "pe")
        P.wait("dve", tmm)
        for j in range(4):
            psl = ctx.pA if j < 2 else ctx.pB
            po = (j % 2) * 512
            tev = P.op("dve", lambda E_, r=r, j=j, psl=psl, po=po: E_.scalar_tensor_tensor(
                ctx.X[r][:, j * 512:(j + 1) * 512], psl[:, po:po + 512], bo[:, r:r + 1],
                ctx.X[r][:, j * 512:(j + 1) * 512], ALU.add, ALU.add))
            P.wait("dve", tev)
        prev_ev = tev
    P.barrier()
    return prev_ev


SUBLAYERS = list(range(8))
GROUPS = [SUBLAYERS]


def build_group(subs, tag):
    P = Prog()
    F = lambda n, shp: P.dram(n, shp, F32, kind="ExternalInput").ap()
    x_in = F("x_in", [1024, TPC])
    ident_in = P.dram("identity_in", [128, 128], BF16, kind="ExternalInput").ap()
    qg_in = F("qgb_in", [128, TPC])
    kg_in = F("kg_in", [128, 32])
    y_out = P.dram("y_out", [1024, TPC], F32, kind="ExternalOutput").ap()

    w = {}
    w["l0aa_wqkv"] = F("L0_attn_a_Wqkv", [1024, 3072])
    w["l0aa_bqkv"] = F("L0_attn_a_bqkv", [3072])
    w["l0aa_wout"] = F("L0_attn_a_Wout", [1024, 1024])
    w["l0aa_bout"] = F("L0_attn_a_bout", [1024])
    w["l0aa_om"] = F("L0_attn_a_omega", [64, 256])
    w["l0ab_wq"] = F("L0_attn_b_Wq", [1024, 1024])
    w["l0ab_wk"] = F("L0_attn_b_Wk", [1024, 1024])
    w["l0ab_wv"] = F("L0_attn_b_Wv", [1024, 1024])
    w["l0f_g"] = F("L0_ffn_gate_w", [16, 1024])
    w["l0f_w1"] = F("L0_ffn_w1", [16 * 1024, 512])
    w["l0f_w3"] = F("L0_ffn_w3", [16 * 1024, 512])
    w["l0f_w2"] = F("L0_ffn_w2", [16 * 512, 1024])
    w["l1aa_wqkv"] = F("L1_attn_a_w_qkv", [1024, 3072])
    w["l1aa_wo"] = F("L1_attn_a_w_out", [1024, 1024])
    w["l1aa_bo"] = F("L1_attn_a_b_out", [1024])
    w["l1aa_g"] = F("L1_attn_a_ln_gamma", [1024])
    w["l1aa_b"] = F("L1_attn_a_ln_beta", [1024])
    w["l1ab_wq"] = F("L1_attn_b_Wq", [1024, 1024])
    w["l1ab_bq"] = F("L1_attn_b_bq", [1024])
    w["l1ab_wk"] = F("L1_attn_b_Wk", [1024, 1024])
    w["l1ab_bk"] = F("L1_attn_b_bk", [1024])
    w["l1ab_wv"] = F("L1_attn_b_Wv", [1024, 1024])
    w["l1ab_bv"] = F("L1_attn_b_bv", [1024])
    w["l1f_g"] = F("L1_ffn_gate_w", [1024, 8])
    w["l1f_wg"] = F("L1_ffn_w_gate", [8 * 1024, 2048])
    w["l1f_wu"] = F("L1_ffn_w_up", [8 * 1024, 2048])
    w["l1f_wd"] = F("L1_ffn_w_down", [8 * 2048, 1024])
    w["l1f_sg"] = F("L1_ffn_ws_gate", [1024, 2048])
    w["l1f_su"] = F("L1_ffn_ws_up", [1024, 2048])
    w["l1f_sd"] = F("L1_ffn_ws_down", [2048, 1024])

    ctx = Ctx(P)
    ctx.init_consts(x_in, ident_in, qg_in, kg_in)
    P.wait("act", ctx.init_tok)
    P.wait("pe", ctx.init_tok)
    P.wait("dve", ctx.init_tok)

    for s in subs:
        emit_ln(ctx)
        P.barrier()
        if s == 0:
            emit_performer(ctx, "p0", w["l0aa_wqkv"], w["l0aa_bqkv"],
                           w["l0aa_wout"], w["l0aa_bout"], w["l0aa_om"])
        elif s in (1, 3):
            emit_moe(ctx, w["l0f_g"], "eh", 16, 512, 4,
                     w["l0f_w1"], w["l0f_w3"], w["l0f_w2"])
        elif s == 2:
            emit_attn_mha(ctx, f"a0_{s}", 16, 64, w["l0ab_wq"], w["l0ab_wk"],
                          w["l0ab_wv"], transposed=True)
        elif s == 4:
            emit_attn_mha(ctx, "a1", 32, 32,
                          w["l1aa_wqkv"][:, 0:1024], w["l1aa_wqkv"][:, 1024:2048],
                          w["l1aa_wqkv"][:, 2048:3072], transposed=False,
                          w_out_ap=w["l1aa_wo"], b_out_ap=w["l1aa_bo"],
                          gamma_ap=w["l1aa_g"], beta_ap=w["l1aa_b"])
        elif s in (5, 7):
            emit_moe(ctx, w["l1f_g"], "he", 8, 2048, 2,
                     w["l1f_wg"], w["l1f_wu"], w["l1f_wd"],
                     shared=(w["l1f_sg"], w["l1f_su"], w["l1f_sd"]), eps20=True)
        elif s == 6:
            emit_attn_causal(ctx, "c0", w["l1ab_wq"], w["l1ab_bq"],
                             w["l1ab_wk"], w["l1ab_bk"], w["l1ab_wv"], w["l1ab_bv"])
        P.barrier()

    st = None
    for c in range(NCH):
        t = P.op("dve", lambda E_, c=c: E_.tensor_copy(ctx.SF[:, :], ctx.X[c][:, :]))
        P.wait("sp", t)
        st = P.dma("sp", y_out[c * 128:(c + 1) * 128, :], ctx.SF[:, :])
        P.wait("dve", st)
    P.wait("sp", st)
    return P.build()


_IDENT = None


def kernel(**inputs):
    global _IDENT
    import numpy as _np
    xf = inputs["L0_attn_a_x"]
    ident = _np.eye(128, dtype=ml_dtypes.bfloat16)
    kg = _np.ascontiguousarray(
        _np.arange(4096, dtype=_np.float32).reshape(32, 128).T)

    base = {
        "L0_attn_a_Wqkv": inputs["L0_attn_a_Wqkv"],
        "L0_attn_a_bqkv": inputs["L0_attn_a_bqkv"],
        "L0_attn_a_Wout": inputs["L0_attn_a_Wout"],
        "L0_attn_a_bout": inputs["L0_attn_a_bout"],
        "L0_attn_a_omega": inputs["L0_attn_a_omega"],
        "L0_attn_b_Wq": inputs["L0_attn_b_Wq"],
        "L0_attn_b_Wk": inputs["L0_attn_b_Wk"],
        "L0_attn_b_Wv": inputs["L0_attn_b_Wv"],
        "L0_ffn_gate_w": inputs["L0_ffn_gate_w"],
        "L0_ffn_w1": _np.ascontiguousarray(inputs["L0_ffn_w1"].reshape(16 * 1024, 512)),
        "L0_ffn_w3": _np.ascontiguousarray(inputs["L0_ffn_w3"].reshape(16 * 1024, 512)),
        "L0_ffn_w2": _np.ascontiguousarray(inputs["L0_ffn_w2"].reshape(16 * 512, 1024)),
        "L1_attn_a_w_qkv": inputs["L1_attn_a_w_qkv"],
        "L1_attn_a_w_out": inputs["L1_attn_a_w_out"],
        "L1_attn_a_b_out": inputs["L1_attn_a_b_out"],
        "L1_attn_a_ln_gamma": inputs["L1_attn_a_ln_gamma"],
        "L1_attn_a_ln_beta": inputs["L1_attn_a_ln_beta"],
        "L1_attn_b_Wq": inputs["L1_attn_b_Wq"],
        "L1_attn_b_bq": inputs["L1_attn_b_bq"],
        "L1_attn_b_Wk": inputs["L1_attn_b_Wk"],
        "L1_attn_b_bk": inputs["L1_attn_b_bk"],
        "L1_attn_b_Wv": inputs["L1_attn_b_Wv"],
        "L1_attn_b_bv": inputs["L1_attn_b_bv"],
        "L1_ffn_gate_w": inputs["L1_ffn_gate_w"],
        "L1_ffn_w_gate": _np.ascontiguousarray(inputs["L1_ffn_w_gate"].reshape(8 * 1024, 2048)),
        "L1_ffn_w_up": _np.ascontiguousarray(inputs["L1_ffn_w_up"].reshape(8 * 1024, 2048)),
        "L1_ffn_w_down": _np.ascontiguousarray(inputs["L1_ffn_w_down"].reshape(8 * 2048, 1024)),
        "L1_ffn_ws_gate": inputs["L1_ffn_ws_gate"],
        "L1_ffn_ws_up": inputs["L1_ffn_ws_up"],
        "L1_ffn_ws_down": inputs["L1_ffn_ws_down"],
        "identity_in": ident,
        "kg_in": kg,
    }

    xcur = [None] * CORES
    for c in range(CORES):
        b, h = c // 2, c % 2
        xcur[c] = _np.ascontiguousarray(xf[b, h * TPC:(h + 1) * TPC, :].T)

    for gi, subs in enumerate(GROUPS):
        nc = build_group(subs, f"g{gi}")
        in_maps = []
        for c in range(CORES):
            b, h = c // 2, c % 2
            im = dict(base)
            im["x_in"] = xcur[c]
            im["qgb_in"] = _np.tile(
                _np.arange(h * TPC, (h + 1) * TPC, dtype=_np.float32), (128, 1))
            in_maps.append(im)
        res = run_bass_kernel_spmd(nc, in_maps, core_ids=list(range(CORES)))
        for c in range(CORES):
            xcur[c] = _np.ascontiguousarray(res.results[c]["y_out"])

    out = _np.empty((B, N, D), _np.float32)
    for c in range(CORES):
        b, h = c // 2, c % 2
        out[b, h * TPC:(h + 1) * TPC, :] = xcur[c].T
    return out
